# version 13
# baseline (speedup 1.0000x reference)
import numpy as np
from contextlib import ExitStack
from types import SimpleNamespace

import concourse.bass as bass
import concourse.mybir as mybir
from concourse.bass_utils import run_bass_kernel_spmd
from concourse.library_config import proxy as _proxy_lib
from concourse.library_config import attn as _attn_lib
from concourse.bass_isa import ReduceOp

F32 = mybir.dt.float32
BF16 = mybir.dt.bfloat16
AF = mybir.ActivationFunctionType
ALU = mybir.AluOpType
AX = mybir.AxisListType

DM, DI, NST, DTR, HF, DH = 1024, 2048, 16, 64, 3072, 4096
KM, KI, KF, KH = DM // 128, DI // 128, HF // 128, DH // 128
NCORE, BPC = 8, 2
EPS = 1e-5


class Prog:
    ENGS = ("t", "v", "s", "g", "q")

    def __init__(self, nc):
        self.nc = nc
        self.streams = {e: [] for e in self.ENGS}
        self.cnt = {}
        self.sem_h = {}
        self.dirty = {e: set() for e in self.ENGS}
        self.rset = {e: set() for e in self.ENGS}
        self.waited = {e: {} for e in self.ENGS}
        self.dma_sems = set()
        self.es = ExitStack()
        self.labels = {e: [] for e in self.ENGS}
        self.meta = {e: [] for e in self.ENGS}

    def sem(self, name):
        if name not in self.sem_h:
            self.sem_h[name] = self.es.enter_context(self.nc.semaphore(name))
            self.cnt[name] = 0
        return self.sem_h[name]

    def wait(self, eng, name, val):
        if val <= 0 or self.waited[eng].get(name, 0) >= val:
            return
        self.waited[eng][name] = val
        h = self.sem(name)
        self.streams[eng].append(lambda E, h=h, v=val: E.wait_ge(h, v))
        self.meta[eng].append(("w", name, val))

    def waitt(self, eng, tok):
        if tok is not None:
            self.wait(eng, tok[0], tok[1])

    def drain(self, eng):
        if self.dirty[eng] or self.rset[eng]:
            self.streams[eng].append(lambda E: E.drain())
            self.labels[eng].append(("drain", 0))
            self.dirty[eng].clear()
            self.rset[eng].clear()

    def op(self, eng, fn, reads=(), writes=(), inc=None, inc_n=1, waits=(),
           clears=()):
        merged = {}
        for tk in waits:
            if tk is not None and tk[1] > merged.get(tk[0], 0):
                merged[tk[0]] = tk[1]
        for nm, v in merged.items():
            self.waitt(eng, (nm, v))
        for nm in clears:
            self.dirty[eng].discard(nm)
            self.rset[eng].discard(nm)
        if (any(r in self.dirty[eng] for r in reads)
                or any(w in self.dirty[eng] or w in self.rset[eng]
                       for w in writes)):
            self.drain(eng)
        if inc is not None:
            self.sem(inc)
            if inc_n == 16:
                self.dma_sems.add(inc)
            self.cnt[inc] += inc_n
            h = self.sem_h[inc]

            def run(E, fn=fn, h=h, n=inc_n):
                fn(E).then_inc(h, n)
        else:
            def run(E, fn=fn):
                fn(E)
        self.streams[eng].append(run)
        import sys as _sys
        f = _sys._getframe(1)
        if f.f_code.co_name == "<lambda>":
            f = f.f_back
        self.labels[eng].append(("op", f.f_lineno))
        self.meta[eng].append(("i", inc, inc_n, f.f_lineno)
                              if inc is not None else ("i", None, 0,
                                                       f.f_lineno))
        if self.labels[eng][-2:-1] == [("drain", 0)]:
            self.labels[eng][-2] = ("drain", f.f_lineno)
        for w in writes:
            self.dirty[eng].add(w)
        for r in reads:
            self.rset[eng].add(r)
        return (inc, self.cnt[inc]) if inc is not None else None

    def raw(self, eng, fn):
        self.streams[eng].append(fn)
        self.labels[eng].append(("raw", 0))

    def tok(self, name):
        self.sem(name)
        return (name, self.cnt[name])

    def dma_wait_all(self, eng, prefixes=None):
        for name in sorted(self.dma_sems):
            if prefixes is None or any(name.startswith(p) for p in prefixes):
                self.wait(eng, name, self.cnt[name])

    def barrier(self, engines=None):
        for eng in (engines or self.ENGS):
            for name in sorted(self.cnt):
                self.wait(eng, name, self.cnt[name])

    def build(self, block):
        for attr, e in (("tensor", "t"), ("vector", "v"), ("scalar", "s"),
                        ("gpsimd", "g"), ("sync", "q")):
            stream = self.streams[e]

            def make(stream):
                def body(E):
                    for f in stream:
                        f(E)
                return body
            getattr(block, attr)(make(stream))


def _dma(out, in_):
    return lambda E: E.dma_start(out=out, in_=in_)


def _dma_nc(out, in_):
    return lambda E: E.dma_start(out=out, in_=in_,
                                 allow_slow_non_contiguous=True)


def _win_starts(L, ws=256):
    stride, starts = ws // 2, []
    for i in range(0, L, stride):
        end = min(i + ws, L)
        starts.append(max(0, end - ws))
        if end == L:
            break
    return starts


def s0_weights(C, need=("mamba", "ffn0", "attn1", "ffn1")):
    P, nc, din, ds, sb, ps = C.P, C.nc, C.din, C.ds, C.sb, C.ps
    st = ExitStack()
    wf = st.enter_context(nc.sbuf_tensor("s0_wf", [128, 2 * 4096], F32))
    wb = st.enter_context(nc.sbuf_tensor("s0_wb", [128, 2 * 4096], BF16))
    sca = st.enter_context(nc.sbuf_tensor("s0_sca", [128, 2 * 192], F32))
    tb = st.enter_context(nc.sbuf_tensor("s0_tb", [128, 2 * 512], BF16))
    row = st.enter_context(nc.sbuf_tensor("s0_row", [128, 8], F32))
    wmod = st.enter_context(nc.sbuf_tensor("s0_wm", [128, 4096], F32))
    ldt, vdt, gqt = {}, {}, {}

    def loadw(par, dst_cols, src_ap, w):
        P.op("q", _dma(wf[:, dst_cols], src_ap), inc=f"qw{par}", inc_n=16,
             waits=[vdt.get((par, "wf"))])
        return P.tok(f"qw{par}")

    def plain_cast(src, dst, ktiles, cols, scale=None, rows=128,
                   tiled=0):
        for k in range(ktiles):
            par = k % 2
            tl = P.op("q", _dma(wf[0:rows, par * 4096:par * 4096 + cols],
                                src[k * rows:(k + 1) * rows, :]),
                      inc=f"qw{par}", inc_n=16, waits=[vdt.get((par, "wf"))])
            if scale is None:
                t = P.op("v", lambda E, par=par, cols=cols, rows=rows:
                         E.tensor_copy(
                             wb[0:rows, par * 4096:par * 4096 + cols],
                             wf[0:rows, par * 4096:par * 4096 + cols]),
                         reads=("s0_wf",), writes=("s0_wb",), inc="v0",
                         waits=[tl, gqt.get((par, "wb"))])
            else:
                t = P.op("v", lambda E, par=par, cols=cols, s=scale,
                         rows=rows: E.tensor_scalar(
                             wb[0:rows, par * 4096:par * 4096 + cols],
                             wf[0:rows, par * 4096:par * 4096 + cols],
                             s, None, ALU.mult),
                         reads=("s0_wf",), writes=("s0_wb",), inc="v0",
                         waits=[tl, gqt.get((par, "wb"))])
            vdt[(par, "wf")] = t
            if tiled:
                P.op("s", lambda E, k=k, par=par, cols=cols, dst=dst,
                     tiled=tiled: E.dma_start(
                         out=dst[:, :, k * 128:(k + 1) * 128]
                         .rearrange("f p c -> p f c"),
                         in_=wb[:, par * 4096:par * 4096 + cols]
                         .rearrange("p (f c) -> p f c", c=128)),
                     inc=f"gw{par}", inc_n=16, waits=[t])
            else:
                P.op("s", lambda E, k=k, par=par, cols=cols, rows=rows,
                     dst=dst: E.dma_start(
                         out=dst[k * rows:(k + 1) * rows, :],
                         in_=wb[0:rows, par * 4096:par * 4096 + cols]),
                     inc=f"gw{par}", inc_n=16, waits=[t])
            gqt[(par, "wb")] = P.tok(f"gw{par}")

    if "mamba" in need:
        plain_cast(din["L0_attn_in_proj_w"], ds["w_inproj"], KM, 2 * DI,
                   tiled=2 * KI)
        plain_cast(din["L0_attn_adj"], ds["w_adjn"], KI, DI, scale=-1.0,
                   tiled=KI)
        plain_cast(din["L0_attn_out_proj_w"], ds["w_outproj"], KI, DM,
                   tiled=KM)
        plain_cast(din["L0_attn_x_proj_w"], ds["w_xproj"], KI, DTR + 2 * NST)
        plain_cast(din["L0_attn_dt_proj_w"], ds["w_dtproj"], 1, DI, rows=64)
    if "attn1" in need:
        for wn, dst in (("L1_attn_Wq", "w_q"), ("L1_attn_Wk", "w_k"),
                        ("L1_attn_Wv", "w_v")):
            plain_cast(din[wn], ds[dst], KM, DM)
        plain_cast(din["L1_attn_Wo"], ds["w_o"], KM, DM, tiled=KM)

    idb, onesf = sb["idb"], sb["onesf"]

    def transpose_store(par, cols, dst, ktile, psum_i, ready, tiled=False):
        nblk = cols // 128
        for g0 in range(0, nblk, 4):
            gn = min(4, nblk - g0)
            tprev = C.psum_free.get(psum_i)
            tmm = None
            for j in range(gn):
                tmm = P.op("t", lambda E, par=par, g0=g0, j=j, psum_i=psum_i:
                           E.transpose(
                               ps[psum_i].ap().bitcast(BF16)
                               [0:128, j * 128:(j + 1) * 128],
                               wb[:, par * 4096 + (g0 + j) * 128:
                                  par * 4096 + (g0 + j + 1) * 128],
                               idb[:]),
                           waits=[tprev, ready] if j == 0 else [],
                           inc="t0" if j == gn - 1 else None)
            te = P.op("v", lambda E, psum_i=psum_i, gn=gn: E.tensor_copy(
                tb[:, psum_i % 2 * 512: psum_i % 2 * 512 + gn * 128],
                ps[psum_i].ap().bitcast(BF16)[:, 0:gn * 128]),
                writes=("s0_tb",), inc="v0",
                waits=[tmm, gqt.get((psum_i % 2, "tb"))])
            C.psum_free[psum_i] = te
            if tiled:
                P.op("s", lambda E, g0=g0, gn=gn, ktile=ktile, dst=dst,
                     psum_i=psum_i: E.dma_start(
                         out=dst[ktile, :, g0 * 128:(g0 + gn) * 128],
                         in_=tb[:, psum_i % 2 * 512:
                                psum_i % 2 * 512 + gn * 128]),
                     inc=f"gts{psum_i % 2}", inc_n=16, waits=[te])
            else:
                P.op("s", lambda E, g0=g0, gn=gn, ktile=ktile, dst=dst,
                     psum_i=psum_i: E.dma_start(
                         out=dst[g0 * 128:(g0 + gn) * 128,
                                 ktile * 128:(ktile + 1) * 128]
                         .rearrange("(g p) c -> p g c", p=128),
                         in_=tb[:, psum_i % 2 * 512:
                                psum_i % 2 * 512 + gn * 128]
                         .rearrange("p (g c) -> p g c", c=128)),
                     inc=f"gts{psum_i % 2}", inc_n=16, waits=[te])
            gqt[(psum_i % 2, "tb")] = P.tok(f"gts{psum_i % 2}")
        vdt[(par, "wbT")] = P.tok("t0")

    def lpbq_weight(src, dstT, O):
        KT, I = O // 128, src.shape[1]
        NB = I // 32
        for k in range(KT):
            par = k % 2
            tl = loadw(par, slice(par * 4096, par * 4096 + I),
                       src[k * 128:(k + 1) * 128, :], I)
            wfs = wf[:, par * 4096:par * 4096 + I]
            wbs = wb[:, par * 4096:par * 4096 + I]
            scs = sca[:, par * 192:par * 192 + NB]
            sns = sca[:, par * 192 + NB:par * 192 + 2 * NB]
            P.op("v", lambda E, wfs=wfs, scs=scs, NB=NB: E.tensor_reduce(
                scs, wfs.rearrange("p (n k) -> p n k", k=32), AX.X,
                ALU.max, apply_absolute_value=True),
                reads=("s0_wf",), writes=("s0_sca",), waits=[tl])
            P.op("v", lambda E, scs=scs: E.tensor_scalar(
                scs, scs, 1.0 / 7.0, 1e-12, ALU.mult, ALU.max),
                reads=("s0_sca",), writes=("s0_sca",))
            P.op("v", lambda E, scs=scs, sns=sns: E.tensor_copy(sns, scs),
                 reads=("s0_sca",), writes=("s0_sca",))
            P.op("v", lambda E, scs=scs: E.reciprocal(scs, scs),
                 reads=("s0_sca",), writes=("s0_sca",))
            def bc(a, NB=NB):
                return a.unsqueeze(2).broadcast_to([128, NB, 32])
            P.op("v", lambda E, wfs=wfs, scs=scs, NB=NB: E.tensor_tensor(
                wfs.rearrange("p (n k) -> p n k", k=32), wfs
                .rearrange("p (n k) -> p n k", k=32), bc(scs), ALU.mult),
                reads=("s0_wf", "s0_sca"), writes=("s0_wf",))
            P.op("v", lambda E, wfs=wfs: E.tensor_scalar(
                wfs, wfs, 12582912.0, 12582912.0, ALU.add, ALU.subtract),
                reads=("s0_wf",), writes=("s0_wf",))
            P.op("v", lambda E, wfs=wfs: E.tensor_scalar(
                wfs, wfs, -8.0, 7.0, ALU.max, ALU.min),
                reads=("s0_wf",), writes=("s0_wf",))
            tq = P.op("v", lambda E, wfs=wfs, sns=sns, wbs=wbs, NB=NB:
                      E.tensor_tensor(
                          wbs.rearrange("p (n k) -> p n k", k=32),
                          wfs.rearrange("p (n k) -> p n k", k=32),
                          bc(sns), ALU.mult),
                      reads=("s0_wf", "s0_sca"), writes=("s0_wb",), inc="v0",
                      waits=[gqt.get((par, "wb")), vdt.get((par, "wbT"))])
            vdt[(par, "wf")] = tq
            transpose_store(par, I, dstT, k, 4 + (k % 2), tq,
                            tiled=(dstT.shape[0] <= 16))

    if "ffn0" in need:
        lpbq_weight(din["L0_ffn_w_up"], ds["w_upT"], HF)
        lpbq_weight(din["L0_ffn_w_gate"], ds["w_gateT"], HF)
        lpbq_weight(din["L0_ffn_w_down"], ds["w_downT"], DM)

    def tern_weight(src, dstT, O, I):
        KT = O // 128
        tprev = C.psum_free.get(6)
        nmm = KT * (I // 512)
        imm = 0
        for k in range(KT):
            par = k % 2
            tl = loadw(par, slice(par * 4096, par * 4096 + I),
                       src[k * 128:(k + 1) * 128, :], I)
            ta = P.op("s", lambda E, par=par, I=I: E.activation(
                wb[:, par * 4096:par * 4096 + I],
                wf[:, par * 4096:par * 4096 + I], AF.Abs),
                reads=("s0_wf",), writes=("s0_wb",), inc="s0",
                waits=[tl, vdt.get((par, "awb")), gqt.get((par, "wb")),
                       vdt.get((par, "wbT"))])
            for g in range(I // 512):
                P.op("t", lambda E, par=par, g=g, first=(imm == 0),
                     last=(imm == nmm - 1):
                     E.matmul(ps[6][0:1, 0:512], sb["onesb"][:, 0:1],
                              wb[:, par * 4096 + g * 512:
                                 par * 4096 + (g + 1) * 512],
                              start=first, stop=last),
                     inc="t0", waits=[ta, tprev])
                imm += 1
            vdt[(par, "awb")] = P.tok("t0")
            vdt[(par, "wf")] = vdt[(par, "awb")]
        tm = P.tok("t0")
        P.op("v", lambda E: E.reduce_sum(row[0:1, 0:1], ps[6][0:1, 0:512],
                                         AX.X), writes=("s0_row",),
             waits=[tm])
        P.op("v", lambda E, n=O * I: E.tensor_scalar(
            row[0:1, 1:2], row[0:1, 0:1], 1.0 / n, 1e-5, ALU.mult, ALU.max),
            reads=("s0_row",), writes=("s0_row",))
        P.op("v", lambda E: E.reciprocal(row[0:1, 2:3], row[0:1, 1:2]),
             reads=("s0_row",), writes=("s0_row",))
        tr = P.op("v", lambda E: E.tensor_scalar(
            row[0:1, 3:4], row[0:1, 1:2], 0.5, None, ALU.mult),
            reads=("s0_row",), writes=("s0_row",), inc="v0")
        tg = P.op("g", lambda E: E.partition_broadcast(row[:, 1:4],
                                                       row[0:1, 1:4]),
                  reads=("s0_row",), writes=("s0_row",), inc="g0",
                  waits=[tr])
        C.psum_free[6] = tg
        for k in range(KT):
            par = k % 2
            tl = loadw(par, slice(par * 4096, par * 4096 + I),
                       src[k * 128:(k + 1) * 128, :], I)
            wfs = wf[:, par * 4096:par * 4096 + I]
            wbs = wb[:, par * 4096:par * 4096 + I]
            ta = P.op("s", lambda E, wfs=wfs, wbs=wbs: E.activation(
                wbs, wfs, AF.Abs), reads=("s0_wf",), writes=("s0_wb",),
                inc="s0", waits=[tl, tg, vdt.get((par, "twb")),
                                 gqt.get((par, "wb")),
                                 vdt.get((par, "wbT"))])
            tsg = P.op("s", lambda E, wfs=wfs: E.sign(wfs, wfs),
                       reads=("s0_wf",), writes=("s0_wf",), inc="s0")
            tmsk = P.op("v", lambda E, wbs=wbs: E.tensor_scalar(
                wbs, wbs, row[:, 3:4], None, ALU.is_ge),
                reads=("s0_wb", "s0_row"), writes=("s0_wb",), waits=[ta],
                inc="v0")
            tq = P.op("v", lambda E, wbs=wbs, wfs=wfs: E.scalar_tensor_tensor(
                wbs, wbs, row[:, 1:2], wfs, ALU.mult, ALU.mult),
                reads=("s0_wb", "s0_wf", "s0_row"), writes=("s0_wb",),
                inc="v0", waits=[tsg])
            vdt[(par, "wf")] = tq
            vdt[(par, "twb")] = tq
            transpose_store(par, I, dstT, k, 4 + (k % 2), tq,
                            tiled=(dstT.shape[0] <= 16))

    if "ffn1" in need:
        tern_weight(din["L1_ffn_w1"], ds["w_w1T"], DH, DM)
        tern_weight(din["L1_ffn_w2"], ds["w_w2T"], DM, DH)
    st.close()
    return [P.tok(s) for s in sorted(P.dma_sems)
            if s.startswith(("gw", "gts"))]


def ln_rows(C, ps_sum, ps_sq, n_inv, rowc, tmm, tc=None):
    P, ps, rows = C.P, C.ps, C.rows
    Tc = tc if tc is not None else C.Tc
    m = rows[0:1, rowc:rowc + Tc]
    e2 = rows[0:1, rowc + Tc:rowc + 2 * Tc]
    r = rows[0:1, rowc + 2 * Tc:rowc + 3 * Tc]
    mr = rows[0:1, rowc + 3 * Tc:rowc + 4 * Tc]
    P.op("v", lambda E: E.tensor_scalar(m, ps_sum[0:1, 0:Tc], n_inv, None,
                                        ALU.mult),
         writes=("rows",), waits=[tmm])
    P.op("v", lambda E: E.tensor_scalar(e2, ps_sq[0:1, 0:Tc], n_inv, None,
                                        ALU.mult),
         reads=(), writes=("rows",))
    P.op("v", lambda E: E.tensor_tensor(r, m, m, ALU.mult),
         reads=("rows",), writes=("rows",))
    P.op("v", lambda E: E.tensor_tensor(r, e2, r, ALU.subtract),
         reads=("rows",), writes=("rows",))
    P.op("v", lambda E: E.tensor_scalar(r, r, EPS, None, ALU.add),
         reads=("rows",), writes=("rows",))
    tv = P.op("v", lambda E: E.reciprocal(r, r), reads=("rows",),
              writes=("rows",), inc="v1")
    ts = P.op("s", lambda E: E.activation(r, r, AF.Sqrt), reads=("rows",),
              writes=("rows",), inc="s1", waits=[tv])
    tv2 = P.op("v", lambda E: E.tensor_tensor(mr, m, r, ALU.mult),
               reads=("rows",), writes=("rows",), inc="v1", waits=[ts])
    onesf = C.sb["onesf"]
    t4 = P.op("t", lambda E: E.matmul(ps[4][:, 0:Tc], onesf[0:1, 0:128], mr,
                                      start=True, stop=True), inc="t1",
              waits=[tv2, C.psum_free.get(4)])
    t5 = P.op("t", lambda E: E.matmul(ps[5][:, 0:Tc], onesf[0:1, 0:128], r,
                                      start=True, stop=True), inc="t1",
              waits=[tv2, C.psum_free.get(5)])
    return t4, t5


def s1_mamba(C):
    P, nc, ds, sb, ps, L = C.P, C.nc, C.ds, C.sb, C.ps, C.L
    Tc = 256
    ncpb = L // Tc
    steps = BPC * ncpb
    st = ExitStack()

    def S(name, shape, dt):
        return st.enter_context(nc.sbuf_tensor(name, shape, dt))
    xlnb = S("m_xlnb", [128, KM * Tc], BF16)
    sqb = S("m_sqb", [128, Tc], BF16)
    moch = S("m_moch", [128, KM * Tc], BF16)
    xtl = S("m_xtl", [128, 2 * Tc], F32)
    kxt = S("m_kxt", [128, 2 * Tc], F32)
    wipf = S("m_wipf", [128, 2 * 1024], BF16)
    adjf = S("m_adjf", [128, 2 * 2048], BF16)
    opf = S("m_opf", [128, 2048], BF16)
    xirch = S("m_xirch", [128, KI * (Tc + 3)], BF16)
    sgt = S("m_sgt", [128, 2 * Tc], F32)
    xdbd = S("m_xdbd", [64, Tc], BF16)
    bcb = S("m_bcb", [32, Tc], BF16)
    hpl = S("m_hpl", [128, NST * Tc], BF16)
    ab = S("m_ab", [128, 8 * Tc], BF16)
    bb = S("m_bb", [128, NST * Tc], BF16)
    bbB = S("m_bbB", [128, NST * Tc], BF16)
    yb = S("m_yb", [128, Tc], F32)
    hplF = hpl.ap().bitcast(F32)
    bbF = bb.ap().bitcast(F32)
    bbBF = bbB.ap().bitcast(F32)
    HPLN = tuple(f"hpl{n}" for n in range(NST))
    BBN = tuple(f"bb{n}" for n in range(NST))
    XLN = tuple(f"xlnb{k}" for k in range(KM))
    xibq = [S(f"m_xib{q}", [128, KI * Tc], BF16) for q in (0, 1)]
    sresq = [S(f"m_sres{q}", [128, KI * Tc], BF16) for q in (0, 1)]
    dltq = [S(f"m_dlt{q}", [128, KI * Tc], BF16) for q in (0, 1)]
    dubq = [S(f"m_dub{q}", [128, KI * Tc], BF16) for q in (0, 1)]
    dly2q = [S(f"m_dly2{q}", [128, KI * Tc], BF16) for q in (0, 1)]
    Brepq = [S(f"m_Brep{q}", [128, NST * Tc], BF16) for q in (0, 1)]
    Crepq = [S(f"m_Crep{q}", [128, NST * Tc], BF16) for q in (0, 1)]
    negA, Dsb, cwsb, cbsb, dtbn = (sb[x] for x in
                                   ("negA", "Dsb", "cwsb", "cbsb", "dtbn"))
    hcar, tails, xprj, dtprj, selb = (sb[x] for x in
                                      ("hcar", "tails", "xprj", "dtprj",
                                       "selb"))
    tx = P.op("q", _dma(xprj.ap().rearrange("p (k c) -> p k c", c=96),
                        ds["w_xproj"].rearrange("(k p) c -> p k c", p=128)),
              inc="qw0", inc_n=16, waits=list(C.w_done))
    td_ = P.op("q", _dma(dtprj[:], ds["w_dtproj"][:, :]), inc="qw1",
               inc_n=16)
    C.s1_resident = (tx, td_)

    rot = {}
    fr = [{}, {}]
    rel = [None, None]

    def front_A_stats(s):
        b, c = divmod(s, ncpb)
        cs = slice(c * Tc, (c + 1) * Tc)
        first = True
        for k in range(KM):
            par = k % 2
            tl = P.op("q", _dma(xtl[:, par * Tc:(par + 1) * Tc],
                                ds["resid"][b, k, :, cs]),
                      inc=f"qt{par}", inc_n=16,
                      waits=[rot.get(("xtl", par)), rot.get(("xtl2", par))])
            tc1 = P.op("v", lambda E, k=k, par=par: E.tensor_copy(
                xlnb[:, k * Tc:(k + 1) * Tc],
                xtl[:, par * Tc:(par + 1) * Tc]),
                writes=(f"xlnb{k}",), inc="v1",
                waits=[tl] + ([rot.get("xlnb_user")] if k == 0 else []),
                clears=XLN if k == 0 else ())
            tc2 = P.op("s", lambda E, par=par: E.activation(
                sqb[:], xtl[:, par * Tc:(par + 1) * Tc], AF.Square),
                writes=("sqb",), inc="s1",
                waits=[tl, rot.get("sqb_user")], clears=("sqb",))
            rot[("xtl", par)] = tc1
            rot[("xtl2", par)] = tc2
            P.op("t", lambda E, k=k, first=first: E.matmul(
                ps[6][0:1, 0:Tc], sb["onesb"][:, 0:1],
                xlnb[:, k * Tc:(k + 1) * Tc], start=first, stop=(k == KM - 1)),
                inc="t1", waits=[tc1] + ([C.psum_free.get(6)]
                                         if first else []))
            t7 = P.op("t", lambda E, first=first, k=k: E.matmul(
                ps[7][0:1, 0:Tc], sb["onesb"][:, 0:1], sqb[:],
                start=first, stop=(k == KM - 1)),
                inc="t1", waits=[tc2] + ([C.psum_free.get(7)]
                                         if first else []))
            rot["sqb_user"] = t7
            first = False
        return P.tok("t1")

    def front_A_rows(s, tmm):
        t4, t5 = ln_rows(C, ps[6], ps[7], 1.0 / DM, 0, tmm, tc=Tc)
        C.psum_free[6] = P.tok("v1")
        C.psum_free[7] = C.psum_free[6]
        tnorm = None
        for k in range(KM):
            P.op("v", lambda E, k=k: E.tensor_tensor(
                xlnb[:, k * Tc:(k + 1) * Tc],
                xlnb[:, k * Tc:(k + 1) * Tc], ps[5][:, 0:Tc], ALU.mult),
                reads=(f"xlnb{k}",), writes=(f"xlnb{k}",), waits=[t4, t5])
        for k in range(KM):
            tnorm = P.op("v", lambda E, k=k: E.tensor_tensor(
                xlnb[:, k * Tc:(k + 1) * Tc],
                xlnb[:, k * Tc:(k + 1) * Tc], ps[4][:, 0:Tc], ALU.subtract),
                reads=(f"xlnb{k}",), writes=(f"xlnb{k}",),
                inc="v1" if k == KM - 1 else None)
        C.psum_free[4] = tnorm
        C.psum_free[5] = tnorm
        return tnorm

    def front_Bpart(s, part, tnorm):
        q = s % 2
        xib, sres = xibq[q], sresq[q]
        relw = list(rel[q]) if rel[q] else []
        for f in range(4 * part, 4 * part + 4):
            par = pi = f % 2
            tw = P.op("q", lambda E, f=f, par=par: E.dma_start(
                out=wipf[:, par * 1024:(par + 1) * 1024],
                in_=ds["w_inproj"][f]),
                inc=f"qp{par}", inc_n=16, waits=[rot.get(("wipf", par))])
            tmm = None
            for k in range(KM):
                tmm = P.op("t", lambda E, k=k, par=par, pi=pi: E.matmul(
                    ps[pi][:, 0:Tc],
                    wipf[:, par * 1024 + k * 128:par * 1024 + (k + 1) * 128],
                    xlnb[:, k * Tc:(k + 1) * Tc],
                    start=(k == 0), stop=(k == KM - 1)),
                    waits=([tw, tnorm, C.psum_free.get(pi)]
                           if k == 0 else []),
                    inc="t1" if k == KM - 1 else None)
            rot[("wipf", par)] = tmm
            if f < KI:
                te = P.op("s", lambda E, f=f, pi=pi: E.activation(
                    xirch[:, f * (Tc + 3) + 3:(f + 1) * (Tc + 3)],
                    ps[pi][:, 0:Tc], AF.Copy),
                    writes=("xirch",), inc="s1",
                    waits=[tmm] + ([rot.get("xirch_user")] if f == 0 else []),
                    clears=("xirch",) if f == 0 else ())
                fr[q][("xiev", f)] = te
            else:
                f2 = f - KI
                te = P.op("s", lambda E, f2=f2, pi=pi, sres=sres:
                          E.activation(
                              sres[:, f2 * Tc:(f2 + 1) * Tc],
                              ps[pi][:, 0:Tc], AF.Silu),
                          writes=(f"sres{q}",), inc="s1",
                          waits=([tmm]
                                 + ([rot.get(("sresK", q))] + relw
                                    if f2 == 0 else [])),
                          clears=(f"sres{q}",) if f2 == 0 else ())
                if f2 == KI - 1:
                    fr[q]["sres"] = te
            C.psum_free[pi] = te
        if part == 7:
            rot["xlnb_user"] = tmm
        if part == 0:
            for f2 in (0, 1):
                fr[q][("adjld", f2)] = P.op(
                    "q", lambda E, f2=f2: E.dma_start(
                        out=adjf[:, f2 * 2048:(f2 + 1) * 2048],
                        in_=ds["w_adjn"][f2]),
                    inc=f"qa{f2}", inc_n=16, waits=[rot.get(("adjf", f2))])

    def back_H_k(t, k):
        q = t % 2
        b, c = divmod(t, ncpb)
        xib, sres, dlt, dub = xibq[q], sresq[q], dltq[q], dubq[q]
        dly2, Brep, Crep = dly2q[q], Brepq[q], Crepq[q]
        frq = fr[q]
        if k == 0 and c == 0:
            P.op("v", lambda E: E.memset(hcar[:], 0.0), writes=("hcar",))
        def emit_exps(n0):
            last = None
            for n in range(n0, n0 + 8):
                r = n % 8
                last = P.op("s", lambda E, k=k, n=n, r=r, dlt=dlt:
                            E.activation(
                                ab[:, r * Tc:(r + 1) * Tc],
                                dlt[:, k * Tc:(k + 1) * Tc], AF.Exp,
                                scale=negA[:, k * NST + n:k * NST + n + 1]),
                            reads=(f"dlt{q}",), writes=(f"ab{r}",), inc="s1",
                            waits=[rot.get(("ab", r))]
                            + ([frq.get("dlt")]
                               if k == 0 and n == 0 else []),
                            clears=(f"ab{r}",))
            return last

        ta7 = emit_exps(0)
        rp = k % 2
        bbr = bb if rp == 0 else bbB
        bbrF = bbF if rp == 0 else bbBF
        BN = "A" if rp == 0 else "B"
        tg7 = tg15 = None
        for n in range(NST):
            tg = P.op("g", lambda E, k=k, n=n, dub=dub, Brep=Brep, bbr=bbr:
                      E.tensor_tensor(
                          bbr[:, n * Tc:(n + 1) * Tc],
                          dub[:, k * Tc:(k + 1) * Tc],
                          Brep[:, n * Tc:(n + 1) * Tc], ALU.mult),
                      inc="g1",
                      waits=([rot.get(("bbfree", rp))] if n == 0 else [])
                      + ([frq.get("dub"), frq.get("Brep")]
                         if k == 0 and n == 0 else []))
            if n == 7:
                tg7 = tg
            elif n == NST - 1:
                tg15 = tg
        ta15 = None
        for n in range(NST):
            r = n % 8
            tsc = P.op("v", lambda E, k=k, n=n, r=r, bbr=bbr:
                       E.tensor_tensor_scan(
                           hpl[:, n * Tc:(n + 1) * Tc],
                           ab[:, r * Tc:(r + 1) * Tc],
                           bbr[:, n * Tc:(n + 1) * Tc],
                           hcar[:, k * NST + n:k * NST + n + 1],
                           ALU.mult, ALU.add),
                       reads=(f"bb{BN}{n}", "hcar"), writes=(f"hpl{n}",),
                       inc="v1",
                       waits=[ta7, tg7] if n == 0
                       else ([ta15, tg15] if n == 8 else ()),
                       clears=(f"bb{BN}{n}",) if n == 0 else ())
            rot[("ab", r)] = tsc
            if n == 7:
                rot[("ab", "half")] = tsc
                ta15 = emit_exps(8)
        P.op("v", lambda E, k=k: E.tensor_copy(
            hcar[:, k * NST:(k + 1) * NST],
            hpl.ap().rearrange("p (n t) -> p n t", n=NST)[:, :, Tc - 1]),
            reads=HPLN, writes=("hcar",))
        for n in range(NST):
            P.op("v", lambda E, n=n, Crep=Crep, bbr=bbr: E.tensor_tensor(
                bbr[:, n * Tc:(n + 1) * Tc],
                hpl[:, n * Tc:(n + 1) * Tc],
                Crep[:, n * Tc:(n + 1) * Tc], ALU.mult),
                reads=(f"hpl{n}", f"Crep{q}"), writes=(f"bb{BN}{n}",),
                waits=[frq.get("Crep")] if k == 0 and n == 0 else ())
        for j in range(8):
            P.op("v", lambda E, j=j, bbr=bbr: E.tensor_tensor(
                hplF[:, j * Tc:(j + 1) * Tc],
                bbr[:, 2 * j * Tc:(2 * j + 1) * Tc],
                bbr[:, (2 * j + 1) * Tc:(2 * j + 2) * Tc], ALU.add),
                reads=(f"bb{BN}{2*j}", f"bb{BN}{2*j+1}"),
                writes=(f"hpl{2*j}", f"hpl{2*j+1}"))
        for j in range(4):
            P.op("v", lambda E, j=j, bbrF=bbrF: E.tensor_tensor(
                bbrF[:, j * Tc:(j + 1) * Tc],
                hplF[:, 2 * j * Tc:(2 * j + 1) * Tc],
                hplF[:, (2 * j + 1) * Tc:(2 * j + 2) * Tc], ALU.add),
                reads=tuple(f"hpl{4*j+i}" for i in range(4)),
                writes=(f"bb{BN}{2*j}", f"bb{BN}{2*j+1}"))
        tst3 = None
        for j in range(2):
            tst3 = P.op("v", lambda E, j=j, bbrF=bbrF: E.tensor_tensor(
                hplF[:, j * Tc:(j + 1) * Tc],
                bbrF[:, 2 * j * Tc:(2 * j + 1) * Tc],
                bbrF[:, (2 * j + 1) * Tc:(2 * j + 2) * Tc], ALU.add),
                reads=tuple(f"bb{BN}{4*j+i}" for i in range(4)),
                writes=(f"hpl{2*j}", f"hpl{2*j+1}"),
                inc="v1" if j == 1 else None)
        rot[("bbfree", rp)] = tst3
        P.op("v", lambda E: E.tensor_tensor(
            yb[:], hplF[:, 0:Tc], hplF[:, Tc:2 * Tc], ALU.add),
            reads=("hpl0", "hpl1", "hpl2", "hpl3"), writes=("yb",))
        P.op("v", lambda E, k=k, xib=xib: E.scalar_tensor_tensor(
            yb[:], xib[:, k * Tc:(k + 1) * Tc], Dsb[:, k:k + 1],
            yb[:], ALU.mult, ALU.add),
            reads=("yb", f"xib{q}"), writes=("yb",),
            waits=[frq.get("xib")] if k == 0 else ())
        ty2 = P.op("v", lambda E, k=k, dly2=dly2, sres=sres: E.tensor_tensor(
            dly2[:, k * Tc:(k + 1) * Tc], yb[:],
            sres[:, k * Tc:(k + 1) * Tc], ALU.mult),
            reads=("yb", f"sres{q}"), writes=(f"dly2{q}",), inc="v1",
            waits=[frq.get("sres"), frq.get("deltap_mms")] if k == 0 else ())
        return ty2 if k == NST - 1 else None

    def front_piece(s, k):
        q = s % 2
        b, c = divmod(s, ncpb)
        frq = fr[q]
        relw = list(rel[q]) if rel[q] else []
        xib, sres, dlt, dub = xibq[q], sresq[q], dltq[q], dubq[q]
        dly2, Brep, Crep = dly2q[q], Brepq[q], Crepq[q]
        if k < 8:
            if k == 0 and c == 0:
                P.op("v", lambda E: E.memset(tails[:], 0.0),
                     writes=tuple(f"tails{f}" for f in range(KI)),
                     waits=[rot.get("tails_user")])
            for f in (2 * k, 2 * k + 1):
                xs = xirch[:, f * (Tc + 3):(f + 1) * (Tc + 3)]
                P.op("v", lambda E, xs=xs, f=f: E.tensor_copy(
                    xs[:, 0:3], tails[:, f * 3:(f + 1) * 3]),
                    reads=(f"tails{f}",), writes=("xirchT",), inc="v1",
                    waits=[frq[("xiev", f)]])
            for f in (2 * k, 2 * k + 1):
                xs = xirch[:, f * (Tc + 3):(f + 1) * (Tc + 3)]
                par = f % 2
                P.op("v", lambda E, xs=xs, f=f, par=par: E.tensor_scalar(
                    sgt[:, par * Tc:(par + 1) * Tc], xs[:, 0:Tc],
                    cwsb[:, f * 4:f * 4 + 1], None, ALU.mult),
                    reads=("xirchT",), writes=(f"sgt{par}",),
                    waits=[rot.get(("sgt", par))], clears=(f"sgt{par}",))
                tcv = None
                for kk in range(1, 4):
                    tcv = P.op("v", lambda E, xs=xs, f=f, kk=kk, par=par:
                               E.scalar_tensor_tensor(
                                   sgt[:, par * Tc:(par + 1) * Tc],
                                   xs[:, kk:kk + Tc],
                                   cwsb[:, f * 4 + kk:f * 4 + kk + 1],
                                   sgt[:, par * Tc:(par + 1) * Tc],
                                   ALU.mult, ALU.add),
                               reads=(f"sgt{par}",), writes=(f"sgt{par}",),
                               inc="v1" if kk == 3 else None)
                tt1 = P.op("v", lambda E, xs=xs, f=f: E.tensor_copy(
                    tails[:, f * 3:(f + 1) * 3], xs[:, Tc:Tc + 3]),
                    reads=(), writes=(), inc="v1")
                rot["tails_user"] = tt1
                ts = P.op("s", lambda E, f=f, par=par, xib=xib: E.activation(
                    xib[:, f * Tc:(f + 1) * Tc],
                    sgt[:, par * Tc:(par + 1) * Tc], AF.Silu,
                    bias=cbsb[:, f:f + 1]),
                    reads=(f"sgt{par}",), writes=(f"xib{q}",), inc="s1",
                    waits=[tcv] + ([rot.get(("sresK", q))] + relw
                                   if f == 0 else []),
                    clears=(f"xib{q}",) if f == 0 else ())
                rot[("sgt", par)] = ts
                if f == KI - 1:
                    frq["xib"] = ts
            if k == 7:
                rot["xirch_user"] = P.tok("v1")
        elif k == 8:
            txib = frq["xib"]
            tA = tB = None
            for kk in range(KI):
                tA = P.op("t", lambda E, kk=kk, xib=xib: E.matmul(
                    ps[2][0:64, 0:Tc], xprj[:, kk * 96:kk * 96 + 64],
                    xib[:, kk * Tc:(kk + 1) * Tc],
                    start=(kk == 0), stop=(kk == KI - 1)),
                    waits=([txib, C.psum_free.get(2), C.s1_resident[0]]
                           if kk == 0 else []),
                    inc="t1" if kk == KI - 1 else None)
            for kk in range(KI):
                tB = P.op("t", lambda E, kk=kk, xib=xib: E.matmul(
                    ps[3][0:32, 0:Tc], xprj[:, kk * 96 + 64:(kk + 1) * 96],
                    xib[:, kk * Tc:(kk + 1) * Tc],
                    start=(kk == 0), stop=(kk == KI - 1)),
                    waits=[C.psum_free.get(3)] if kk == 0 else [],
                    inc="t1" if kk == KI - 1 else None)
            tev1 = P.op("s", lambda E: E.activation(xdbd[:], ps[2][0:64, 0:Tc],
                                                    AF.Copy),
                        writes=("xdbd",), inc="s1",
                        waits=[tA, rot.get("xdbd_user")])
            tev2 = P.op("s", lambda E: E.activation(bcb[:], ps[3][0:32, 0:Tc],
                                                    AF.Copy),
                        writes=("bcb",), inc="s1",
                        waits=[tB, rot.get("bcb_user")])
            C.psum_free[2] = tev1
            C.psum_free[3] = tev2
            frq["tev1"] = tev1
            frq["tev2"] = tev2
        elif k == 9:
            tev1 = frq["tev1"]
            for f in range(KI):
                pi = par = f % 2
                tmm = P.op("t", lambda E, f=f, pi=pi: E.matmul(
                    ps[pi][:, 0:Tc], dtprj[:, f * 128:(f + 1) * 128],
                    xdbd[:], start=True, stop=True),
                    waits=[tev1, C.psum_free.get(pi), C.s1_resident[1]],
                    inc="t1")
                tsg = P.op("s", lambda E, pi=pi, f=f, par=par: E.activation(
                    sgt[:, par * Tc:(par + 1) * Tc], ps[pi][:, 0:Tc],
                    AF.Sigmoid, scale=-1.0, bias=dtbn[:, f:f + 1]),
                    writes=(f"sgt{par}",), inc="s1",
                    waits=[tmm, rot.get(("sgt", par))],
                    clears=(f"sgt{par}",))
                C.psum_free[pi] = tsg
                tln = P.op("s", lambda E, f=f, par=par, dly2=dly2:
                           E.activation(
                               dly2[:, f * Tc:(f + 1) * Tc],
                               sgt[:, par * Tc:(par + 1) * Tc], AF.Ln),
                           reads=(f"sgt{par}",), writes=(f"dly2{q}",),
                           inc="s1", waits=relw if f == 0 else (),
                           clears=(f"dly2{q}",) if f == 0 else ())
                rot[("sgt", par)] = tln
            frq["dL"] = P.tok("s1")
        elif k in (10, 11):
            tdl = frq["dL"]
            for f in range(8 * (k - 10), 8 * (k - 9)):
                par = pi = f % 2
                tmm = None
                for kk in range(KI):
                    tmm = P.op("t", lambda E, kk=kk, par=par, pi=pi,
                               dly2=dly2: E.matmul(
                                   ps[pi][:, 0:Tc],
                                   adjf[:, par * 2048 + kk * 128:
                                        par * 2048 + (kk + 1) * 128],
                                   dly2[:, kk * Tc:(kk + 1) * Tc],
                                   start=(kk == 0), stop=(kk == KI - 1)),
                               waits=([frq[("adjld", f)], tdl,
                                       C.psum_free.get(pi)]
                                      if kk == 0 else []),
                               inc="t1" if kk == KI - 1 else None)
                if f + 2 < KI:
                    frq[("adjld", f + 2)] = P.op(
                        "q", lambda E, f=f, par=par: E.dma_start(
                            out=adjf[:, par * 2048:(par + 1) * 2048],
                            in_=ds["w_adjn"][f + 2]),
                        inc=f"qa{par}", inc_n=16, waits=[tmm])
                else:
                    rot[("adjf", par)] = tmm
                te = P.op("s", lambda E, f=f, pi=pi, dlt=dlt: E.activation(
                    dlt[:, f * Tc:(f + 1) * Tc], ps[pi][:, 0:Tc], AF.Copy),
                    writes=(f"dlt{q}",), inc="s1",
                    waits=[tmm] + (relw if f == 0 else []),
                    clears=(f"dlt{q}",) if f == 0 else ())
                C.psum_free[pi] = te
            if k == 11:
                frq["dlt"] = P.tok("s1")
                frq["deltap_mms"] = P.tok("t1")
        elif k == 12:
            tev2 = frq["tev2"]
            for n in range(NST):
                pi = n % 2
                tm1 = P.op("t", lambda E, n=n, pi=pi: E.matmul(
                    ps[pi][:, 0:Tc], selb[:, n * 128:(n + 1) * 128], bcb[:],
                    start=True, stop=True), inc="t1",
                    waits=[tev2, C.psum_free.get(pi)])
                te1 = P.op("s", lambda E, n=n, pi=pi, Brep=Brep: E.activation(
                    Brep[:, n * Tc:(n + 1) * Tc], ps[pi][:, 0:Tc], AF.Copy),
                    writes=(f"Brep{q}",), inc="s1",
                    waits=[tm1] + (relw if n == 0 else []),
                    clears=(f"Brep{q}",) if n == 0 else ())
                C.psum_free[pi] = te1
                tm2 = P.op("t", lambda E, n=n, pi=pi: E.matmul(
                    ps[pi][:, 0:Tc], selb[:, (16 + n) * 128:(17 + n) * 128],
                    bcb[:], start=True, stop=True), inc="t1",
                    waits=[C.psum_free.get(pi)])
                te2 = P.op("s", lambda E, n=n, pi=pi, Crep=Crep: E.activation(
                    Crep[:, n * Tc:(n + 1) * Tc], ps[pi][:, 0:Tc], AF.Copy),
                    writes=(f"Crep{q}",), inc="s1",
                    waits=[tm2] + (relw if n == 0 else []),
                    clears=(f"Crep{q}",) if n == 0 else ())
                C.psum_free[pi] = te2
            frq["Brep"] = P.tok("s1")
            frq["Crep"] = frq["Brep"]
            rot["bcb_user"] = frq["Brep"]
            rot["xdbd_user"] = frq["Brep"]
        elif k == 13:
            tdu = None
            for kk in range(KI):
                tdu = P.op("v", lambda E, kk=kk, dub=dub, dlt=dlt, xib=xib:
                           E.tensor_tensor(
                               dub[:, kk * Tc:(kk + 1) * Tc],
                               dlt[:, kk * Tc:(kk + 1) * Tc],
                               xib[:, kk * Tc:(kk + 1) * Tc], ALU.mult),
                           reads=(f"dlt{q}", f"xib{q}"), writes=(f"dub{q}",),
                           waits=[frq["dlt"]] + (relw if kk == 0 else []),
                           clears=(f"dub{q}",) if kk == 0 else (),
                           inc="v1" if kk == KI - 1 else None)
            frq["dub"] = tdu

    def back_I(t, ty2):
        q = t % 2
        dly2 = dly2q[q]
        for f in range(KM):
            pi = f % 2
            ld = P.op("q", lambda E, f=f: E.dma_start(
                out=opf[:], in_=ds["w_outproj"][f]),
                inc="qo0", inc_n=16, waits=[rot.get("opf_user")])
            tmm = None
            for k in range(KI):
                tmm = P.op("t", lambda E, k=k, pi=pi, dly2=dly2:
                           E.matmul(
                               ps[pi][:, 0:Tc],
                               opf[:, k * 128:(k + 1) * 128],
                               dly2[:, k * Tc:(k + 1) * Tc],
                               start=(k == 0), stop=(k == KI - 1)),
                           waits=([ld, ty2, C.psum_free.get(pi)]
                                  if k == 0 else []),
                           inc="t1" if k == KI - 1 else None)
            rot["opf_user"] = tmm
            te = P.op("s", lambda E, f=f, pi=pi: E.activation(
                moch[:, f * Tc:(f + 1) * Tc], ps[pi][:, 0:Tc], AF.Copy),
                writes=("moch",), inc="s1",
                waits=[tmm] + ([rot.get("moch_user")] if f == 0 else []),
                clears=("moch",) if f == 0 else ())
            C.psum_free[pi] = te
        tmo_mms = P.tok("t1")
        tmo = P.tok("s1")
        rel[q] = (ty2, tmo_mms)
        return tmo

    def back_JK(t, ty2, tmo):
        q = t % 2
        b, c = divmod(t, ncpb)
        cs = slice(c * Tc, (c + 1) * Tc)
        sres = sresq[q]
        first = True
        for k in range(KM):
            P.op("t", lambda E, k=k, first=first: E.matmul(
                ps[2][0:1, 0:Tc], sb["onesb"][:, 0:1],
                moch[:, k * Tc:(k + 1) * Tc], start=first,
                stop=(k == KM - 1)),
                inc="t1", waits=[tmo] + ([C.psum_free.get(2)]
                                         if first else []))
            tsq = P.op("s", lambda E, k=k: E.activation(
                sqb[:], moch[:, k * Tc:(k + 1) * Tc], AF.Square),
                writes=("sqb",), inc="s1",
                waits=[tmo, rot.get("sqb_user")], clears=("sqb",))
            t7 = P.op("t", lambda E, first=first, k=k: E.matmul(
                ps[3][0:1, 0:Tc], sb["onesb"][:, 0:1], sqb[:],
                start=first, stop=(k == KM - 1)), inc="t1",
                waits=[tsq] + ([C.psum_free.get(3)] if first else []))
            rot["sqb_user"] = t7
            first = False
        tmm = P.tok("t1")
        t4, t5 = ln_rows(C, ps[2], ps[3], 1.0 / DM, 1040, tmm, tc=Tc)
        C.psum_free[2] = P.tok("v1")
        C.psum_free[3] = C.psum_free[2]
        xnew = sres.ap().bitcast(F32)
        tadd = None
        tld = {}
        for k in range(2):
            tld[k] = P.op("q", _dma(kxt[:, k * Tc:(k + 1) * Tc],
                                    ds["resid"][b, k, :, cs]),
                          inc=f"qk{k}", inc_n=16,
                          waits=[rot.get(("kxt", k))])
        for k in range(KM):
            P.op("v", lambda E, k=k, xnew=xnew: E.tensor_tensor(
                xnew[:, k * Tc:(k + 1) * Tc],
                moch[:, k * Tc:(k + 1) * Tc], ps[5][:, 0:Tc], ALU.mult),
                reads=("moch",), writes=(f"xnK{k}",),
                waits=[t4, t5, ty2] if k == 0 else (),
                clears=(f"sres{q}",) if k == 0 else ())
        for k in range(KM):
            P.op("v", lambda E, k=k, xnew=xnew: E.tensor_tensor(
                xnew[:, k * Tc:(k + 1) * Tc],
                xnew[:, k * Tc:(k + 1) * Tc], ps[4][:, 0:Tc],
                ALU.subtract),
                reads=(f"xnK{k}",), writes=(f"xnK{k}",))
        for k in range(KM):
            par = k % 2
            tadd = P.op("v", lambda E, k=k, par=par, xnew=xnew:
                        E.tensor_tensor(
                            xnew[:, k * Tc:(k + 1) * Tc],
                            xnew[:, k * Tc:(k + 1) * Tc],
                            kxt[:, par * Tc:(par + 1) * Tc], ALU.add),
                        reads=(f"xnK{k}",), writes=(f"xnK{k}",), inc="v1",
                        waits=[tld[k]])
            rot[("kxt", par)] = tadd
            if k + 2 < KM:
                tld[k + 2] = P.op(
                    "q", _dma(kxt[:, par * Tc:(par + 1) * Tc],
                              ds["resid"][b, k + 2, :, cs]),
                    inc=f"qk{par}", inc_n=16, waits=[tadd])
        C.psum_free[4] = tadd
        C.psum_free[5] = tadd
        tst = P.op("s", lambda E, b=b, cs=cs, xnew=xnew: E.dma_start(
            out=ds["resid"][b, :, :, cs].rearrange("k p t -> p k t"),
            in_=xnew.rearrange("p (k t) -> p k t", k=KM)),
            inc="gr0", inc_n=16, waits=[tadd])
        rot["moch_user"] = tst
        rot[("sresK", q)] = tst

    pend = {}
    for s in range(steps + 1):
        tnorm = None
        if s < steps:
            if s not in pend:
                pend[s] = front_A_stats(s)
            tnorm = front_A_rows(s, pend.pop(s))
        ty2 = None
        for k in range(NST):
            if s >= 1:
                tk = back_H_k(s - 1, k)
                if tk is not None:
                    ty2 = tk
            if s < steps:
                if k < 8:
                    front_Bpart(s, k, tnorm)
                front_piece(s, k)
            if k == 14 and s + 1 < steps:
                pend[s + 1] = front_A_stats(s + 1)
        if s >= 1:
            tmo = back_I(s - 1, ty2)
            back_JK(s - 1, ty2, tmo)
    st.close()
    return P.tok("gr0")

def s_out(C, done_tok, src="resid"):
    P, nc, ds, sb, ps, L = C.P, C.nc, C.ds, C.sb, C.ps, C.L
    st = ExitStack()
    xf = st.enter_context(nc.sbuf_tensor("o_xf", [128, 2 * KM * 128], F32))
    ot = st.enter_context(nc.sbuf_tensor("o_ot", [128, 2 * DM], F32))
    rot = {}
    for b in range(BPC):
        for tt in range(L // 128):
            par = tt % 2
            tl = P.op("q", lambda E, b=b, tt=tt, par=par: E.dma_start(
                out=xf[:, par * KM * 128:(par + 1) * KM * 128]
                .rearrange("p (k t) -> p k t", t=128),
                in_=ds[src][b, :, :, tt * 128:(tt + 1) * 128]
                .rearrange("k p t -> p k t")),
                inc=f"qf{par}", inc_n=16,
                waits=[done_tok, rot.get(("xf", par))])
            for half in range(2):
                pi = half
                tmm = None
                for j in range(4):
                    k = half * 4 + j
                    tmm = P.op("t", lambda E, k=k, j=j, par=par, pi=pi:
                               E.transpose(
                                   ps[pi][:, j * 128:(j + 1) * 128],
                                   xf[:, par * KM * 128 + k * 128:
                                      par * KM * 128 + (k + 1) * 128],
                                   sb["idf"][:]),
                               waits=([tl, C.psum_free.get(pi)]
                                      if j == 0 else []),
                               inc="t1" if j == 3 else None)
                te = P.op("s", lambda E, par=par, half=half, pi=pi:
                          E.activation(
                              ot[:, par * DM + half * 512:
                                 par * DM + (half + 1) * 512],
                              ps[pi][:, 0:512], AF.Copy),
                          writes=("o_ot",), inc="s1",
                          waits=[tmm, rot.get(("ot", par))])
                C.psum_free[pi] = te
            tev = P.tok("s1")
            rot[("xf", par)] = tev
            ts = P.op("s", lambda E, b=b, tt=tt, par=par: E.dma_start(
                out=C.out_d[b, tt * 128:(tt + 1) * 128, :],
                in_=ot[:, par * DM:(par + 1) * DM]),
                inc=f"go{par}", inc_n=16, waits=[tev])
            rot[("ot", par)] = ts
    st.close()
    P.dma_wait_all("g", prefixes=("go",))


def emit_postln_add(C, srcname, srcch, b, cs, rot, tready):
    P, ps, sb, Tc = C.P, C.ps, C.sb, C.Tc
    xtl, sgt = C.xtl, C.sgt
    first = True
    for k in range(KM):
        t6 = P.op("t", lambda E, k=k, first=first: E.matmul(
            ps[6][0:1, 0:Tc], sb["onesf"][:, 0:1],
            srcch[:, k * Tc:(k + 1) * Tc], start=first, stop=(k == KM - 1)),
            inc="t1", waits=[tready] + ([C.psum_free.get(6)]
                                        if first else []))
        tsq = P.op("s", lambda E, k=k: E.activation(
            sgt[:, 0:Tc], srcch[:, k * Tc:(k + 1) * Tc], AF.Square),
            writes=(C.sgtname,), inc="s1",
            waits=[tready, rot.get("sgt_user")], clears=(C.sgtname,))
        t7 = P.op("t", lambda E, first=first, k=k: E.matmul(
            ps[7][0:1, 0:Tc], sb["onesf"][:, 0:1], sgt[:, 0:Tc],
            start=first, stop=(k == KM - 1)), inc="t1",
            waits=[tsq] + ([C.psum_free.get(7)] if first else []))
        rot["sgt_user"] = t7
        first = False
    tmm = P.tok("t1")
    t4, t5 = ln_rows(C, ps[6], ps[7], 1.0 / DM, 0, tmm)
    C.psum_free[6] = P.tok("v1")
    C.psum_free[7] = C.psum_free[6]
    tld = {}
    for k in range(4):
        tld[k] = P.op("q", _dma(xtl[:, k * Tc:(k + 1) * Tc],
                                C.ds["resid"][b, k, :, cs]),
                      inc=f"qz{k}", inc_n=16,
                      waits=[rot.get(("xtl", k))])
    for k in range(KM):
        P.op("v", lambda E, k=k: E.tensor_tensor(
            srcch[:, k * Tc:(k + 1) * Tc], srcch[:, k * Tc:(k + 1) * Tc],
            ps[5][:, 0:Tc], ALU.mult),
            reads=(srcname,), writes=(f"plnK{k}",),
            waits=[t4, t5] if k == 0 else ())
    for k in range(KM):
        P.op("v", lambda E, k=k: E.tensor_tensor(
            srcch[:, k * Tc:(k + 1) * Tc], srcch[:, k * Tc:(k + 1) * Tc],
            ps[4][:, 0:Tc], ALU.subtract),
            reads=(f"plnK{k}",), writes=(f"plnK{k}",))
    ta = None
    for k in range(KM):
        par = k % 4
        ta = P.op("v", lambda E, k=k, par=par: E.tensor_tensor(
            srcch[:, k * Tc:(k + 1) * Tc], srcch[:, k * Tc:(k + 1) * Tc],
            xtl[:, par * Tc:(par + 1) * Tc], ALU.add),
            reads=(f"plnK{k}",), writes=(f"plnK{k}",), inc="v1",
            waits=[tld[k]])
        rot[("xtl", par)] = ta
        if k + 4 < KM:
            tld[k + 4] = P.op("q", _dma(xtl[:, par * Tc:(par + 1) * Tc],
                                        C.ds["resid"][b, k + 4, :, cs]),
                              inc=f"qz{par}", inc_n=16, waits=[ta])
    tadd = P.tok("v1")
    C.psum_free[4] = tadd
    C.psum_free[5] = tadd
    tst = P.op("s", lambda E, b=b, cs=cs: E.dma_start(
        out=C.ds["resid"][b, :, :, cs].rearrange("k p t -> p k t"),
        in_=srcch.ap().rearrange("p (k t) -> p k t", k=KM)),
        inc="gr0", inc_n=16, waits=[tadd])
    return tadd, tst


def emit_qdq_pp(C, src, dst, inv_col, s_col, lo, hi, tmp, tmpname,
                reads=(), writes=(), waits=()):
    P = C.P
    M = 12582912.0
    P.op("v", lambda E: E.tensor_scalar(tmp, src, inv_col, M, ALU.mult,
                                        ALU.add),
         reads=reads, writes=(tmpname,), waits=waits)
    P.op("v", lambda E: E.tensor_scalar(tmp, tmp, M, float(lo),
                                        ALU.subtract, ALU.max),
         reads=(tmpname,), writes=(tmpname,))
    return P.op("v", lambda E: E.tensor_scalar(dst, tmp, float(hi), s_col,
                                               ALU.min, ALU.mult),
                reads=(tmpname,), writes=writes, inc="v1")

def emit_qdq_bc(C, src, dst, sinv_b, s_b, lo, hi, tmp, tmpname,
                reads=(), writes=(), waits=(), bcreads=()):
    P = C.P
    M = 12582912.0
    P.op("v", lambda E: E.tensor_tensor(tmp, src, sinv_b, ALU.mult),
         reads=tuple(reads) + tuple(bcreads), writes=(tmpname,), waits=waits)
    P.op("v", lambda E: E.tensor_scalar(tmp, tmp, M, M, ALU.add,
                                        ALU.subtract),
         reads=(tmpname,), writes=(tmpname,))
    P.op("v", lambda E: E.tensor_scalar(tmp, tmp, float(lo), float(hi),
                                        ALU.max, ALU.min),
         reads=(tmpname,), writes=(tmpname,))
    return P.op("v", lambda E: E.tensor_tensor(dst, tmp, s_b, ALU.mult),
                reads=(tmpname,) + tuple(bcreads), writes=writes, inc="v1")


def emit_armax(C, macc_list, scl, derived, ncore):
    P, nc, ds = C.P, C.nc, C.ds
    row = C.ccrow
    tz = P.op("g", lambda E: E.memset(row[0:1, :], 0.0), writes=("ccrow",),
              inc="g1", waits=[C.cc_prev])
    toks = []
    for i, (m, name) in enumerate(macc_list):
        tr = P.op("v", lambda E, m=m, i=i: E.reduce_max(
            C.mrow[:, 2 * i:2 * i + 1], m, AX.X), reads=(name,),
            writes=("mrow",), inc="v1",
            waits=[getattr(C, "mrow_prev", None)])
        tp = P.op("g", lambda E, i=i: E.partition_all_reduce(
            C.mrow[:, 2 * i + 1:2 * i + 2], C.mrow[:, 2 * i:2 * i + 1],
            128, ReduceOp.max),
            reads=("mrow",), writes=("mrow",), inc="g1", waits=[tr])
        tc = P.op("g", lambda E, i=i: E.tensor_copy(
            row[0:1, i:i + 1], C.mrow[0:1, 2 * i + 1:2 * i + 2]),
            reads=("mrow",), writes=("ccrow",), inc="g1",
            waits=[tp, tz])
        toks.append(tc)
    td = P.op("g", _dma(ds["cc_in"][:, :], row[0:1, 0:8]), inc="gcc",
              inc_n=16, waits=toks)
    tcc = P.op("g", lambda E, ncore=ncore: E.collective_compute(
        "AllReduce", ALU.max, replica_groups=[list(range(ncore))],
        ins=[ds["cc_in"].opt()], outs=[ds["cc_out"].opt()]),
        inc="g1", waits=[td])
    tl = P.op("g", _dma(row[0:1, 8:16], ds["cc_out"][:, :]), inc="gcc",
              inc_n=16, waits=[tcc])
    tout = None
    for i, _ in enumerate(macc_list):
        tout = P.op("g", lambda E, i=i: E.partition_broadcast(
            scl[:, i:i + 1], row[0:1, 8 + i:9 + i]),
            reads=("ccrow",), writes=("scl",), inc="g1", waits=[tl])
    tv = None
    for i, (qmax, inv_col, s_col) in enumerate(derived):
        P.op("v", lambda E, i=i, qmax=qmax, s_col=s_col:
             E.tensor_scalar(scl[:, s_col:s_col + 1],
                             scl[:, i:i + 1], 1.0 / qmax, 1e-12,
                             ALU.mult, ALU.max),
             reads=("scl",), writes=("scl",), waits=[tout])
        tv = P.op("v", lambda E, inv_col=inv_col, s_col=s_col:
                  E.reciprocal(scl[:, inv_col:inv_col + 1],
                               scl[:, s_col:s_col + 1]),
                  reads=("scl",), writes=("scl",), inc="v1")
    C.cc_prev = P.tok("g1")
    C.mrow_prev = C.cc_prev
    return tv


def s2_ffn0(C, ncore):
    P, nc, ds, sb, ps, Tc, L = C.P, C.nc, C.ds, C.sb, C.ps, C.Tc, C.L
    st = ExitStack()
    stw = ExitStack()
    def S(name, shape, dt, stack=None):
        return (stack or st).enter_context(nc.sbuf_tensor(name, shape, dt))
    wupg = stw.enter_context(
        nc.sbuf_tensor("f_wupg", [128, KM * HF], BF16, side="right"))
    wgat = stw.enter_context(
        nc.sbuf_tensor("f_wgat", [128, KM * HF], BF16, side="right"))
    xch = S("f_xch", [128, KM * Tc], F32)
    xqb = S("f_xqb", [128, KM * Tc], BF16)
    tile = S("f_tile", [128, 8 * Tc], F32)
    tmp2 = S("f_tmp2", [128, Tc], F32)
    sqb2 = S("f_sqb", [128, Tc], BF16)
    qt2 = S("f_qt2", [128, Tc], F32)
    macc = S("f_macc", [128, 32], F32)
    scl = S("f_scl", [128, 24], F32)
    mrow = S("f_mrow", [128, 4], F32)
    ccrow = S("f_ccrow", [1, 16], F32)
    xtl2 = S("f_xtl", [128, 4 * Tc], F32)
    C.xtl = xtl2
    C.sgt = tmp2
    C.sgtname = "f_tmp2"
    C.mrow, C.ccrow = mrow, ccrow
    C.scl = scl
    C.cc_prev = None
    C.qtmp2, C.qtmp2name = qt2, "f_qt2"
    rot = {}
    NCH = L // Tc
    XCHN = tuple(f"f_xch{k}" for k in range(KM))

    def absred(slot, src_ap, reads=(), waits=()):
        return P.op("v", lambda E, slot=slot, src_ap=src_ap: E.tensor_reduce(
            macc[:, slot:slot + 1], src_ap, AX.X, ALU.max,
            apply_absolute_value=True),
            reads=reads, writes=(f"f_mc{slot}",), inc="v1", waits=waits)

    def fold(s0, n, acc, accname, waits=()):
        P.op("v", lambda E, s0=s0, n=n: E.tensor_reduce(
            macc[:, 24:25], macc[:, s0:s0 + n], AX.X, ALU.max),
            reads=tuple(f"f_mc{s0+i}" for i in range(n)),
            writes=("f_mfold",), waits=waits)
        return P.op("v", lambda E, acc=acc: E.tensor_tensor(
            macc[:, acc:acc + 1], macc[:, acc:acc + 1], macc[:, 24:25],
            ALU.max),
            reads=("f_mfold", accname), writes=(accname,), inc="v1")

    twu = P.op("q", _dma(wupg.ap().rearrange("p (k c) -> p k c", c=HF),
                         ds["w_upT"].rearrange("(k p) c -> p k c", p=128)),
               inc="qw0", inc_n=16)
    twg = P.op("q", _dma(wgat.ap().rearrange("p (k c) -> p k c", c=HF),
                         ds["w_gateT"].rearrange("(k p) c -> p k c", p=128)),
               inc="qw1", inc_n=16)

    P.op("v", lambda E: E.memset(macc[:, 16:17], 0.0), writes=("f_ma0",))
    for b in range(BPC):
        for c in range(NCH):
            cs = slice(c * Tc, (c + 1) * Tc)
            first = True
            for k in range(KM):
                par = k % 2
                tl = P.op("q", _dma(xtl2[:, par * Tc:(par + 1) * Tc],
                                    ds["resid"][b, k, :, cs]),
                          inc=f"qt{par}", inc_n=16,
                          waits=[rot.get(("xtlA", par)),
                                 rot.get(("xtlA2", par))])
                tc1 = P.op("v", lambda E, k=k, par=par: E.tensor_copy(
                    xqb[:, k * Tc:(k + 1) * Tc],
                    xtl2[:, par * Tc:(par + 1) * Tc]),
                    writes=(f"f_xqb{k}",), inc="v1",
                    waits=[tl] + ([rot.get("xqb_user")] if k == 0 else []),
                    clears=tuple(f"f_xqb{i}" for i in range(KM))
                    if k == 0 else ())
                tc2 = P.op("s", lambda E, par=par: E.activation(
                    sqb2[:], xtl2[:, par * Tc:(par + 1) * Tc], AF.Square),
                    writes=("f_sqb",), inc="s1",
                    waits=[tl, rot.get("sqb_user")], clears=("f_sqb",))
                tc3 = P.op("v", lambda E, k=k, par=par: E.tensor_copy(
                    xch[:, k * Tc:(k + 1) * Tc],
                    xtl2[:, par * Tc:(par + 1) * Tc]),
                    writes=(f"f_xch{k}",), inc="v1",
                    waits=[tl] + ([rot.get("xch_user")] if k == 0 else []),
                    clears=XCHN + ("f_xchW",) if k == 0 else ())
                rot[("xtlA", par)] = tc3
                rot[("xtlA2", par)] = tc2
                P.op("t", lambda E, k=k, first=first: E.matmul(
                    ps[6][0:1, 0:Tc], sb["onesb"][:, 0:1],
                    xqb[:, k * Tc:(k + 1) * Tc], start=first,
                    stop=(k == KM - 1)),
                    inc="t1", waits=[tc1] + ([C.psum_free.get(6)]
                                             if first else []))
                t7 = P.op("t", lambda E, first=first, k=k: E.matmul(
                    ps[7][0:1, 0:Tc], sb["onesb"][:, 0:1], sqb2[:],
                    start=first, stop=(k == KM - 1)),
                    inc="t1", waits=[tc2] + ([C.psum_free.get(7)]
                                             if first else []))
                rot["sqb_user"] = t7
                first = False
            tmm = P.tok("t1")
            t4, t5 = ln_rows(C, ps[6], ps[7], 1.0 / DM, 0, tmm)
            C.psum_free[6] = P.tok("v1")
            C.psum_free[7] = C.psum_free[6]
            for k in range(KM):
                P.op("v", lambda E, k=k: E.tensor_tensor(
                    xch[:, k * Tc:(k + 1) * Tc], xch[:, k * Tc:(k + 1) * Tc],
                    ps[5][:, 0:Tc], ALU.mult),
                    reads=(f"f_xch{k}",), writes=(f"f_xch{k}",),
                    waits=[t4, t5] if k == 0 else ())
            tn = None
            for k in range(KM):
                tn = P.op("v", lambda E, k=k: E.tensor_tensor(
                    xch[:, k * Tc:(k + 1) * Tc], xch[:, k * Tc:(k + 1) * Tc],
                    ps[4][:, 0:Tc], ALU.subtract),
                    reads=(f"f_xch{k}",), writes=(f"f_xch{k}",),
                    inc="v1" if k == KM - 1 else None)
            for k in range(KM):
                absred(k, xch[:, k * Tc:(k + 1) * Tc],
                       reads=(f"f_xch{k}",))
            rot["xchW_user"] = fold(0, KM, 16, "f_ma0")
            C.psum_free[4] = tn
            C.psum_free[5] = tn
            tst = P.op("s", lambda E, b=b, cs=cs: E.dma_start(
                out=ds["xln1"][b, :, :, cs].rearrange("k p t -> p k t"),
                in_=xch.ap().rearrange("p (k t) -> p k t", k=KM)),
                inc="gx0", inc_n=16, waits=[tn])
            rot["xch_user"] = tst
            rot["xqb_user"] = tmm
    tm0 = emit_armax(C, [(macc[:, 16:17], "f_ma0")], scl,
                     [(32767.0, 8, 9)], ncore)

    P.dma_wait_all("q", prefixes=("gx", "gu", "gv"))
    P.op("v", lambda E: E.memset(macc[:, 17:19], 0.0),
         writes=("f_ma1", "f_ma2"))
    M = 12582912.0
    for b in range(BPC):
        for c in range(NCH):
            cs = slice(c * Tc, (c + 1) * Tc)
            lds = []
            for k in range(KM):
                lds.append(P.op("q", _dma(xch[:, k * Tc:(k + 1) * Tc],
                                          ds["xln1"][b, k, :, cs]),
                           inc=f"qt{k % 4}", inc_n=16,
                           waits=[rot.get("xchW_user")]))
            P.op("v", lambda E: E.tensor_scalar(
                xch[:], xch[:], scl[:, 8:9], M, ALU.mult, ALU.add),
                reads=("scl",), writes=("f_xchW",),
                waits=lds + [tm0], clears=XCHN + ("f_xchW",))
            P.op("v", lambda E: E.tensor_scalar(
                xch[:], xch[:], M, -32768.0, ALU.subtract, ALU.max),
                reads=("f_xchW",), writes=("f_xchW",))
            tqd = P.op("v", lambda E: E.tensor_scalar(
                xqb[:], xch[:], 32767.0, scl[:, 9:10], ALU.min, ALU.mult),
                reads=("f_xchW",), writes=("f_xqbW",), inc="v1",
                waits=[rot.get("xqb_user")],
                clears=tuple(f"f_xqb{i}" for i in range(KM)) + ("f_xqbW",))
            rot["xchW_user"] = tqd
            for which, (wst, tw, slot0, acc, accname, spill, sem) in \
                    enumerate([(wupg, twu, 0, 17, "f_ma1", "up_raw", "gu"),
                               (wgat, twg, 8, 18, "f_ma2", "gate_raw",
                                "gv")]):
                for f in range(KF):
                    par, pi = f % 2, (2 * which + f % 2) % 4
                    tmm = None
                    for k in range(KM):
                        tmm = P.op("t", lambda E, k=k, f=f, pi=pi, wst=wst:
                                   E.matmul(
                                       ps[pi][:, 0:Tc],
                                       wst[:, k * HF + f * 128:
                                           k * HF + (f + 1) * 128],
                                       xqb[:, k * Tc:(k + 1) * Tc],
                                       start=(k == 0), stop=(k == KM - 1)),
                                   waits=([tqd, tw, C.psum_free.get(pi),
                                           rot.get(("psr", pi))]
                                          if k == 0 else []),
                                   inc="t1" if k == KM - 1 else None)
                    te = P.op("s", lambda E, pi=pi, which=which, par=par:
                              E.activation(
                                  tile[:, (2 * which + par) * Tc:
                                       (2 * which + par + 1) * Tc],
                                  ps[pi][:, 0:Tc], AF.Copy),
                              writes=(f"f_tile{2 * which + par}",), inc="s1",
                              waits=[tmm, rot.get(("tile", which, par))],
                              clears=(f"f_tile{2 * which + par}",))
                    trd = absred(slot0 + f % 8, ps[pi][:, 0:Tc],
                                 waits=[tmm, te])
                    rot[("psr", pi)] = trd
                    C.psum_free[pi] = te
                    if f % 8 == 7:
                        fold(slot0, 8, acc, accname)
                    tsp = P.op("s", lambda E, b=b, f=f, cs=cs, which=which,
                               par=par, spill=spill: E.dma_start(
                                   out=ds[spill][b, f, :, cs],
                                   in_=tile[:, (2 * which + par) * Tc:
                                            (2 * which + par + 1) * Tc]),
                               inc=f"{sem}{par}", inc_n=16, waits=[te])
                    rot[("tile", which, par)] = tsp
            rot["xqb_user"] = P.tok("t1")
    P.barrier()
    stw.close()
    tm12 = emit_armax(C, [(macc[:, 17:18], "f_ma1"),
                          (macc[:, 18:19], "f_ma2")], scl,
                      [(32767.0, 10, 11), (32767.0, 12, 13)], ncore)

    P.dma_wait_all("q", prefixes=("gx", "gu", "gv"))
    P.op("v", lambda E: E.memset(macc[:, 19:20], 0.0), writes=("f_ma3",))
    TAN = ("f_tileA",)
    TBN = ("f_tileB",)
    for b in range(BPC):
        for c in range(NCH):
            cs = slice(c * Tc, (c + 1) * Tc)
            for g in range(KF // 4):
                f0 = 4 * g
                lds = []
                for j in range(4):
                    lds.append(P.op(
                        "q", lambda E, j=j, f0=f0, b=b, cs=cs: E.dma_start(
                            out=tile[:, j * Tc:(j + 1) * Tc],
                            in_=ds["gate_raw"][b, f0 + j, :, cs]),
                        inc=f"qt{j}", inc_n=16,
                        waits=[rot.get("t3a")]))
                tq0 = P.op("v", lambda E: E.tensor_scalar(
                    tile[:, 4 * Tc:8 * Tc], tile[:, 0:4 * Tc],
                    scl[:, 12:13], M, ALU.mult, ALU.add),
                    reads=("scl",), writes=TBN,
                    waits=lds + [tm12] + list(rot.get("t3b", ())),
                    clears=TAN + TBN, inc="v1")
                rot["t3a"] = tq0
                P.op("v", lambda E: E.tensor_scalar(
                    tile[:, 4 * Tc:8 * Tc], tile[:, 4 * Tc:8 * Tc], M,
                    -32768.0, ALU.subtract, ALU.max),
                    reads=TBN, writes=TBN)
                tgq = P.op("v", lambda E: E.tensor_scalar(
                    tile[:, 4 * Tc:8 * Tc], tile[:, 4 * Tc:8 * Tc],
                    32767.0, scl[:, 13:14], ALU.min, ALU.mult),
                    reads=TBN, writes=TBN, inc="v1")
                tsg = P.op("s", lambda E: E.activation(
                    xch[:, 0:4 * Tc], tile[:, 4 * Tc:8 * Tc], AF.Sigmoid),
                    writes=("f_sig",), inc="s1",
                    waits=[tgq, rot.get("sig_user")], clears=("f_sig",))
                P.op("v", lambda E: E.tensor_scalar(
                    xch[:, 0:4 * Tc], xch[:, 0:4 * Tc], 65536.0, M,
                    ALU.mult, ALU.add),
                    reads=(), writes=("f_sig",), waits=[tsg],
                    clears=("f_xchW",) + XCHN)
                P.op("v", lambda E: E.tensor_scalar(
                    xch[:, 0:4 * Tc], xch[:, 0:4 * Tc], M, 0.0,
                    ALU.subtract, ALU.max),
                    reads=("f_sig",), writes=("f_sig",))
                P.op("v", lambda E: E.tensor_scalar(
                    xch[:, 0:4 * Tc], xch[:, 0:4 * Tc], 65535.0,
                    1.0 / 65536.0, ALU.min, ALU.mult),
                    reads=("f_sig",), writes=("f_sig",))
                tg2 = P.op("v", lambda E: E.tensor_tensor(
                    tile[:, 4 * Tc:8 * Tc], tile[:, 4 * Tc:8 * Tc],
                    xch[:, 0:4 * Tc], ALU.mult),
                    reads=("f_sig",) + TBN, writes=TBN, inc="v1")
                rot["sig_user"] = tg2
                trd = None
                for j in range(4):
                    trd = absred((4 * g + j) % 8,
                                 tile[:, (4 + j) * Tc:(5 + j) * Tc],
                                 reads=TBN)
                if g % 2 == 1:
                    fold(0, 8, 19, "f_ma3")
                sps = []
                for j in range(4):
                    sps.append(P.op(
                        "s", lambda E, j=j, f0=f0, b=b, cs=cs: E.dma_start(
                            out=ds["g2pre"][b, f0 + j, :, cs],
                            in_=tile[:, (4 + j) * Tc:(5 + j) * Tc]),
                        inc=f"gu{j % 2}", inc_n=16, waits=[tg2]))
                rot["t3b"] = sps + [trd]
    tm3 = emit_armax(C, [(macc[:, 19:20], "f_ma3")], scl,
                     [(32767.0, 14, 15)], ncore)

    P.dma_wait_all("q", prefixes=("gx", "gu", "gv"))
    P.op("v", lambda E: E.memset(macc[:, 20:21], 0.0), writes=("f_ma4",))
    for b in range(BPC):
        for c in range(NCH):
            cs = slice(c * Tc, (c + 1) * Tc)
            for g in range(KF // 4):
                f0 = 4 * g
                ldA, ldB = [], []
                for j in range(4):
                    ldA.append(P.op(
                        "q", lambda E, j=j, f0=f0, b=b, cs=cs: E.dma_start(
                            out=tile[:, j * Tc:(j + 1) * Tc],
                            in_=ds["g2pre"][b, f0 + j, :, cs]),
                        inc=f"qt{j}", inc_n=16,
                        waits=list(rot.get("t4a", ())) + [rot.get("t3a")]))
                    ldB.append(P.op(
                        "q", lambda E, j=j, f0=f0, b=b, cs=cs: E.dma_start(
                            out=tile[:, (4 + j) * Tc:(5 + j) * Tc],
                            in_=ds["up_raw"][b, f0 + j, :, cs]),
                        inc=f"qo{j}", inc_n=16,
                        waits=[rot.get("t4b")] + list(rot.get("t3b", ()))))
                P.op("v", lambda E: E.tensor_scalar(
                    tile[:, 0:4 * Tc], tile[:, 0:4 * Tc],
                    scl[:, 14:15], M, ALU.mult, ALU.add),
                    reads=("scl",), writes=TAN, waits=ldA + [tm3],
                    clears=TAN + TBN)
                P.op("v", lambda E: E.tensor_scalar(
                    tile[:, 0:4 * Tc], tile[:, 0:4 * Tc], M, -32768.0,
                    ALU.subtract, ALU.max),
                    reads=TAN, writes=TAN)
                P.op("v", lambda E: E.tensor_scalar(
                    tile[:, 0:4 * Tc], tile[:, 0:4 * Tc], 32767.0,
                    scl[:, 15:16], ALU.min, ALU.mult),
                    reads=TAN, writes=TAN)
                P.op("v", lambda E: E.tensor_scalar(
                    tile[:, 4 * Tc:8 * Tc], tile[:, 4 * Tc:8 * Tc],
                    scl[:, 10:11], M, ALU.mult, ALU.add),
                    reads=TBN, writes=TBN, waits=ldB)
                P.op("v", lambda E: E.tensor_scalar(
                    tile[:, 4 * Tc:8 * Tc], tile[:, 4 * Tc:8 * Tc], M,
                    -32768.0, ALU.subtract, ALU.max),
                    reads=TBN, writes=TBN)
                P.op("v", lambda E: E.tensor_scalar(
                    tile[:, 4 * Tc:8 * Tc], tile[:, 4 * Tc:8 * Tc],
                    32767.0, scl[:, 11:12], ALU.min, ALU.mult),
                    reads=TBN, writes=TBN)
                to = P.op("v", lambda E: E.tensor_tensor(
                    tile[:, 0:4 * Tc], tile[:, 0:4 * Tc],
                    tile[:, 4 * Tc:8 * Tc], ALU.mult),
                    reads=TAN + TBN, writes=TAN, inc="v1")
                rot["t4b"] = to
                trd = None
                for j in range(4):
                    trd = absred((4 * g + j) % 8,
                                 tile[:, j * Tc:(j + 1) * Tc], reads=TAN)
                if g % 2 == 1:
                    fold(0, 8, 20, "f_ma4")
                sps = []
                for j in range(4):
                    sps.append(P.op(
                        "s", lambda E, j=j, f0=f0, b=b, cs=cs: E.dma_start(
                            out=ds["o_pre"][b, f0 + j, :, cs],
                            in_=tile[:, j * Tc:(j + 1) * Tc]),
                        inc=f"gu{j % 2}", inc_n=16, waits=[to]))
                rot["t4a"] = sps + [trd]
    tm4 = emit_armax(C, [(macc[:, 20:21], "f_ma4")], scl,
                     [(32767.0, 16, 17)], ncore)

    P.dma_wait_all("q", prefixes=("gx", "gu", "gv"))
    och = S("f_och", [128, KF * Tc], BF16)
    dnf = S("f_dnf", [128, 2 * KF * 128], BF16)
    dch = S("f_dch", [128, KM * Tc], F32)
    for b in range(BPC):
        for c in range(NCH):
            cs = slice(c * Tc, (c + 1) * Tc)
            for g in range(KF // 4):
                f0 = 4 * g
                lds = []
                for j in range(4):
                    lds.append(P.op(
                        "q", lambda E, j=j, f0=f0, b=b, cs=cs: E.dma_start(
                            out=tile[:, j * Tc:(j + 1) * Tc],
                            in_=ds["o_pre"][b, f0 + j, :, cs]),
                        inc=f"qt{j}", inc_n=16,
                        waits=[rot.get("t5a")] + list(rot.get("t4a", ()))))
                t5a = P.op("v", lambda E: E.tensor_scalar(
                    tile[:, 4 * Tc:8 * Tc], tile[:, 0:4 * Tc],
                    scl[:, 16:17], M, ALU.mult, ALU.add),
                    reads=("scl",), writes=TBN,
                    waits=lds + [tm4, rot.get("t5b"), rot.get("t4b")],
                    clears=TAN + TBN, inc="v1")
                rot["t5a"] = t5a
                P.op("v", lambda E: E.tensor_scalar(
                    tile[:, 4 * Tc:8 * Tc], tile[:, 4 * Tc:8 * Tc], M,
                    -32768.0, ALU.subtract, ALU.max),
                    reads=TBN, writes=TBN)
                tq = P.op("v", lambda E, f0=f0: E.tensor_scalar(
                    och[:, f0 * Tc:(f0 + 4) * Tc], tile[:, 4 * Tc:8 * Tc],
                    32767.0, scl[:, 17:18], ALU.min, ALU.mult),
                    reads=TBN, writes=("f_och",), inc="v1",
                    waits=[rot.get("och_user")] if g == 0 else (),
                    clears=("f_och",) if g == 0 else ())
                rot["t5b"] = tq
            tqd = P.tok("v1")
            for f in range(KM):
                par, pi = f % 2, f % 2
                tw = P.op("q", lambda E, f=f, par=par: E.dma_start(
                    out=dnf[:, par * KF * 128:(par + 1) * KF * 128],
                    in_=ds["w_downT"][f]),
                    inc=f"qp{par}", inc_n=16, waits=[rot.get(("dnf", par))])
                tmm = None
                for k in range(KF):
                    tmm = P.op("t", lambda E, k=k, par=par, pi=pi: E.matmul(
                        ps[pi][:, 0:Tc],
                        dnf[:, par * KF * 128 + k * 128:
                            par * KF * 128 + (k + 1) * 128],
                        och[:, k * Tc:(k + 1) * Tc],
                        start=(k == 0), stop=(k == KF - 1)),
                        waits=([tw, tqd, C.psum_free.get(pi)]
                               if k == 0 else []),
                        inc="t1" if k == KF - 1 else None)
                rot[("dnf", par)] = tmm
                te = P.op("s", lambda E, f=f, pi=pi: E.activation(
                    dch[:, f * Tc:(f + 1) * Tc], ps[pi][:, 0:Tc], AF.Copy),
                    writes=("f_dch",), inc="s1",
                    waits=[tmm] + ([rot.get("dch_user")] if f == 0 else []),
                    clears=("f_dch",) if f == 0 else ())
                C.psum_free[pi] = te
            tdn = P.tok("s1")
            rot["och_user"] = P.tok("t1")
            tadd, tst = emit_postln_add(C, "f_dch", dch, b, cs, rot, tdn)
            rot["dch_user"] = tst
    st.close()
    return P.tok("gr0")

def s3_attn(C):
    P, nc, ds, sb, ps, Tc, L = C.P, C.nc, C.ds, C.sb, C.ps, C.Tc, C.L
    starts = _win_starts(L)
    stP = ExitStack()
    _uid = [0]
    def S(name, shape, dt, stack=None):
        _uid[0] += 1
        return (stack or stP).enter_context(
            nc.sbuf_tensor(f"{name}_{_uid[0]}", shape, dt))
    qb = S("a_qb", [128, KM * L], BF16)
    kb = S("a_kb", [128, KM * L], BF16)
    vT = S("a_vT", [128, (L // 128) * DM], BF16)
    bqt = S("a_bqt", [128, 3 * KM], F32)
    bvrow = S("a_bvrow", [1, DM], F32)
    bvrep = S("a_bvrep", [128, DM], F32)
    cinv = S("a_cinv", [128, L], BF16)
    srow = S("a_srow", [128, 8], F32)
    xtl3 = S("a_xtl", [128, 4 * Tc], F32)
    C.xtl = xtl3
    sgt4 = S("a_sgt", [128, Tc], F32)
    C.sgt, C.sgtname = sgt4, "a_sgt"
    rot = {}
    NCH = L // Tc

    tb1 = P.op("q", _dma_nc(bqt[:, 0:KM],
                         C.din["L1_attn_bq"].rearrange("(k p) -> p k", p=128)),
               inc="qc", inc_n=16)
    tb2 = P.op("q", _dma_nc(bqt[:, KM:2 * KM],
                         C.din["L1_attn_bk"].rearrange("(k p) -> p k", p=128)),
               inc="qc", inc_n=16)
    tb3 = P.op("q", _dma_nc(bqt[:, 2 * KM:3 * KM],
                         C.din["L1_attn_bo"].rearrange("(k p) -> p k", p=128)),
               inc="qc", inc_n=16)
    tb4 = P.op("q", _dma(bvrow[0:1, :],
                         C.din["L1_attn_bv"].rearrange("(o c) -> o c", o=1)),
               inc="qc", inc_n=16)
    tm = P.op("t", lambda E: E.matmul(ps[4][:, 0:512], sb["onesf"][0:1, 0:128],
                                      bvrow[0:1, 0:512], start=True,
                                      stop=True),
              inc="t1", waits=[tb4, C.psum_free.get(4)])
    tm2 = P.op("t", lambda E: E.matmul(ps[5][:, 0:512],
                                       sb["onesf"][0:1, 0:128],
                                       bvrow[0:1, 512:1024], start=True,
                                       stop=True),
               inc="t1", waits=[tb4, C.psum_free.get(5)])
    te1 = P.op("s", lambda E: E.activation(bvrep[:, 0:512], ps[4][:, 0:512],
                                           AF.Copy),
               writes=("a_bvrep",), inc="s1", waits=[tm])
    te2 = P.op("s", lambda E: E.activation(bvrep[:, 512:1024],
                                           ps[5][:, 0:512], AF.Copy),
               writes=("a_bvrep",), inc="s1", waits=[tm2])
    C.psum_free[4] = te1
    C.psum_free[5] = te2
    counts = np.zeros(L, np.int32)
    for st0 in starts:
        counts[st0:st0 + 256] += 1
    runs, cur, r0 = [], counts[0], 0
    for i in range(1, L):
        if counts[i] != cur:
            runs.append((r0, i, cur)); r0, cur = i, counts[i]
    runs.append((r0, L, cur))
    tcv = None
    for (a, b_, cval) in runs:
        tcv = P.op("v", lambda E, a=a, b_=b_, cval=cval: E.memset(
            cinv[:, a:b_], 1.0 / float(cval)), writes=("a_cinv",), inc="v1")

    def _one_batch(b):
        stA = ExitStack()
        xln2 = S("a_xln2", [128, KM * L], BF16, stA)
        wqb = S("a_wqb", [128, KM * DM], BF16, stA)
        sqb3 = S("a_sqb", [128, Tc], BF16, stA)
        for c in range(NCH):
            cs = slice(c * Tc, (c + 1) * Tc)
            first = True
            for k in range(KM):
                par = k % 2
                tl = P.op("q", _dma(xtl3[:, par * Tc:(par + 1) * Tc],
                                    ds["resid"][b, k, :, cs]),
                          inc=f"qt{par}", inc_n=16,
                          waits=[rot.get(("xtl", par)),
                                 rot.get(("xtl2", par))])
                tc1 = P.op("v", lambda E, k=k, c=c, par=par: E.tensor_copy(
                    xln2[:, k * L + c * Tc:k * L + (c + 1) * Tc],
                    xtl3[:, par * Tc:(par + 1) * Tc]),
                    writes=("a_xln2",), inc="v1",
                    waits=[tl, rot.get("xln2_user")])
                tc2 = P.op("s", lambda E, par=par: E.activation(
                    sqb3[:], xtl3[:, par * Tc:(par + 1) * Tc], AF.Square),
                    writes=("a_sqb",), inc="s1",
                    waits=[tl, rot.get("sqb_user")])
                rot[("xtl", par)] = tc1
                rot[("xtl2", par)] = tc2
                t6 = P.op("t", lambda E, k=k, c=c, first=first: E.matmul(
                    ps[6][0:1, 0:Tc], sb["onesb"][:, 0:1],
                    xln2[:, k * L + c * Tc:k * L + (c + 1) * Tc],
                    start=first, stop=(k == KM - 1)),
                    inc="t1", waits=[tc1] + ([C.psum_free.get(6)]
                                             if first else []))
                t7 = P.op("t", lambda E, first=first, k=k: E.matmul(
                    ps[7][0:1, 0:Tc], sb["onesb"][:, 0:1], sqb3[:],
                    start=first, stop=(k == KM - 1)),
                    inc="t1", waits=[tc2] + ([C.psum_free.get(7)]
                                             if first else []))
                rot["sqb_user"] = t7
                first = False
            tmm = P.tok("t1")
            t4, t5 = ln_rows(C, ps[6], ps[7], 1.0 / DM, 0, tmm)
            C.psum_free[6] = P.tok("v1")
            C.psum_free[7] = C.psum_free[6]
            tn = None
            for k in range(KM):
                P.op("v", lambda E, k=k, c=c: E.tensor_tensor(
                    xln2[:, k * L + c * Tc:k * L + (c + 1) * Tc],
                    xln2[:, k * L + c * Tc:k * L + (c + 1) * Tc],
                    ps[5][:, 0:Tc], ALU.mult),
                    reads=("a_xln2",), writes=("a_xln2",), waits=[t4, t5])
                tn = P.op("v", lambda E, k=k, c=c: E.tensor_tensor(
                    xln2[:, k * L + c * Tc:k * L + (c + 1) * Tc],
                    xln2[:, k * L + c * Tc:k * L + (c + 1) * Tc],
                    ps[4][:, 0:Tc], ALU.subtract),
                    reads=("a_xln2",), writes=("a_xln2",),
                    inc="v1" if k == KM - 1 else None)
            C.psum_free[4] = tn
            C.psum_free[5] = tn
            rot["xln2_user"] = tn
        tln = P.tok("v1")

        for which, (wname, dst, dstname, bcol) in enumerate(
                [("w_q", qb, "a_qb", 0), ("w_k", kb, "a_kb", KM)]):
            tw = P.op("q", lambda E, wname=wname: E.dma_start(
                out=wqb.ap().rearrange("p (k c) -> p k c", c=DM),
                in_=ds[wname].rearrange("(k p) c -> p k c", p=128)),
                inc="qw0", inc_n=16, waits=[rot.get("wqb_user")])
            for f in range(KM):
                for c in range(NCH):
                    pi = (f * NCH + c) % 4
                    tmm = None
                    for k in range(KM):
                        tmm = P.op("t", lambda E, k=k, f=f, c=c, pi=pi:
                                   E.matmul(
                                       ps[pi][:, 0:Tc],
                                       wqb[:, k * DM + f * 128:
                                           k * DM + (f + 1) * 128],
                                       xln2[:, k * L + c * Tc:
                                            k * L + (c + 1) * Tc],
                                       start=(k == 0), stop=(k == KM - 1)),
                                   waits=([tw, tln, C.psum_free.get(pi)]
                                          if k == 0 else []),
                                   inc="t1" if k == KM - 1 else None)
                    te = P.op("v", lambda E, f=f, c=c, pi=pi, dst=dst,
                              bcol=bcol: E.tensor_scalar(
                                  dst[:, f * L + c * Tc:f * L + (c + 1) * Tc],
                                  ps[pi][:, 0:Tc],
                                  bqt[:, bcol + f:bcol + f + 1], None,
                                  ALU.add),
                              writes=(dstname,), inc="v1",
                              waits=[tmm, rot.get((dstname, "user"))])
                    C.psum_free[pi] = te
            rot["wqb_user"] = P.tok("t1")
        tw = P.op("q", lambda E: E.dma_start(
            out=wqb.ap().rearrange("p (k c) -> p k c", c=DM),
            in_=ds["w_v"].rearrange("(k p) c -> p k c", p=128)),
            inc="qw0", inc_n=16, waits=[rot.get("wqb_user")])
        for tt in range(L // 128):
            for h in range(2):
                pi = (tt * 2 + h) % 4
                tmm = None
                for k in range(KM):
                    tmm = P.op("t", lambda E, k=k, tt=tt, h=h, pi=pi:
                               E.matmul(
                                   ps[pi][:, 0:512],
                                   xln2[:, k * L + tt * 128:
                                        k * L + (tt + 1) * 128],
                                   wqb[:, k * DM + h * 512:
                                       k * DM + (h + 1) * 512],
                                   start=(k == 0), stop=(k == KM - 1)),
                               waits=([tw, tln, C.psum_free.get(pi)]
                                      if k == 0 else []),
                               inc="t1" if k == KM - 1 else None)
                te = P.op("v", lambda E, tt=tt, h=h, pi=pi: E.tensor_tensor(
                    vT[:, tt * DM + h * 512:tt * DM + (h + 1) * 512],
                    ps[pi][:, 0:512], bvrep[:, h * 512:(h + 1) * 512],
                    ALU.add),
                    reads=("a_bvrep",), writes=("a_vT",), inc="v1",
                    waits=[tmm, rot.get("vT_user")])
                C.psum_free[pi] = te
        tqkv = P.tok("v1")
        stA.close()
        P.barrier()

        stB = ExitStack()
        aout = S("a_aout", [128, KM * L], BF16, stB)
        pf = S("a_pf", [128, 256], F32, stB)
        pnb = S("a_pnb", [128, 256], BF16, stB)
        pT = S("a_pT", [128, 2 * 256], BF16, stB)
        aobf = S("a_aobf", [128, KM * Tc], BF16, stB)
        woch = S("a_woch", [128, KM * Tc], F32, stB)
        wo_res = S("a_wores", [128, KM * 128], BF16, stB)
        tz = None
        for k in range(KM):
            tz = P.op("v", lambda E, k=k: E.memset(
                aout[:, k * L:(k + 1) * L], 0.0), writes=("a_aout",),
                inc="v1" if k == KM - 1 else None)
        for st0 in starts:
            for qt in range(2):
                pi = qt
                tmm = None
                for k in range(KM):
                    tmm = P.op("t", lambda E, k=k, st0=st0, qt=qt, pi=pi:
                               E.matmul(
                                   ps[pi][:, 0:256],
                                   qb[:, k * L + st0 + qt * 128:
                                      k * L + st0 + (qt + 1) * 128],
                                   kb[:, k * L + st0:k * L + st0 + 256],
                                   start=(k == 0), stop=(k == KM - 1)),
                               waits=([tqkv, C.psum_free.get(pi)]
                                      if k == 0 else []),
                               inc="t1" if k == KM - 1 else None)
                tr = P.op("v", lambda E, pi=pi: E.reduce_max(
                    srow[:, 0:1], ps[pi][:, 0:256], AX.X, negate=True),
                    writes=("a_srow",), waits=[tmm,
                                               rot.get("srow_user")])
                tsc = P.op("v", lambda E: E.tensor_scalar(
                    srow[:, 1:2], srow[:, 0:1], 0.125, None, ALU.mult),
                    reads=("a_srow",), writes=("a_srow",), inc="v1")
                tex = P.op("s", lambda E, pi=pi: E.activation(
                    pf[:], ps[pi][:, 0:256], AF.Exp, scale=0.125,
                    bias=srow[:, 1:2], accum_out=srow[:, 2:3]),
                    reads=("a_srow",), writes=("a_pf", "a_srow"), inc="s1",
                    waits=[tsc, rot.get("pf_user")])
                trc = P.op("v", lambda E: E.reciprocal(srow[:, 3:4],
                                                       srow[:, 2:3]),
                           reads=("a_srow",), writes=("a_srow",),
                           waits=[tex])
                tpn = P.op("v", lambda E: E.tensor_scalar(
                    pnb[:], pf[:], srow[:, 3:4], None, ALU.mult),
                    reads=("a_pf", "a_srow"), writes=("a_pnb",), inc="v1",
                    waits=[rot.get("pnb_user")])
                rot["srow_user"] = tpn
                rot["pf_user"] = tpn
                C.psum_free[pi] = tpn
                ttr = None
                for j in range(2):
                    ttr = P.op("t", lambda E, j=j, qt=qt: E.transpose(
                        ps[2].ap().bitcast(BF16)[0:128, j * 128:(j + 1) * 128],
                        pnb[:, j * 128:(j + 1) * 128], sb["idb"][:]),
                        waits=([tpn, C.psum_free.get(2)] if j == 0 else []),
                        inc="t1" if j == 1 else None)
                tpe = P.op("s", lambda E, qt=qt: E.activation(
                    pT.ap().bitcast(F32)
                    [:, (0 * 256 + qt * 128) // 2:(0 * 256 + qt * 128) // 2 + 64],
                    ps[2][:, 0:64], AF.Copy),
                    writes=("a_pT",), inc="s1",
                    waits=[ttr, rot.get("pT_user")])
                tpe2 = P.op("s", lambda E, qt=qt: E.activation(
                    pT.ap().bitcast(F32)
                    [:, (1 * 256 + qt * 128) // 2:(1 * 256 + qt * 128) // 2 + 64],
                    ps[2][:, 64:128], AF.Copy),
                    writes=("a_pT",), inc="s1")
                C.psum_free[2] = tpe2
            tpT = P.tok("s1")
            rot["pT_user"] = None
            for f in range(KM):
                pi = 4 + f % 2
                tmm = None
                for kt in range(2):
                    tmm = P.op("t", lambda E, kt=kt, f=f, st0=st0, pi=pi:
                               E.matmul(
                                   ps[pi][:, 0:256],
                                   vT[:, (st0 // 128 + kt) * DM + f * 128:
                                      (st0 // 128 + kt) * DM + (f + 1) * 128],
                                   pT[:, kt * 256:(kt + 1) * 256],
                                   start=(kt == 0), stop=(kt == 1)),
                               waits=([tpT, C.psum_free.get(pi)]
                                      if kt == 0 else []),
                               inc="t1" if kt == 1 else None)
                ta = P.op("v", lambda E, f=f, st0=st0, pi=pi: E.tensor_tensor(
                    aout[:, f * L + st0:f * L + st0 + 256],
                    ps[pi][:, 0:256],
                    aout[:, f * L + st0:f * L + st0 + 256], ALU.add),
                    reads=("a_aout",), writes=("a_aout",), inc="v1",
                    waits=[tmm, tz])
                C.psum_free[pi] = ta
            rot["pT_user"] = P.tok("t1")
        taw = P.tok("v1")
        if C.dbg_aout:
            P.op("s", lambda E, b=b: E.dma_start(
                out=C.out_d[b].rearrange("(a p) d -> p a d", p=128),
                in_=aout.ap().rearrange("p (a d) -> p a d", d=DM)),
                inc=f"go{b}", inc_n=16, waits=[taw])

        for c in range(NCH):
            cs = slice(c * Tc, (c + 1) * Tc)
            tcb = None
            for k in range(KM):
                tcb = P.op("v", lambda E, k=k, c=c, cs=cs: E.tensor_tensor(
                    aobf[:, k * Tc:(k + 1) * Tc],
                    aout[:, k * L + c * Tc:k * L + (c + 1) * Tc],
                    cinv[:, cs], ALU.mult),
                    reads=("a_aout", "a_cinv"), writes=("a_aobf",),
                    waits=[taw] + ([rot.get("aobf_user")] if k == 0 else []),
                    inc="v1" if k == KM - 1 else None)
            for f in range(KM):
                pi = f % 4
                tw = P.op("q", lambda E, f=f: E.dma_start(
                    out=wo_res.ap(), in_=ds["w_o"][f]),
                    inc="qo0", inc_n=16, waits=[rot.get("wores_user")])
                tmm = None
                for k in range(KM):
                    tmm = P.op("t", lambda E, k=k, f=f, pi=pi: E.matmul(
                        ps[pi][:, 0:Tc],
                        wo_res[:, k * 128:(k + 1) * 128],
                        aobf[:, k * Tc:(k + 1) * Tc],
                        start=(k == 0), stop=(k == KM - 1)),
                        waits=([tw, tcb, C.psum_free.get(pi)]
                               if k == 0 else []),
                        inc="t1" if k == KM - 1 else None)
                rot["wores_user"] = tmm
                te = P.op("v", lambda E, f=f, pi=pi: E.tensor_scalar(
                    woch[:, f * Tc:(f + 1) * Tc], ps[pi][:, 0:Tc],
                    bqt[:, 2 * KM + f:2 * KM + f + 1], None, ALU.add),
                    writes=("a_woch",), inc="v1",
                    waits=[tmm, rot.get("woch_user")])
                C.psum_free[pi] = te
            rot["aobf_user"] = P.tok("t1")
            two = P.tok("v1")
            tadd, tst = emit_postln_add(C, "a_woch", woch, b, cs, rot, two)
            rot["woch_user"] = tst
        stB.close()
        P.barrier()

    for b in range(BPC):
        _one_batch(b)

    stP.close()
    return P.tok("gr0")


def s4_ffn1(C):
    P, nc, ds, sb, ps, L = C.P, C.nc, C.ds, C.sb, C.ps, C.L
    Tc = 256
    W = Tc + 2
    st = ExitStack()
    def S(name, shape, dt):
        return st.enter_context(nc.sbuf_tensor(name, shape, dt))
    w1T = S("n_w1T", [128, KM * DH], BF16)
    xln = S("n_xln", [128, KM * W], F32)
    xqb = S("n_xqb", [128, KM * W], BF16)
    sqb4 = S("n_sqb", [128, W], BF16)
    h2b = S("n_h2b", [128, KH * Tc], BF16)
    hr = S("n_hr", [128, 2 * W], F32)
    hmax = S("n_hmax", [128, 2 * W], F32)
    scb = S("n_scb", [128, 2 * W], F32)
    tmpw = S("n_tmpw", [128, W], F32)
    tmpw2 = S("n_tmpw2", [128, W], F32)
    nab = S("n_abs", [128, W], F32)
    w2f = S("n_w2f", [128, 2 * KH * 128], BF16)
    dch = S("n_dch", [128, KM * Tc], F32)
    cw1 = S("n_cw1", [128, KH * 3], F32)
    cb1 = S("n_cb1", [128, KH], F32)
    xtl4 = S("n_xtl", [128, 2 * W], F32)
    C.xtl = xtl4
    C.qtmp2, C.qtmp2name = tmpw2, "n_tmpw2"
    sgt5 = S("n_sgt", [128, Tc], F32)
    C.sgt, C.sgtname = sgt5, "n_sgt"
    rot = {}
    NCH = L // Tc

    tw1 = P.op("q", _dma(w1T.ap().rearrange("p (k c) -> p k c", c=DH),
                         ds["w_w1T"].rearrange("(k p) c -> p k c", p=128)),
               inc="qw0", inc_n=16)
    tcw = P.op("q", _dma(cw1.ap().rearrange("p (k c) -> p k c", c=3),
                         C.din["L1_ffn_conv_w"]
                         .rearrange("(k p) o c -> p k (o c)", p=128)),
               inc="qc", inc_n=16)
    tcb = P.op("q", _dma_nc(cb1[:],
                         C.din["L1_ffn_conv_b"]
                         .rearrange("(k p) -> p k", p=128)),
               inc="qc", inc_n=16)

    for b in range(BPC):
        for c in range(NCH):
            c0 = c * Tc
            hs = max(c0 - 1, 0)
            he = min(c0 + Tc + 1, L)
            Wl = he - hs
            off = 1 if c == 0 else 0
            first = True
            for k in range(KM):
                par = k % 2
                tl = P.op("q", lambda E, b=b, k=k, hs=hs, he=he, par=par:
                          E.dma_start(
                              out=xtl4[:, par * W:par * W + (he - hs)],
                              in_=ds["resid"][b, k, :, hs:he]),
                          inc=f"qt{par}", inc_n=16,
                          waits=[rot.get(("xtl", par)),
                                 rot.get(("xtl2", par))])
                tc1 = P.op("v", lambda E, k=k, par=par, off=off, Wl=Wl:
                           E.tensor_copy(
                               xqb[:, k * W + off:k * W + off + Wl],
                               xtl4[:, par * W:par * W + Wl]),
                           writes=("n_xqb",), inc="v1",
                           waits=[tl, rot.get("xqb_user")])
                tc2 = P.op("s", lambda E, par=par, Wl=Wl: E.activation(
                    sqb4[:, 0:Wl], xtl4[:, par * W:par * W + Wl],
                    AF.Square),
                    writes=("n_sqb",), inc="s1",
                    waits=[tl, rot.get("sqb_user")])
                tc3 = P.op("v", lambda E, k=k, par=par, off=off, Wl=Wl:
                           E.tensor_copy(
                               xln[:, k * W + off:k * W + off + Wl],
                               xtl4[:, par * W:par * W + Wl]),
                           writes=("n_xln",), inc="v1",
                           waits=[tl, rot.get("xln_user")])
                rot[("xtl", par)] = tc3
                rot[("xtl2", par)] = tc2
                t6 = P.op("t", lambda E, k=k, first=first, off=off, Wl=Wl:
                          E.matmul(
                              ps[6][0:1, 0:Wl], sb["onesb"][:, 0:1],
                              xqb[:, k * W + off:k * W + off + Wl],
                              start=first, stop=(k == KM - 1)),
                          inc="t1", waits=[tc1] + ([C.psum_free.get(6)]
                                                   if first else []))
                t7 = P.op("t", lambda E, first=first, k=k, Wl=Wl: E.matmul(
                    ps[7][0:1, 0:Wl], sb["onesb"][:, 0:1], sqb4[:, 0:Wl],
                    start=first, stop=(k == KM - 1)),
                    inc="t1", waits=[tc2] + ([C.psum_free.get(7)]
                                             if first else []))
                rot["sqb_user"] = t7
                first = False
            tmm = P.tok("t1")
            oldTc = C.Tc
            C.Tc = Wl
            t4, t5 = ln_rows(C, ps[6], ps[7], 1.0 / DM, 0, tmm)
            C.Tc = oldTc
            C.psum_free[6] = P.tok("v1")
            C.psum_free[7] = C.psum_free[6]
            tz = P.op("v", lambda E: E.memset(hmax[:, 0:W], 0.0),
                      writes=("n_hmax",), inc="v1",
                      waits=[rot.get("hmax_user")])
            tn = None
            for k in range(KM):
                P.op("v", lambda E, k=k, off=off, Wl=Wl: E.tensor_tensor(
                    xln[:, k * W + off:k * W + off + Wl],
                    xln[:, k * W + off:k * W + off + Wl],
                    ps[5][:, 0:Wl], ALU.mult),
                    reads=("n_xln",), writes=("n_xln",), waits=[t4, t5])
                tnk = P.op("v", lambda E, k=k, off=off, Wl=Wl: E.tensor_tensor(
                    xln[:, k * W + off:k * W + off + Wl],
                    xln[:, k * W + off:k * W + off + Wl],
                    ps[4][:, 0:Wl], ALU.subtract),
                    reads=("n_xln",), writes=("n_xln",), inc="v1")
                tab = P.op("s", lambda E, k=k, off=off, Wl=Wl: E.activation(
                    nab[:, 0:Wl], xln[:, k * W + off:k * W + off + Wl],
                    AF.Abs),
                    reads=("n_xln",), writes=("n_abs",), inc="s1",
                    waits=[tnk, rot.get("nab_user")])
                tn = P.op("v", lambda E, off=off, Wl=Wl: E.tensor_tensor(
                    hmax[:, off:off + Wl], hmax[:, off:off + Wl],
                    nab[:, 0:Wl], ALU.max),
                    reads=("n_abs", "n_hmax"), writes=("n_hmax",),
                    inc="v1", waits=[tz, tab])
                rot["nab_user"] = tn
            C.psum_free[4] = tn
            C.psum_free[5] = tn
            rot["xln_user"] = tn
            tpr = P.op("g", lambda E, off=off, Wl=Wl: E.partition_all_reduce(
                hmax[:, W + off:W + off + Wl], hmax[:, off:off + Wl], 128,
                ReduceOp.max),
                reads=("n_hmax",), writes=("n_hmax",), inc="g1", waits=[tn])
            P.op("v", lambda E, off=off, Wl=Wl: E.tensor_scalar(
                hmax[:, W + off:W + off + Wl], hmax[:, W + off:W + off + Wl],
                1e-5, None, ALU.max),
                reads=("n_hmax",), writes=("n_hmax",), waits=[tpr])
            P.op("v", lambda E, off=off, Wl=Wl: E.reciprocal(
                scb[:, off:off + Wl], hmax[:, W + off:W + off + Wl]),
                reads=("n_hmax",), writes=("n_scb",),
                waits=[rot.get("scb_user")])
            P.op("v", lambda E, off=off, Wl=Wl: E.tensor_scalar(
                scb[:, off:off + Wl], scb[:, off:off + Wl], 127.0, None,
                ALU.mult),
                reads=("n_scb",), writes=("n_scb",))
            tsc = P.op("v", lambda E, off=off, Wl=Wl: E.tensor_scalar(
                scb[:, W + off:W + off + Wl],
                hmax[:, W + off:W + off + Wl], 1.0 / 127.0, None, ALU.mult),
                reads=("n_hmax",), writes=("n_scb",), inc="v1")
            rot["hmax_user"] = tsc
            tq = None
            for k in range(KM):
                if off == 1:
                    P.op("v", lambda E, k=k: E.memset(
                        xqb[:, k * W:k * W + 1], 0.0), writes=("n_xqb",))
                if he == L:
                    P.op("v", lambda E, k=k: E.memset(
                        xqb[:, k * W + off + Wl:(k + 1) * W], 0.0),
                        writes=("n_xqb",))
                tq = emit_qdq_bc(
                    C, xln[:, k * W + off:k * W + off + Wl],
                    xqb[:, k * W + off:k * W + off + Wl],
                    scb[:, off:off + Wl], scb[:, W + off:W + off + Wl],
                    -128.0, 127.0, tmpw[:, 0:Wl], "n_tmpw",
                    reads=("n_xln",), writes=("n_xqb",),
                    waits=[tsc], bcreads=("n_scb",))
            tzh = P.op("v", lambda E: E.memset(hmax[:, 0:Tc], 0.0),
                       writes=("n_hmax",), inc="v1", waits=[tsc])
            Tcc = min(Tc, L - c0)
            for f in range(KH):
                par, pi = f % 2, f % 2
                tmm = None
                for k in range(KM):
                    tmm = P.op("t", lambda E, k=k, f=f, pi=pi, W=W: E.matmul(
                        ps[pi][:, 0:W], w1T[:, k * DH + f * 128:
                                            k * DH + (f + 1) * 128],
                        xqb[:, k * W:(k + 1) * W],
                        start=(k == 0), stop=(k == KM - 1)),
                        waits=([tq, tw1, C.psum_free.get(pi)]
                               if k == 0 else []),
                        inc="t1" if k == KM - 1 else None)
                te = P.op("s", lambda E, pi=pi, par=par, W=W: E.activation(
                    hr[:, par * W:(par + 1) * W], ps[pi][:, 0:W], AF.Copy),
                    writes=("n_hr",), inc="s1",
                    waits=[tmm, rot.get(("hr", par))])
                C.psum_free[pi] = te
                tcv2 = None
                P.op("v", lambda E, f=f, par=par, Tcc=Tcc, W=W:
                     E.tensor_scalar(
                         sgt5[:, 0:Tcc], hr[:, par * W:par * W + Tcc],
                         cw1[:, f * 3:f * 3 + 1], None, ALU.mult),
                     reads=("n_hr", "n_cw1"), writes=("n_sgt",),
                     waits=[te, tcb, rot.get("sgt_user")])
                for kk in range(1, 3):
                    tcv2 = P.op("v", lambda E, f=f, par=par, kk=kk, Tcc=Tcc,
                                W=W: E.scalar_tensor_tensor(
                                    sgt5[:, 0:Tcc],
                                    hr[:, par * W + kk:par * W + kk + Tcc],
                                    cw1[:, f * 3 + kk:f * 3 + kk + 1],
                                    sgt5[:, 0:Tcc], ALU.mult, ALU.add),
                                reads=("n_hr", "n_sgt", "n_cw1"),
                                writes=("n_sgt",),
                                inc="v1" if kk == 2 else None)
                rot[("hr", par)] = tcv2
                ts = P.op("s", lambda E, f=f, Tcc=Tcc: E.activation(
                    h2b[:, f * Tc:f * Tc + Tcc], sgt5[:, 0:Tcc], AF.Silu,
                    bias=cb1[:, f:f + 1]),
                    reads=("n_sgt",), writes=("n_h2b",), inc="s1",
                    waits=[tcv2, tcb, rot.get("h2b_user")])
                rot["sgt_user"] = ts
                tab2 = P.op("s", lambda E, f=f, Tcc=Tcc: E.activation(
                    nab[:, 0:Tcc], h2b[:, f * Tc:f * Tc + Tcc], AF.Abs),
                    reads=("n_h2b",), writes=("n_abs",), inc="s1",
                    waits=[ts, rot.get("nab_user")])
                tmx = P.op("v", lambda E, Tcc=Tcc: E.tensor_tensor(
                    hmax[:, 0:Tcc], hmax[:, 0:Tcc], nab[:, 0:Tcc], ALU.max),
                    reads=("n_abs", "n_hmax"), writes=("n_hmax",),
                    inc="v1", waits=[tzh, tab2])
                rot["nab_user"] = tmx
            th = P.tok("v1")
            tpr = P.op("g", lambda E, Tcc=Tcc: E.partition_all_reduce(
                hmax[:, W:W + Tcc], hmax[:, 0:Tcc], 128, ReduceOp.max),
                reads=("n_hmax",), writes=("n_hmax",), inc="g1", waits=[th])
            P.op("v", lambda E, Tcc=Tcc: E.tensor_scalar(
                hmax[:, W:W + Tcc], hmax[:, W:W + Tcc], 1e-5, None,
                ALU.max),
                reads=("n_hmax",), writes=("n_hmax",), waits=[tpr])
            P.op("v", lambda E, Tcc=Tcc: E.reciprocal(
                scb[:, 0:Tcc], hmax[:, W:W + Tcc]),
                reads=("n_hmax",), writes=("n_scb",), waits=[tsc])
            P.op("v", lambda E, Tcc=Tcc: E.tensor_scalar(
                scb[:, 0:Tcc], scb[:, 0:Tcc], 127.0, None, ALU.mult),
                reads=("n_scb",), writes=("n_scb",))
            tsc2 = P.op("v", lambda E, Tcc=Tcc: E.tensor_scalar(
                scb[:, W:W + Tcc], hmax[:, W:W + Tcc], 1.0 / 127.0, None,
                ALU.mult),
                reads=("n_hmax",), writes=("n_scb",), inc="v1")
            rot["hmax_user"] = tsc2
            rot["scb_user"] = tsc2
            tq2 = None
            for f in range(KH):
                tq2 = emit_qdq_bc(
                    C, h2b[:, f * Tc:f * Tc + Tcc],
                    h2b[:, f * Tc:f * Tc + Tcc],
                    scb[:, 0:Tcc], scb[:, W:W + Tcc], -128.0, 127.0,
                    tmpw[:, 0:Tcc], "n_tmpw",
                    reads=("n_h2b",), writes=("n_h2b",),
                    waits=[tsc2], bcreads=("n_scb",))
            rot["h2b_user"] = tq2
            cs = slice(c0, c0 + Tcc)
            for f in range(KM):
                par, pi = f % 2, f % 2
                tw2 = P.op("q", lambda E, f=f, par=par: E.dma_start(
                    out=w2f[:, par * KH * 128:(par + 1) * KH * 128],
                    in_=ds["w_w2T"][f]),
                    inc=f"qp{par}", inc_n=16, waits=[rot.get(("w2f", par))])
                tmm = None
                for k in range(KH):
                    tmm = P.op("t", lambda E, k=k, par=par, pi=pi, Tcc=Tcc:
                               E.matmul(
                                   ps[pi][:, 0:Tcc],
                                   w2f[:, par * KH * 128 + k * 128:
                                       par * KH * 128 + (k + 1) * 128],
                                   h2b[:, k * Tc:k * Tc + Tcc],
                                   start=(k == 0), stop=(k == KH - 1)),
                               waits=([tw2, tq2, C.psum_free.get(pi)]
                                      if k == 0 else []),
                               inc="t1" if k == KH - 1 else None)
                rot[("w2f", par)] = tmm
                te = P.op("s", lambda E, f=f, pi=pi, Tcc=Tcc: E.activation(
                    dch[:, f * Tc:f * Tc + Tcc], ps[pi][:, 0:Tcc], AF.Copy),
                    writes=("n_dch",), inc="s1",
                    waits=[tmm, rot.get("dch_user")])
                C.psum_free[pi] = te
            tdn = P.tok("s1")
            oldTc = C.Tc
            C.Tc = Tcc
            dchv = dch.ap().rearrange("p (k t) -> p k t", t=Tc)[:, :, 0:Tcc]
            tadd, tst = emit_postln_add_v(C, "n_dch", dch, Tc, b, cs, rot,
                                          tdn, dst="xln1")
            C.Tc = oldTc
            rot["dch_user"] = tst
            rot["xqb_user"] = P.tok("t1")
    st.close()
    return P.tok("gr0")


def emit_postln_add_v(C, srcname, srcch, stride, b, cs, rot, tready,
                      dst="resid"):
    P, ps, sb = C.P, C.ps, C.sb
    Tc = C.Tc
    xtl, sgt = C.xtl, C.sgt
    first = True
    for k in range(KM):
        t6 = P.op("t", lambda E, k=k, first=first: E.matmul(
            ps[6][0:1, 0:Tc], sb["onesf"][:, 0:1],
            srcch[:, k * stride:k * stride + Tc], start=first,
            stop=(k == KM - 1)),
            inc="t1", waits=[tready] + ([C.psum_free.get(6)]
                                        if first else []))
        tsq = P.op("s", lambda E, k=k: E.activation(
            sgt[:, 0:Tc], srcch[:, k * stride:k * stride + Tc], AF.Square),
            writes=(C.sgtname,), inc="s1",
            waits=[tready, rot.get("sgt_user")])
        t7 = P.op("t", lambda E, first=first, k=k: E.matmul(
            ps[7][0:1, 0:Tc], sb["onesf"][:, 0:1], sgt[:, 0:Tc],
            start=first, stop=(k == KM - 1)), inc="t1",
            waits=[tsq] + ([C.psum_free.get(7)] if first else []))
        rot["sgt_user"] = t7
        first = False
    tmm = P.tok("t1")
    t4, t5 = ln_rows(C, ps[6], ps[7], 1.0 / DM, 0, tmm)
    C.psum_free[6] = P.tok("v1")
    C.psum_free[7] = C.psum_free[6]
    for k in range(KM):
        par = k % 2
        HW = xtl.shape[1] // 2
        tl = P.op("q", lambda E, k=k, par=par, b=b, cs=cs, HW=HW:
                  E.dma_start(out=xtl[:, par * HW:par * HW + Tc],
                              in_=C.ds["resid"][b, k, :, cs]),
                  inc=f"qz{par}", inc_n=16,
                  waits=[rot.get(("xtl", par)), rot.get(("xtl2", par))])
        P.op("v", lambda E, k=k: E.tensor_tensor(
            srcch[:, k * stride:k * stride + Tc],
            srcch[:, k * stride:k * stride + Tc], ps[5][:, 0:Tc], ALU.mult),
            reads=(srcname,), writes=(srcname,), waits=[t4, t5])
        P.op("v", lambda E, k=k: E.tensor_tensor(
            srcch[:, k * stride:k * stride + Tc],
            srcch[:, k * stride:k * stride + Tc], ps[4][:, 0:Tc],
            ALU.subtract),
            reads=(srcname,), writes=(srcname,))
        ta = P.op("v", lambda E, k=k, par=par, HW=HW: E.tensor_tensor(
            srcch[:, k * stride:k * stride + Tc],
            srcch[:, k * stride:k * stride + Tc],
            xtl[:, par * HW:par * HW + Tc], ALU.add),
            reads=(srcname,), writes=(srcname,), inc="v1", waits=[tl])
        rot[("xtl", par)] = ta
        rot[("xtl2", par)] = ta
    tadd = P.tok("v1")
    C.psum_free[4] = tadd
    C.psum_free[5] = tadd
    tst = P.op("s", lambda E, b=b, cs=cs, dst=dst: E.dma_start(
        out=C.ds[dst][b, :, :, cs].rearrange("k p t -> p k t"),
        in_=srcch.ap().rearrange("p (k t) -> p k t", t=stride)
        [:, :, 0:Tc]),
        inc="gr0", inc_n=16, waits=[tadd])
    return tadd, tst


def build_nc(L=2048, Tc=512, dbg=None, ncore=NCORE):
    global _LASTP
    nc = bass.Bass("TRN2", target_bir_lowering=False)
    P = Prog(nc)
    _LASTP = P
    es = P.es

    din = {}
    for name, shape in [
        ("L0_attn_x", [BPC, L, DM]), ("L0_attn_adj", [DI, DI]),
        ("L0_attn_in_proj_w", [DM, 2 * DI]), ("L0_attn_conv_w", [DI, 1, 4]),
        ("L0_attn_conv_b", [DI]), ("L0_attn_x_proj_w", [DI, DTR + 2 * NST]),
        ("L0_attn_dt_proj_w", [DTR, DI]), ("L0_attn_dt_proj_b", [DI]),
        ("L0_attn_A_log", [DI, NST]), ("L0_attn_D", [DI]),
        ("L0_attn_out_proj_w", [DI, DM]),
        ("L0_ffn_w_gate", [HF, DM]), ("L0_ffn_w_up", [HF, DM]),
        ("L0_ffn_w_down", [DM, HF]),
        ("L1_attn_Wq", [DM, DM]), ("L1_attn_bq", [DM]),
        ("L1_attn_Wk", [DM, DM]), ("L1_attn_bk", [DM]),
        ("L1_attn_Wv", [DM, DM]), ("L1_attn_bv", [DM]),
        ("L1_attn_Wo", [DM, DM]), ("L1_attn_bo", [DM]),
        ("L1_ffn_w1", [DH, DM]), ("L1_ffn_conv_w", [DH, 1, 3]),
        ("L1_ffn_conv_b", [DH]), ("L1_ffn_w2", [DM, DH]),
    ]:
        din[name] = nc.dram_tensor(name, shape, F32, kind="ExternalInput").ap()
    out_d = nc.dram_tensor("out", [BPC, L, DM], F32, kind="ExternalOutput").ap()

    ds = {}
    for name, shape, dt in [
        ("w_inproj", [2 * KI, 128, DM], BF16),
        ("w_adjn", [KI, 128, DI], BF16),
        ("w_outproj", [KM, 128, DI], BF16), ("w_xproj", [DI, 96], BF16),
        ("w_dtproj", [DTR, DI], BF16),
        ("w_upT", [DM, HF], BF16), ("w_gateT", [DM, HF], BF16),
        ("w_downT", [KM, 128, HF], BF16),
        ("w_q", [DM, DM], BF16), ("w_k", [DM, DM], BF16),
        ("w_v", [DM, DM], BF16), ("w_o", [KM, 128, DM], BF16),
        ("w_w1T", [DM, DH], BF16), ("w_w2T", [KM, 128, DH], BF16),
        ("resid", [BPC, KM, 128, L], F32),
        ("xln1", [BPC, KM, 128, L], F32),
        ("up_raw", [BPC, KF, 128, L], F32),
        ("gate_raw", [BPC, KF, 128, L], F32),
        ("g2pre", [BPC, KF, 128, L], F32),
        ("o_pre", [BPC, KF, 128, L], F32),
        ("cc_in", [1, 8], F32), ("cc_out", [1, 8], F32),
    ]:
        ds[name] = nc.dram_tensor(name, shape, dt).ap()

    sb = {}
    def sbuf(name, shape, dt):
        sb[name] = es.enter_context(nc.sbuf_tensor(name, shape, dt))
        return sb[name]
    sbuf("idf", [128, 128], F32)
    sbuf("idb", [128, 128], BF16)
    sbuf("onesf", [128, 128], F32)
    sbuf("onesb", [128, 128], BF16)
    sbuf("selb", [32, 32 * 128], BF16)
    init_es = ExitStack()
    sb["self_sel"] = init_es.enter_context(
        nc.sbuf_tensor("self_sel", [32, 32 * 128], F32, side="right"))
    sbuf("negA", [128, KI * NST], F32)
    sbuf("Dsb", [128, KI], F32)
    sbuf("cwsb", [128, KI * 4], F32)
    sbuf("cbsb", [128, KI], F32)
    sbuf("dtbn", [128, KI], F32)
    sbuf("hcar", [128, KI * NST], F32)
    sbuf("tails", [128, KI * 3], F32)
    sbuf("xprj", [128, KI * 96], BF16)
    sbuf("dtprj", [64, DI], BF16)
    sbuf("rows", [1, 4 * 520], F32)
    ps = [es.enter_context(nc.psum_tensor(f"ps{i}", [128, 512], F32))
          for i in range(8)]

    C = SimpleNamespace(P=P, nc=nc, din=din, ds=ds, sb=sb, ps=ps, out_d=out_d,
                        L=L, Tc=Tc, rows=sb["rows"], psum_free={})

    P.op("g", lambda E: E.iota(sb["idf"][:], pattern=[[1, 128]], base=0,
                               channel_multiplier=-1,
                               allow_small_or_imprecise_dtypes=True),
         writes=("idf",))
    P.op("g", lambda E: E.tensor_scalar(sb["idf"][:], sb["idf"][:], 0.0,
                                        None, ALU.is_equal),
         reads=("idf",), writes=("idf",))
    P.op("g", lambda E: E.iota(sb["self_sel"][:], pattern=[[1, 32], [0, 128]],
                               base=0, channel_multiplier=-1,
                               allow_small_or_imprecise_dtypes=True),
         writes=("self_sel",))
    P.op("g", lambda E: E.tensor_scalar(sb["selb"][:], sb["self_sel"][:],
                                        0.0, None, ALU.is_equal),
         reads=("self_sel",), writes=("selb",))
    P.op("g", lambda E: E.tensor_copy(sb["idb"][:], sb["idf"][:]),
         reads=("idf",), writes=("idb",))
    P.op("g", lambda E: E.memset(sb["onesf"][:], 1.0), writes=("onesf",))
    tinit = P.op("g", lambda E: E.memset(sb["onesb"][:], 1.0),
                 writes=("onesb",), inc="g0")
    P.raw("g", lambda E: E.load_library(_proxy_lib))
    for e in ("t", "v", "s", "q"):
        P.waitt(e, tinit)

    init_es.close()
    for k in range(KI):
        P.op("q", _dma(sb["negA"][:, k * NST:(k + 1) * NST],
                       din["L0_attn_A_log"][k * 128:(k + 1) * 128, :]),
             inc="qc", inc_n=16)
        P.op("q", _dma(sb["cwsb"][:, k * 4:(k + 1) * 4],
                       din["L0_attn_conv_w"][k * 128:(k + 1) * 128, 0, :]),
             inc="qc", inc_n=16)
        P.op("q", _dma(sb["Dsb"][:, k:k + 1],
                       din["L0_attn_D"][k * 128:(k + 1) * 128]
                       .rearrange("(p o) -> p o", o=1)), inc="qc", inc_n=16)
        P.op("q", _dma(sb["cbsb"][:, k:k + 1],
                       din["L0_attn_conv_b"][k * 128:(k + 1) * 128]
                       .rearrange("(p o) -> p o", o=1)), inc="qc", inc_n=16)
        P.op("q", _dma(sb["dtbn"][:, k:k + 1],
                       din["L0_attn_dt_proj_b"][k * 128:(k + 1) * 128]
                       .rearrange("(p o) -> p o", o=1)), inc="qc", inc_n=16)
    qc = P.tok("qc")
    tna = P.op("s", lambda E: E.activation(sb["negA"][:], sb["negA"][:],
                                           AF.Exp),
               reads=("negA",), writes=("negA",), inc="s1", waits=[qc])
    P.op("v", lambda E: E.tensor_scalar(sb["negA"][:], sb["negA"][:], -1.0,
                                        None, ALU.mult),
         reads=("negA",), writes=("negA",), waits=[tna])
    P.op("v", lambda E: E.tensor_scalar(sb["dtbn"][:], sb["dtbn"][:], -1.0,
                                        None, ALU.mult),
         reads=("dtbn",), writes=("dtbn",), waits=[qc], inc="v1")

    st = ExitStack()
    ti = st.enter_context(nc.sbuf_tensor("i_ti", [128, 2 * DM], F32))
    to = st.enter_context(nc.sbuf_tensor("i_to", [128, 2 * KM * 128], F32))
    rot = {}
    for b in range(BPC):
        for tt in range(L // 128):
            par = tt % 2
            tl = P.op("q", lambda E, b=b, tt=tt, par=par: E.dma_start(
                out=ti[:, par * DM:(par + 1) * DM],
                in_=din["L0_attn_x"][b, tt * 128:(tt + 1) * 128, :]),
                inc=f"qi{par}", inc_n=16, waits=[rot.get(("ti", par))])
            for half in range(2):
                pi = half
                tmm = None
                for j in range(4):
                    k = half * 4 + j
                    tmm = P.op("t", lambda E, k=k, j=j, par=par, pi=pi:
                               E.transpose(
                                   ps[pi][:, j * 128:(j + 1) * 128],
                                   ti[:, par * DM + k * 128:
                                      par * DM + (k + 1) * 128],
                                   sb["idf"][:]),
                               waits=([tl, C.psum_free.get(pi)]
                                      if j == 0 else []),
                               inc="t1" if j == 3 else None)
                te = P.op("s", lambda E, par=par, half=half, pi=pi:
                          E.activation(
                              to[:, par * KM * 128 + half * 512:
                                 par * KM * 128 + (half + 1) * 512],
                              ps[pi][:, 0:512], AF.Copy),
                          writes=("i_to",), inc="s1",
                          waits=[tmm, rot.get(("to", par))])
                C.psum_free[pi] = te
            tev = P.tok("s1")
            rot[("ti", par)] = tev
            tst = P.op("s", lambda E, b=b, tt=tt, par=par: E.dma_start(
                out=ds["resid"][b, :, :, tt * 128:(tt + 1) * 128]
                .rearrange("k p t -> p k t"),
                in_=to[:, par * KM * 128:(par + 1) * KM * 128]
                .rearrange("p (k t) -> p k t", t=128)),
                inc=f"gi{par}", inc_n=16, waits=[tev])
            rot[("to", par)] = tst
    st.close()
    P.barrier()
    x_staged = None

    need = {"x1": ("mamba",), "x2": ("mamba", "ffn0"),
            "x3": ("mamba", "ffn0", "attn1"),
            "aout": ("mamba", "ffn0", "attn1")}.get(
                dbg, ("mamba", "ffn0", "attn1", "ffn1"))
    C.dbg_aout = False
    C.w_done = s0_weights(C, need)
    P.barrier()
    s1_done = s1_mamba(C)
    P.raw("g", lambda E: E.load_library(_attn_lib))
    def _finish():
        block = es.enter_context(nc.Block())
        P.build(block)
        es.close()
        mybir.codegen_inst_isa_subclasses(nc)
        return nc
    if dbg == "x1":
        P.barrier()
        s_out(C, s1_done)
        return _finish()
    P.barrier()
    s2_done = s2_ffn0(C, ncore)
    if dbg == "x2":
        P.barrier()
        s_out(C, s2_done)
        return _finish()
    P.barrier()
    C.dbg_aout = (dbg == "aout")
    s3_done = s3_attn(C)
    if dbg == "aout":
        P.dma_wait_all("g", prefixes=("go",))
        return _finish()
    if dbg == "x3":
        P.barrier()
        s_out(C, s3_done)
        return _finish()
    P.barrier()
    s4_done = s4_ffn1(C)
    P.barrier()
    s_out(C, s4_done, src="xln1")
    return _finish()


def kernel(**inputs):
    full = {k: np.ascontiguousarray(np.asarray(v, dtype=np.float32))
            for k, v in inputs.items()}
    nc = build_nc(L=2048, Tc=512, dbg=None)
    in_maps = []
    for c in range(NCORE):
        m = dict(full)
        m["L0_attn_x"] = full["L0_attn_x"][c * BPC:(c + 1) * BPC]
        in_maps.append(m)
    res = run_bass_kernel_spmd(nc, in_maps, core_ids=list(range(NCORE)))
    return np.concatenate([r["out"] for r in res.results], axis=0)


if __name__ == "__main__":
    print("kernel module OK")


# version 14
# speedup vs baseline: 1.0470x; 1.0470x over previous
import numpy as np
from contextlib import ExitStack
from types import SimpleNamespace

import concourse.bass as bass
import concourse.mybir as mybir
from concourse.bass_utils import run_bass_kernel_spmd
from concourse.library_config import attn as _attn_lib
from concourse.bass_isa import ReduceOp

F32 = mybir.dt.float32
BF16 = mybir.dt.bfloat16
AF = mybir.ActivationFunctionType
ALU = mybir.AluOpType
AX = mybir.AxisListType

DM, DI, NST, DTR, HF, DH = 1024, 2048, 16, 64, 3072, 4096
KM, KI, KF, KH = DM // 128, DI // 128, HF // 128, DH // 128
NCORE, BPC = 8, 2
EPS = 1e-5


class Prog:
    ENGS = ("t", "v", "s", "g", "q")

    def __init__(self, nc):
        self.nc = nc
        self.streams = {e: [] for e in self.ENGS}
        self.cnt = {}
        self.sem_h = {}
        self.dirty = {e: set() for e in self.ENGS}
        self.rset = {e: set() for e in self.ENGS}
        self.waited = {e: {} for e in self.ENGS}
        self.dma_sems = set()
        self.es = ExitStack()
        self.labels = {e: [] for e in self.ENGS}
        self.meta = {e: [] for e in self.ENGS}

    def sem(self, name):
        if name not in self.sem_h:
            self.sem_h[name] = self.es.enter_context(self.nc.semaphore(name))
            self.cnt[name] = 0
        return self.sem_h[name]

    def wait(self, eng, name, val):
        if val <= 0 or self.waited[eng].get(name, 0) >= val:
            return
        self.waited[eng][name] = val
        h = self.sem(name)
        self.streams[eng].append(lambda E, h=h, v=val: E.wait_ge(h, v))
        self.meta[eng].append(("w", name, val))

    def waitt(self, eng, tok):
        if tok is not None:
            self.wait(eng, tok[0], tok[1])

    def drain(self, eng):
        if self.dirty[eng] or self.rset[eng]:
            self.streams[eng].append(lambda E: E.drain())
            self.labels[eng].append(("drain", 0))
            self.dirty[eng].clear()
            self.rset[eng].clear()

    def op(self, eng, fn, reads=(), writes=(), inc=None, inc_n=1, waits=(),
           clears=()):
        merged = {}
        for tk in waits:
            if tk is not None and tk[1] > merged.get(tk[0], 0):
                merged[tk[0]] = tk[1]
        for nm, v in merged.items():
            self.waitt(eng, (nm, v))
        for nm in clears:
            self.dirty[eng].discard(nm)
            self.rset[eng].discard(nm)
        if (any(r in self.dirty[eng] for r in reads)
                or any(w in self.dirty[eng] or w in self.rset[eng]
                       for w in writes)):
            self.drain(eng)
        if inc is not None:
            self.sem(inc)
            if inc_n == 16:
                self.dma_sems.add(inc)
            self.cnt[inc] += inc_n
            h = self.sem_h[inc]

            def run(E, fn=fn, h=h, n=inc_n):
                fn(E).then_inc(h, n)
        else:
            def run(E, fn=fn):
                fn(E)
        self.streams[eng].append(run)
        import sys as _sys
        f = _sys._getframe(1)
        if f.f_code.co_name == "<lambda>":
            f = f.f_back
        self.labels[eng].append(("op", f.f_lineno))
        self.meta[eng].append(("i", inc, inc_n, f.f_lineno)
                              if inc is not None else ("i", None, 0,
                                                       f.f_lineno))
        if self.labels[eng][-2:-1] == [("drain", 0)]:
            self.labels[eng][-2] = ("drain", f.f_lineno)
        for w in writes:
            self.dirty[eng].add(w)
        for r in reads:
            self.rset[eng].add(r)
        return (inc, self.cnt[inc]) if inc is not None else None

    def raw(self, eng, fn):
        self.streams[eng].append(fn)
        self.labels[eng].append(("raw", 0))

    def tok(self, name):
        self.sem(name)
        return (name, self.cnt[name])

    def dma_wait_all(self, eng, prefixes=None):
        for name in sorted(self.dma_sems):
            if prefixes is None or any(name.startswith(p) for p in prefixes):
                self.wait(eng, name, self.cnt[name])

    def barrier(self, engines=None):
        for eng in (engines or self.ENGS):
            for name in sorted(self.cnt):
                self.wait(eng, name, self.cnt[name])

    def build(self, block):
        for attr, e in (("tensor", "t"), ("vector", "v"), ("scalar", "s"),
                        ("gpsimd", "g"), ("sync", "q")):
            stream = self.streams[e]

            def make(stream):
                def body(E):
                    for f in stream:
                        f(E)
                return body
            getattr(block, attr)(make(stream))


def _dma(out, in_):
    return lambda E: E.dma_start(out=out, in_=in_)


def _dma_nc(out, in_):
    return lambda E: E.dma_start(out=out, in_=in_,
                                 allow_slow_non_contiguous=True)


def _win_starts(L, ws=256):
    stride, starts = ws // 2, []
    for i in range(0, L, stride):
        end = min(i + ws, L)
        starts.append(max(0, end - ws))
        if end == L:
            break
    return starts


def s0_weights(C, need=("mamba", "ffn0", "attn1", "ffn1")):
    P, nc, din, ds, sb, ps = C.P, C.nc, C.din, C.ds, C.sb, C.ps
    st = ExitStack()
    wf = st.enter_context(nc.sbuf_tensor("s0_wf", [128, 3 * 4096], F32))
    wb = st.enter_context(nc.sbuf_tensor("s0_wb", [128, 3 * 4096], BF16))
    sca = st.enter_context(nc.sbuf_tensor("s0_sca", [128, 2 * 192], F32))
    tb = st.enter_context(nc.sbuf_tensor("s0_tb", [128, 3 * 512], BF16))
    row = st.enter_context(nc.sbuf_tensor("s0_row", [128, 8], F32))
    wmod = st.enter_context(nc.sbuf_tensor("s0_wm", [128, 4096], F32))
    ldt, vdt, gqt = {}, {}, {}

    def loadw(par, dst_cols, src_ap, w):
        P.op("q", _dma(wf[:, dst_cols], src_ap), inc=f"qw{par}", inc_n=16,
             waits=[vdt.get((par, "wf"))])
        return P.tok(f"qw{par}")

    def plain_cast(src, dst, ktiles, cols, scale=None, rows=128,
                   tiled=0):
        for k in range(ktiles):
            par = k % 3
            tl = P.op("q", _dma(wf[0:rows, par * 4096:par * 4096 + cols],
                                src[k * rows:(k + 1) * rows, :]),
                      inc=f"qw{par}", inc_n=16, waits=[vdt.get((par, "wf"))])
            if scale is None:
                t = P.op("v", lambda E, par=par, cols=cols, rows=rows:
                         E.tensor_copy(
                             wb[0:rows, par * 4096:par * 4096 + cols],
                             wf[0:rows, par * 4096:par * 4096 + cols]),
                         reads=("s0_wf",), writes=("s0_wb",), inc="v0",
                         waits=[tl, gqt.get((par, "wb"))])
            else:
                t = P.op("v", lambda E, par=par, cols=cols, s=scale,
                         rows=rows: E.tensor_scalar(
                             wb[0:rows, par * 4096:par * 4096 + cols],
                             wf[0:rows, par * 4096:par * 4096 + cols],
                             s, None, ALU.mult),
                         reads=("s0_wf",), writes=("s0_wb",), inc="v0",
                         waits=[tl, gqt.get((par, "wb"))])
            vdt[(par, "wf")] = t
            if tiled:
                P.op("s", lambda E, k=k, par=par, cols=cols, dst=dst,
                     tiled=tiled: E.dma_start(
                         out=dst[:, :, k * 128:(k + 1) * 128]
                         .rearrange("f p c -> p f c"),
                         in_=wb[:, par * 4096:par * 4096 + cols]
                         .rearrange("p (f c) -> p f c", c=128)),
                     inc=f"gw{par}", inc_n=16, waits=[t])
            else:
                P.op("s", lambda E, k=k, par=par, cols=cols, rows=rows,
                     dst=dst: E.dma_start(
                         out=dst[k * rows:(k + 1) * rows, :],
                         in_=wb[0:rows, par * 4096:par * 4096 + cols]),
                     inc=f"gw{par}", inc_n=16, waits=[t])
            gqt[(par, "wb")] = P.tok(f"gw{par}")

    if "mamba" in need:
        plain_cast(din["L0_attn_in_proj_w"], ds["w_inproj"], KM, 2 * DI,
                   tiled=2 * KI)
        plain_cast(din["L0_attn_adj"], ds["w_adjn"], KI, DI, scale=-1.0,
                   tiled=KI)
        plain_cast(din["L0_attn_out_proj_w"], ds["w_outproj"], KI, DM,
                   tiled=KM)
        plain_cast(din["L0_attn_x_proj_w"], ds["w_xproj"], KI, DTR + 2 * NST)
        plain_cast(din["L0_attn_dt_proj_w"], ds["w_dtproj"], 1, DI, rows=64)
    if "attn1" in need:
        for wn, dst in (("L1_attn_Wq", "w_q"), ("L1_attn_Wk", "w_k"),
                        ("L1_attn_Wv", "w_v")):
            plain_cast(din[wn], ds[dst], KM, DM)
        plain_cast(din["L1_attn_Wo"], ds["w_o"], KM, DM, tiled=KM)

    idb, onesf = sb["idb"], sb["onesf"]

    def transpose_store(par, cols, dst, ktile, psum_i, ready, tiled=False):
        nblk = cols // 128
        for g0 in range(0, nblk, 4):
            gn = min(4, nblk - g0)
            tprev = C.psum_free.get(psum_i)
            tmm = None
            for j in range(gn):
                tmm = P.op("t", lambda E, par=par, g0=g0, j=j, psum_i=psum_i:
                           E.transpose(
                               ps[psum_i].ap().bitcast(BF16)
                               [0:128, j * 128:(j + 1) * 128],
                               wb[:, par * 4096 + (g0 + j) * 128:
                                  par * 4096 + (g0 + j + 1) * 128],
                               idb[:]),
                           waits=[tprev, ready] if j == 0 else [],
                           inc="t0" if j == gn - 1 else None)
            te = P.op("v", lambda E, psum_i=psum_i, gn=gn: E.tensor_copy(
                tb[:, psum_i % 3 * 512: psum_i % 3 * 512 + gn * 128],
                ps[psum_i].ap().bitcast(BF16)[:, 0:gn * 128]),
                writes=("s0_tb",), inc="v0",
                waits=[tmm, gqt.get((psum_i % 3, "tb"))])
            C.psum_free[psum_i] = te
            if tiled:
                P.op("s", lambda E, g0=g0, gn=gn, ktile=ktile, dst=dst,
                     psum_i=psum_i: E.dma_start(
                         out=dst[ktile, :, g0 * 128:(g0 + gn) * 128],
                         in_=tb[:, psum_i % 3 * 512:
                                psum_i % 3 * 512 + gn * 128]),
                     inc=f"gts{psum_i % 3}", inc_n=16, waits=[te])
            else:
                P.op("s", lambda E, g0=g0, gn=gn, ktile=ktile, dst=dst,
                     psum_i=psum_i: E.dma_start(
                         out=dst[g0 * 128:(g0 + gn) * 128,
                                 ktile * 128:(ktile + 1) * 128]
                         .rearrange("(g p) c -> p g c", p=128),
                         in_=tb[:, psum_i % 3 * 512:
                                psum_i % 3 * 512 + gn * 128]
                         .rearrange("p (g c) -> p g c", c=128)),
                     inc=f"gts{psum_i % 3}", inc_n=16, waits=[te])
            gqt[(psum_i % 3, "tb")] = P.tok(f"gts{psum_i % 3}")
        vdt[(par, "wbT")] = P.tok("t0")

    def lpbq_weight(src, dstT, O):
        KT, I = O // 128, src.shape[1]
        NB = I // 32
        for k in range(KT):
            par = k % 2
            tl = loadw(par, slice(par * 4096, par * 4096 + I),
                       src[k * 128:(k + 1) * 128, :], I)
            wfs = wf[:, par * 4096:par * 4096 + I]
            wbs = wb[:, par * 4096:par * 4096 + I]
            scs = sca[:, par * 192:par * 192 + NB]
            sns = sca[:, par * 192 + NB:par * 192 + 2 * NB]
            P.op("v", lambda E, wfs=wfs, scs=scs, NB=NB: E.tensor_reduce(
                scs, wfs.rearrange("p (n k) -> p n k", k=32), AX.X,
                ALU.max, apply_absolute_value=True),
                reads=("s0_wf",), writes=("s0_sca",), waits=[tl])
            P.op("v", lambda E, scs=scs: E.tensor_scalar(
                scs, scs, 1.0 / 7.0, 1e-12, ALU.mult, ALU.max),
                reads=("s0_sca",), writes=("s0_sca",))
            P.op("v", lambda E, scs=scs, sns=sns: E.tensor_copy(sns, scs),
                 reads=("s0_sca",), writes=("s0_sca",))
            P.op("v", lambda E, scs=scs: E.reciprocal(scs, scs),
                 reads=("s0_sca",), writes=("s0_sca",))
            def bc(a, NB=NB):
                return a.unsqueeze(2).broadcast_to([128, NB, 32])
            P.op("v", lambda E, wfs=wfs, scs=scs, NB=NB: E.tensor_tensor(
                wfs.rearrange("p (n k) -> p n k", k=32), wfs
                .rearrange("p (n k) -> p n k", k=32), bc(scs), ALU.mult),
                reads=("s0_wf", "s0_sca"), writes=("s0_wf",))
            P.op("v", lambda E, wfs=wfs: E.tensor_scalar(
                wfs, wfs, 12582912.0, 12582912.0, ALU.add, ALU.subtract),
                reads=("s0_wf",), writes=("s0_wf",))
            P.op("v", lambda E, wfs=wfs: E.tensor_scalar(
                wfs, wfs, -8.0, 7.0, ALU.max, ALU.min),
                reads=("s0_wf",), writes=("s0_wf",))
            tq = P.op("v", lambda E, wfs=wfs, sns=sns, wbs=wbs, NB=NB:
                      E.tensor_tensor(
                          wbs.rearrange("p (n k) -> p n k", k=32),
                          wfs.rearrange("p (n k) -> p n k", k=32),
                          bc(sns), ALU.mult),
                      reads=("s0_wf", "s0_sca"), writes=("s0_wb",), inc="v0",
                      waits=[gqt.get((par, "wb")), vdt.get((par, "wbT"))])
            vdt[(par, "wf")] = tq
            transpose_store(par, I, dstT, k, 4 + (k % 3), tq,
                            tiled=(dstT.shape[0] <= 16))

    if "ffn0" in need:
        lpbq_weight(din["L0_ffn_w_up"], ds["w_upT"], HF)
        lpbq_weight(din["L0_ffn_w_gate"], ds["w_gateT"], HF)
        lpbq_weight(din["L0_ffn_w_down"], ds["w_downT"], DM)

    def tern_weight(src, dstT, O, I):
        KT = O // 128
        tprev = C.psum_free.get(6)
        nmm = KT * (I // 512)
        imm = 0
        for k in range(KT):
            par = k % 2
            tl = loadw(par, slice(par * 4096, par * 4096 + I),
                       src[k * 128:(k + 1) * 128, :], I)
            ta = P.op("s", lambda E, par=par, I=I: E.activation(
                wb[:, par * 4096:par * 4096 + I],
                wf[:, par * 4096:par * 4096 + I], AF.Abs),
                reads=("s0_wf",), writes=("s0_wb",), inc="s0",
                waits=[tl, vdt.get((par, "awb")), gqt.get((par, "wb")),
                       vdt.get((par, "wbT"))])
            for g in range(I // 512):
                P.op("t", lambda E, par=par, g=g, first=(imm == 0),
                     last=(imm == nmm - 1):
                     E.matmul(ps[6][0:1, 0:512], sb["onesb"][:, 0:1],
                              wb[:, par * 4096 + g * 512:
                                 par * 4096 + (g + 1) * 512],
                              start=first, stop=last),
                     inc="t0", waits=[ta, tprev])
                imm += 1
            vdt[(par, "awb")] = P.tok("t0")
            vdt[(par, "wf")] = vdt[(par, "awb")]
        tm = P.tok("t0")
        P.op("v", lambda E: E.reduce_sum(row[0:1, 0:1], ps[6][0:1, 0:512],
                                         AX.X), writes=("s0_row",),
             waits=[tm])
        P.op("v", lambda E, n=O * I: E.tensor_scalar(
            row[0:1, 1:2], row[0:1, 0:1], 1.0 / n, 1e-5, ALU.mult, ALU.max),
            reads=("s0_row",), writes=("s0_row",))
        P.op("v", lambda E: E.reciprocal(row[0:1, 2:3], row[0:1, 1:2]),
             reads=("s0_row",), writes=("s0_row",))
        tr = P.op("v", lambda E: E.tensor_scalar(
            row[0:1, 3:4], row[0:1, 1:2], 0.5, None, ALU.mult),
            reads=("s0_row",), writes=("s0_row",), inc="v0")
        tg = P.op("g", lambda E: E.partition_broadcast(row[:, 1:4],
                                                       row[0:1, 1:4]),
                  reads=("s0_row",), writes=("s0_row",), inc="g0",
                  waits=[tr])
        C.psum_free[6] = tg
        for k in range(KT):
            par = k % 2
            tl = loadw(par, slice(par * 4096, par * 4096 + I),
                       src[k * 128:(k + 1) * 128, :], I)
            wfs = wf[:, par * 4096:par * 4096 + I]
            wbs = wb[:, par * 4096:par * 4096 + I]
            ta = P.op("s", lambda E, wfs=wfs, wbs=wbs: E.activation(
                wbs, wfs, AF.Abs), reads=("s0_wf",), writes=("s0_wb",),
                inc="s0", waits=[tl, tg, vdt.get((par, "twb")),
                                 gqt.get((par, "wb")),
                                 vdt.get((par, "wbT"))])
            tsg = P.op("s", lambda E, wfs=wfs: E.sign(wfs, wfs),
                       reads=("s0_wf",), writes=("s0_wf",), inc="s0")
            tmsk = P.op("v", lambda E, wbs=wbs: E.tensor_scalar(
                wbs, wbs, row[:, 3:4], None, ALU.is_ge),
                reads=("s0_wb", "s0_row"), writes=("s0_wb",), waits=[ta],
                inc="v0")
            tq = P.op("v", lambda E, wbs=wbs, wfs=wfs: E.scalar_tensor_tensor(
                wbs, wbs, row[:, 1:2], wfs, ALU.mult, ALU.mult),
                reads=("s0_wb", "s0_wf", "s0_row"), writes=("s0_wb",),
                inc="v0", waits=[tsg])
            vdt[(par, "wf")] = tq
            vdt[(par, "twb")] = tq
            transpose_store(par, I, dstT, k, 4 + (k % 3), tq,
                            tiled=(dstT.shape[0] <= 16))

    if "ffn1" in need:
        tern_weight(din["L1_ffn_w1"], ds["w_w1T"], DH, DM)
        tern_weight(din["L1_ffn_w2"], ds["w_w2T"], DM, DH)
    st.close()
    return [P.tok(s) for s in sorted(P.dma_sems)
            if s.startswith(("gw", "gts"))]


def ln_rows(C, ps_sum, ps_sq, n_inv, rowc, tmm, tc=None):
    P, ps, rows = C.P, C.ps, C.rows
    Tc = tc if tc is not None else C.Tc
    m = rows[0:1, rowc:rowc + Tc]
    e2 = rows[0:1, rowc + Tc:rowc + 2 * Tc]
    r = rows[0:1, rowc + 2 * Tc:rowc + 3 * Tc]
    mr = rows[0:1, rowc + 3 * Tc:rowc + 4 * Tc]
    P.op("v", lambda E: E.tensor_scalar(m, ps_sum[0:1, 0:Tc], n_inv, None,
                                        ALU.mult),
         writes=("rows",), waits=[tmm])
    P.op("v", lambda E: E.tensor_scalar(e2, ps_sq[0:1, 0:Tc], n_inv, None,
                                        ALU.mult),
         reads=(), writes=("rows",))
    P.op("v", lambda E: E.tensor_tensor(r, m, m, ALU.mult),
         reads=("rows",), writes=("rows",))
    P.op("v", lambda E: E.tensor_tensor(r, e2, r, ALU.subtract),
         reads=("rows",), writes=("rows",))
    P.op("v", lambda E: E.tensor_scalar(r, r, EPS, None, ALU.add),
         reads=("rows",), writes=("rows",))
    tv = P.op("v", lambda E: E.reciprocal(r, r), reads=("rows",),
              writes=("rows",), inc="v1")
    ts = P.op("s", lambda E: E.activation(r, r, AF.Sqrt), reads=("rows",),
              writes=("rows",), inc="s1", waits=[tv])
    tv2 = P.op("v", lambda E: E.tensor_tensor(mr, m, r, ALU.mult),
               reads=("rows",), writes=("rows",), inc="v1", waits=[ts])
    onesf = C.sb["onesf"]
    t4 = P.op("t", lambda E: E.matmul(ps[4][:, 0:Tc], onesf[0:1, 0:128], mr,
                                      start=True, stop=True), inc="t1",
              waits=[tv2, C.psum_free.get(4)])
    t5 = P.op("t", lambda E: E.matmul(ps[5][:, 0:Tc], onesf[0:1, 0:128], r,
                                      start=True, stop=True), inc="t1",
              waits=[tv2, C.psum_free.get(5)])
    return t4, t5


def s1_mamba(C):
    P, nc, ds, sb, ps, L = C.P, C.nc, C.ds, C.sb, C.ps, C.L
    Tc = 256
    ncpb = L // Tc
    steps = BPC * ncpb
    st = ExitStack()

    def S(name, shape, dt):
        return st.enter_context(nc.sbuf_tensor(name, shape, dt))
    xlnb = S("m_xlnb", [128, KM * Tc], BF16)
    sqb = S("m_sqb", [128, Tc], BF16)
    moch = S("m_moch", [128, KM * Tc], BF16)
    xtl = S("m_xtl", [128, 2 * Tc], F32)
    kxt = S("m_kxt", [128, 4 * Tc], F32)
    wipf = S("m_wipf", [128, 2 * 1024], BF16)
    adjf = S("m_adjf", [128, 2 * 2048], BF16)
    opf = S("m_opf", [128, 2 * 2048], BF16)
    xirch = S("m_xirch", [128, KI * (Tc + 3)], BF16)
    sgt = S("m_sgt", [128, 2 * Tc], F32)
    xdbd = S("m_xdbd", [64, Tc], BF16)
    bcb = S("m_bcb", [32, Tc], BF16)
    hpl = S("m_hpl", [128, NST * Tc], BF16)
    ab = S("m_ab", [128, 8 * Tc], BF16)
    bb = S("m_bb", [128, NST * Tc], BF16)
    yb = S("m_yb", [128, Tc], F32)
    hplF = hpl.ap().bitcast(F32)
    bbF = bb.ap().bitcast(F32)
    HPLN = tuple(f"hpl{n}" for n in range(NST))
    BBN = tuple(f"bb{n}" for n in range(NST))
    XLN = tuple(f"xlnb{k}" for k in range(KM))
    xibq = [S(f"m_xib{q}", [128, KI * Tc], BF16) for q in (0, 1)]
    sresq = [S(f"m_sres{q}", [128, KI * Tc], BF16) for q in (0, 1)]
    dltq = [S(f"m_dlt{q}", [128, KI * Tc], BF16) for q in (0, 1)]
    dubq = [S(f"m_dub{q}", [128, KI * Tc], BF16) for q in (0, 1)]
    dly2q = [S(f"m_dly2{q}", [128, KI * Tc], BF16) for q in (0, 1)]
    Brepq = [S(f"m_Brep{q}", [128, NST * Tc], BF16) for q in (0, 1)]
    Crepq = [S(f"m_Crep{q}", [128, NST * Tc], BF16) for q in (0, 1)]
    negA, Dsb, cwsb, cbsb, dtbn = (sb[x] for x in
                                   ("negA", "Dsb", "cwsb", "cbsb", "dtbn"))
    hcar, tails, xprj, dtprj, selb = (sb[x] for x in
                                      ("hcar", "tails", "xprj", "dtprj",
                                       "selb"))
    tx = P.op("q", _dma(xprj.ap().rearrange("p (k c) -> p k c", c=96),
                        ds["w_xproj"].rearrange("(k p) c -> p k c", p=128)),
              inc="qw0", inc_n=16, waits=list(C.w_done))
    td_ = P.op("q", _dma(dtprj[:], ds["w_dtproj"][:, :]), inc="qw1",
               inc_n=16)
    C.s1_resident = (tx, td_)

    rot = {}
    fr = [{}, {}]
    rel = [None, None]

    def front_A_stats(s):
        b, c = divmod(s, ncpb)
        cs = slice(c * Tc, (c + 1) * Tc)
        first = True
        for k in range(KM):
            par = k % 2
            tl = P.op("q", _dma(xtl[:, par * Tc:(par + 1) * Tc],
                                ds["resid"][b, k, :, cs]),
                      inc=f"qt{par}", inc_n=16,
                      waits=[rot.get(("xtl", par)), rot.get(("xtl2", par))])
            tc1 = P.op("v", lambda E, k=k, par=par: E.tensor_copy(
                xlnb[:, k * Tc:(k + 1) * Tc],
                xtl[:, par * Tc:(par + 1) * Tc]),
                writes=(f"xlnb{k}",), inc="v1",
                waits=[tl] + ([rot.get("xlnb_user")] if k == 0 else []),
                clears=XLN if k == 0 else ())
            tc2 = P.op("s", lambda E, par=par: E.activation(
                sqb[:], xtl[:, par * Tc:(par + 1) * Tc], AF.Square),
                writes=("sqb",), inc="s1",
                waits=[tl, rot.get("sqb_user")], clears=("sqb",))
            rot[("xtl", par)] = tc1
            rot[("xtl2", par)] = tc2
            P.op("t", lambda E, k=k, first=first: E.matmul(
                ps[6][0:1, 0:Tc], sb["onesb"][:, 0:1],
                xlnb[:, k * Tc:(k + 1) * Tc], start=first, stop=(k == KM - 1)),
                inc="t1", waits=[tc1] + ([C.psum_free.get(6)]
                                         if first else []))
            t7 = P.op("t", lambda E, first=first, k=k: E.matmul(
                ps[7][0:1, 0:Tc], sb["onesb"][:, 0:1], sqb[:],
                start=first, stop=(k == KM - 1)),
                inc="t1", waits=[tc2] + ([C.psum_free.get(7)]
                                         if first else []))
            rot["sqb_user"] = t7
            first = False
        return P.tok("t1")

    def front_A_rows(s, tmm):
        t4, t5 = ln_rows(C, ps[6], ps[7], 1.0 / DM, 0, tmm, tc=Tc)
        C.psum_free[6] = P.tok("v1")
        C.psum_free[7] = C.psum_free[6]
        tnorm = None
        for k in range(KM):
            P.op("v", lambda E, k=k: E.tensor_tensor(
                xlnb[:, k * Tc:(k + 1) * Tc],
                xlnb[:, k * Tc:(k + 1) * Tc], ps[5][:, 0:Tc], ALU.mult),
                reads=(f"xlnb{k}",), writes=(f"xlnb{k}",), waits=[t4, t5])
        for k in range(KM):
            tnorm = P.op("v", lambda E, k=k: E.tensor_tensor(
                xlnb[:, k * Tc:(k + 1) * Tc],
                xlnb[:, k * Tc:(k + 1) * Tc], ps[4][:, 0:Tc], ALU.subtract),
                reads=(f"xlnb{k}",), writes=(f"xlnb{k}",),
                inc="v1" if k == KM - 1 else None)
        C.psum_free[4] = tnorm
        C.psum_free[5] = tnorm
        return tnorm

    def front_Bpart(s, part, tnorm):
        q = s % 2
        xib, sres = xibq[q], sresq[q]
        relw = list(rel[q]) if rel[q] else []
        for f in range(4 * part, 4 * part + 4):
            par = pi = f % 2
            tw = P.op("q", lambda E, f=f, par=par: E.dma_start(
                out=wipf[:, par * 1024:(par + 1) * 1024],
                in_=ds["w_inproj"][f]),
                inc=f"qp{par}", inc_n=16, waits=[rot.get(("wipf", par))])
            tmm = None
            for k in range(KM):
                tmm = P.op("t", lambda E, k=k, par=par, pi=pi: E.matmul(
                    ps[pi][:, 0:Tc],
                    wipf[:, par * 1024 + k * 128:par * 1024 + (k + 1) * 128],
                    xlnb[:, k * Tc:(k + 1) * Tc],
                    start=(k == 0), stop=(k == KM - 1)),
                    waits=([tw, tnorm, C.psum_free.get(pi)]
                           if k == 0 else []),
                    inc="t1" if k == KM - 1 else None)
            rot[("wipf", par)] = tmm
            if f < KI:
                te = P.op("s", lambda E, f=f, pi=pi: E.activation(
                    xirch[:, f * (Tc + 3) + 3:(f + 1) * (Tc + 3)],
                    ps[pi][:, 0:Tc], AF.Copy),
                    writes=("xirch",), inc="s1",
                    waits=[tmm] + ([rot.get("xirch_user")] if f == 0 else []),
                    clears=("xirch",) if f == 0 else ())
                fr[q][("xiev", f)] = te
            else:
                f2 = f - KI
                te = P.op("s", lambda E, f2=f2, pi=pi, sres=sres:
                          E.activation(
                              sres[:, f2 * Tc:(f2 + 1) * Tc],
                              ps[pi][:, 0:Tc], AF.Silu),
                          writes=(f"sres{q}",), inc="s1",
                          waits=([tmm]
                                 + ([rot.get(("sresK", q))] + relw
                                    if f2 == 0 else [])),
                          clears=(f"sres{q}",) if f2 == 0 else ())
                if f2 == KI - 1:
                    fr[q]["sres"] = te
            C.psum_free[pi] = te
        if part == 7:
            rot["xlnb_user"] = tmm
        if part == 0:
            for f2 in (0, 1):
                fr[q][("adjld", f2)] = P.op(
                    "q", lambda E, f2=f2: E.dma_start(
                        out=adjf[:, f2 * 2048:(f2 + 1) * 2048],
                        in_=ds["w_adjn"][f2]),
                    inc=f"qa{f2}", inc_n=16, waits=[rot.get(("adjf", f2))])

    def back_H_k(t, k):
        q = t % 2
        b, c = divmod(t, ncpb)
        xib, sres, dlt, dub = xibq[q], sresq[q], dltq[q], dubq[q]
        dly2, Brep, Crep = dly2q[q], Brepq[q], Crepq[q]
        frq = fr[q]
        if k == 0 and c == 0:
            P.op("v", lambda E: E.memset(hcar[:], 0.0), writes=("hcar",))
        def emit_exps(n0):
            last = None
            for n in range(n0, n0 + 8):
                r = n % 8
                last = P.op("s", lambda E, k=k, n=n, r=r, dlt=dlt:
                            E.activation(
                                ab[:, r * Tc:(r + 1) * Tc],
                                dlt[:, k * Tc:(k + 1) * Tc], AF.Exp,
                                scale=negA[:, k * NST + n:k * NST + n + 1]),
                            reads=(f"dlt{q}",), writes=(f"ab{r}",), inc="s1",
                            waits=[rot.get(("ab", r))]
                            + ([frq.get("dlt")]
                               if k == 0 and n == 0 else []),
                            clears=(f"ab{r}",))
            return last

        ta7 = emit_exps(0)
        for n in range(NST):
            P.op("v", lambda E, k=k, n=n, dub=dub, Brep=Brep:
                 E.tensor_tensor(
                     bb[:, n * Tc:(n + 1) * Tc],
                     dub[:, k * Tc:(k + 1) * Tc],
                     Brep[:, n * Tc:(n + 1) * Tc], ALU.mult),
                 reads=(f"dub{q}", f"Brep{q}"), writes=(f"bb{n}",),
                 waits=[frq.get("dub"), frq.get("Brep")]
                 if k == 0 and n == 0 else ())
        ta15 = None
        for n in range(NST):
            r = n % 8
            tsc = P.op("v", lambda E, k=k, n=n, r=r:
                       E.tensor_tensor_scan(
                           hpl[:, n * Tc:(n + 1) * Tc],
                           ab[:, r * Tc:(r + 1) * Tc],
                           bb[:, n * Tc:(n + 1) * Tc],
                           hcar[:, k * NST + n:k * NST + n + 1],
                           ALU.mult, ALU.add),
                       reads=(f"bb{n}", "hcar"), writes=(f"hpl{n}",),
                       inc="v1",
                       waits=[ta7] if n == 0 else ([ta15] if n == 8 else ()))
            rot[("ab", r)] = tsc
            if n == 7:
                rot[("ab", "half")] = tsc
                ta15 = emit_exps(8)
        P.op("v", lambda E, k=k: E.tensor_copy(
            hcar[:, k * NST:(k + 1) * NST],
            hpl.ap().rearrange("p (n t) -> p n t", n=NST)[:, :, Tc - 1]),
            reads=HPLN, writes=("hcar",))
        for n in range(NST):
            P.op("v", lambda E, n=n, Crep=Crep: E.tensor_tensor(
                bb[:, n * Tc:(n + 1) * Tc],
                hpl[:, n * Tc:(n + 1) * Tc],
                Crep[:, n * Tc:(n + 1) * Tc], ALU.mult),
                reads=(f"hpl{n}", f"Crep{q}"), writes=(f"bb{n}",),
                waits=[frq.get("Crep")] if k == 0 and n == 0 else ())
        for j in range(8):
            P.op("v", lambda E, j=j: E.tensor_tensor(
                hplF[:, j * Tc:(j + 1) * Tc],
                bb[:, 2 * j * Tc:(2 * j + 1) * Tc],
                bb[:, (2 * j + 1) * Tc:(2 * j + 2) * Tc], ALU.add),
                reads=(f"bb{2*j}", f"bb{2*j+1}"),
                writes=(f"hpl{2*j}", f"hpl{2*j+1}"))
        for j in range(4):
            P.op("v", lambda E, j=j: E.tensor_tensor(
                bbF[:, j * Tc:(j + 1) * Tc],
                hplF[:, 2 * j * Tc:(2 * j + 1) * Tc],
                hplF[:, (2 * j + 1) * Tc:(2 * j + 2) * Tc], ALU.add),
                reads=tuple(f"hpl{4*j+i}" for i in range(4)),
                writes=(f"bb{2*j}", f"bb{2*j+1}"))
        for j in range(2):
            P.op("v", lambda E, j=j: E.tensor_tensor(
                hplF[:, j * Tc:(j + 1) * Tc],
                bbF[:, 2 * j * Tc:(2 * j + 1) * Tc],
                bbF[:, (2 * j + 1) * Tc:(2 * j + 2) * Tc], ALU.add),
                reads=tuple(f"bb{4*j+i}" for i in range(4)),
                writes=(f"hpl{2*j}", f"hpl{2*j+1}"))
        P.op("v", lambda E: E.tensor_tensor(
            yb[:], hplF[:, 0:Tc], hplF[:, Tc:2 * Tc], ALU.add),
            reads=("hpl0", "hpl1", "hpl2", "hpl3"), writes=("yb",))
        P.op("v", lambda E, k=k, xib=xib: E.scalar_tensor_tensor(
            yb[:], xib[:, k * Tc:(k + 1) * Tc], Dsb[:, k:k + 1],
            yb[:], ALU.mult, ALU.add),
            reads=("yb", f"xib{q}"), writes=("yb",),
            waits=[frq.get("xib")] if k == 0 else ())
        ty2 = P.op("v", lambda E, k=k, dly2=dly2, sres=sres: E.tensor_tensor(
            dly2[:, k * Tc:(k + 1) * Tc], yb[:],
            sres[:, k * Tc:(k + 1) * Tc], ALU.mult),
            reads=("yb", f"sres{q}"), writes=(f"dly2{q}",), inc="v1",
            waits=[frq.get("sres"), frq.get("deltap_mms")] if k == 0 else ())
        return ty2 if k == NST - 1 else None

    def front_piece(s, k):
        q = s % 2
        b, c = divmod(s, ncpb)
        frq = fr[q]
        relw = list(rel[q]) if rel[q] else []
        xib, sres, dlt, dub = xibq[q], sresq[q], dltq[q], dubq[q]
        dly2, Brep, Crep = dly2q[q], Brepq[q], Crepq[q]
        if k < 8:
            if k == 0 and c == 0:
                P.op("v", lambda E: E.memset(tails[:], 0.0),
                     writes=tuple(f"tails{f}" for f in range(KI)),
                     waits=[rot.get("tails_user")])
            for f in (2 * k, 2 * k + 1):
                xs = xirch[:, f * (Tc + 3):(f + 1) * (Tc + 3)]
                P.op("v", lambda E, xs=xs, f=f: E.tensor_copy(
                    xs[:, 0:3], tails[:, f * 3:(f + 1) * 3]),
                    reads=(f"tails{f}",), writes=("xirchT",), inc="v1",
                    waits=[frq[("xiev", f)]])
            for f in (2 * k, 2 * k + 1):
                xs = xirch[:, f * (Tc + 3):(f + 1) * (Tc + 3)]
                par = f % 2
                P.op("v", lambda E, xs=xs, f=f, par=par: E.tensor_scalar(
                    sgt[:, par * Tc:(par + 1) * Tc], xs[:, 0:Tc],
                    cwsb[:, f * 4:f * 4 + 1], None, ALU.mult),
                    reads=("xirchT",), writes=(f"sgt{par}",),
                    waits=[rot.get(("sgt", par))], clears=(f"sgt{par}",))
                tcv = None
                for kk in range(1, 4):
                    tcv = P.op("v", lambda E, xs=xs, f=f, kk=kk, par=par:
                               E.scalar_tensor_tensor(
                                   sgt[:, par * Tc:(par + 1) * Tc],
                                   xs[:, kk:kk + Tc],
                                   cwsb[:, f * 4 + kk:f * 4 + kk + 1],
                                   sgt[:, par * Tc:(par + 1) * Tc],
                                   ALU.mult, ALU.add),
                               reads=(f"sgt{par}",), writes=(f"sgt{par}",),
                               inc="v1" if kk == 3 else None)
                tt1 = P.op("v", lambda E, xs=xs, f=f: E.tensor_copy(
                    tails[:, f * 3:(f + 1) * 3], xs[:, Tc:Tc + 3]),
                    reads=(), writes=(), inc="v1")
                rot["tails_user"] = tt1
                ts = P.op("s", lambda E, f=f, par=par, xib=xib: E.activation(
                    xib[:, f * Tc:(f + 1) * Tc],
                    sgt[:, par * Tc:(par + 1) * Tc], AF.Silu,
                    bias=cbsb[:, f:f + 1]),
                    reads=(f"sgt{par}",), writes=(f"xib{q}",), inc="s1",
                    waits=[tcv] + ([rot.get(("sresK", q))] + relw
                                   if f == 0 else []),
                    clears=(f"xib{q}",) if f == 0 else ())
                rot[("sgt", par)] = ts
                if f == KI - 1:
                    frq["xib"] = ts
            if k == 7:
                rot["xirch_user"] = P.tok("v1")
        elif k == 8:
            txib = frq["xib"]
            tA = tB = None
            for kk in range(KI):
                tA = P.op("t", lambda E, kk=kk, xib=xib: E.matmul(
                    ps[2][0:64, 0:Tc], xprj[:, kk * 96:kk * 96 + 64],
                    xib[:, kk * Tc:(kk + 1) * Tc],
                    start=(kk == 0), stop=(kk == KI - 1)),
                    waits=([txib, C.psum_free.get(2), C.s1_resident[0]]
                           if kk == 0 else []),
                    inc="t1" if kk == KI - 1 else None)
            for kk in range(KI):
                tB = P.op("t", lambda E, kk=kk, xib=xib: E.matmul(
                    ps[3][0:32, 0:Tc], xprj[:, kk * 96 + 64:(kk + 1) * 96],
                    xib[:, kk * Tc:(kk + 1) * Tc],
                    start=(kk == 0), stop=(kk == KI - 1)),
                    waits=[C.psum_free.get(3)] if kk == 0 else [],
                    inc="t1" if kk == KI - 1 else None)
            tev1 = P.op("s", lambda E: E.activation(xdbd[:], ps[2][0:64, 0:Tc],
                                                    AF.Copy),
                        writes=("xdbd",), inc="s1",
                        waits=[tA, rot.get("xdbd_user")])
            tev2 = P.op("s", lambda E: E.activation(bcb[:], ps[3][0:32, 0:Tc],
                                                    AF.Copy),
                        writes=("bcb",), inc="s1",
                        waits=[tB, rot.get("bcb_user")])
            C.psum_free[2] = tev1
            C.psum_free[3] = tev2
            frq["tev1"] = tev1
            frq["tev2"] = tev2
        elif k == 9:
            tev1 = frq["tev1"]
            for f in range(KI):
                pi = par = f % 2
                tmm = P.op("t", lambda E, f=f, pi=pi: E.matmul(
                    ps[pi][:, 0:Tc], dtprj[:, f * 128:(f + 1) * 128],
                    xdbd[:], start=True, stop=True),
                    waits=[tev1, C.psum_free.get(pi), C.s1_resident[1]],
                    inc="t1")
                tsg = P.op("s", lambda E, pi=pi, f=f, par=par: E.activation(
                    sgt[:, par * Tc:(par + 1) * Tc], ps[pi][:, 0:Tc],
                    AF.Sigmoid, scale=-1.0, bias=dtbn[:, f:f + 1]),
                    writes=(f"sgt{par}",), inc="s1",
                    waits=[tmm, rot.get(("sgt", par))],
                    clears=(f"sgt{par}",))
                C.psum_free[pi] = tsg
                tln = P.op("s", lambda E, f=f, par=par, dly2=dly2:
                           E.activation(
                               dly2[:, f * Tc:(f + 1) * Tc],
                               sgt[:, par * Tc:(par + 1) * Tc], AF.Ln),
                           reads=(f"sgt{par}",), writes=(f"dly2{q}",),
                           inc="s1", waits=relw if f == 0 else (),
                           clears=(f"dly2{q}",) if f == 0 else ())
                rot[("sgt", par)] = tln
            frq["dL"] = P.tok("s1")
        elif k in (10, 11):
            tdl = frq["dL"]
            for f in range(8 * (k - 10), 8 * (k - 9)):
                par = pi = f % 2
                tmm = None
                for kk in range(KI):
                    tmm = P.op("t", lambda E, kk=kk, par=par, pi=pi,
                               dly2=dly2: E.matmul(
                                   ps[pi][:, 0:Tc],
                                   adjf[:, par * 2048 + kk * 128:
                                        par * 2048 + (kk + 1) * 128],
                                   dly2[:, kk * Tc:(kk + 1) * Tc],
                                   start=(kk == 0), stop=(kk == KI - 1)),
                               waits=([frq[("adjld", f)], tdl,
                                       C.psum_free.get(pi)]
                                      if kk == 0 else []),
                               inc="t1" if kk == KI - 1 else None)
                if f + 2 < KI:
                    frq[("adjld", f + 2)] = P.op(
                        "q", lambda E, f=f, par=par: E.dma_start(
                            out=adjf[:, par * 2048:(par + 1) * 2048],
                            in_=ds["w_adjn"][f + 2]),
                        inc=f"qa{par}", inc_n=16, waits=[tmm])
                else:
                    rot[("adjf", par)] = tmm
                te = P.op("s", lambda E, f=f, pi=pi, dlt=dlt: E.activation(
                    dlt[:, f * Tc:(f + 1) * Tc], ps[pi][:, 0:Tc], AF.Copy),
                    writes=(f"dlt{q}",), inc="s1",
                    waits=[tmm] + (relw if f == 0 else []),
                    clears=(f"dlt{q}",) if f == 0 else ())
                C.psum_free[pi] = te
            if k == 11:
                frq["dlt"] = P.tok("s1")
                frq["deltap_mms"] = P.tok("t1")
        elif k == 12:
            tev2 = frq["tev2"]
            for n in range(NST):
                pi = n % 2
                tm1 = P.op("t", lambda E, n=n, pi=pi: E.matmul(
                    ps[pi][:, 0:Tc], selb[:, n * 128:(n + 1) * 128], bcb[:],
                    start=True, stop=True), inc="t1",
                    waits=[tev2, C.psum_free.get(pi)])
                te1 = P.op("s", lambda E, n=n, pi=pi, Brep=Brep: E.activation(
                    Brep[:, n * Tc:(n + 1) * Tc], ps[pi][:, 0:Tc], AF.Copy),
                    writes=(f"Brep{q}",), inc="s1",
                    waits=[tm1] + (relw if n == 0 else []),
                    clears=(f"Brep{q}",) if n == 0 else ())
                C.psum_free[pi] = te1
                tm2 = P.op("t", lambda E, n=n, pi=pi: E.matmul(
                    ps[pi][:, 0:Tc], selb[:, (16 + n) * 128:(17 + n) * 128],
                    bcb[:], start=True, stop=True), inc="t1",
                    waits=[C.psum_free.get(pi)])
                te2 = P.op("s", lambda E, n=n, pi=pi, Crep=Crep: E.activation(
                    Crep[:, n * Tc:(n + 1) * Tc], ps[pi][:, 0:Tc], AF.Copy),
                    writes=(f"Crep{q}",), inc="s1",
                    waits=[tm2] + (relw if n == 0 else []),
                    clears=(f"Crep{q}",) if n == 0 else ())
                C.psum_free[pi] = te2
            frq["Brep"] = P.tok("s1")
            frq["Crep"] = frq["Brep"]
            rot["bcb_user"] = frq["Brep"]
            rot["xdbd_user"] = frq["Brep"]
        elif k == 13:
            tdu = None
            for kk in range(KI):
                tdu = P.op("v", lambda E, kk=kk, dub=dub, dlt=dlt, xib=xib:
                           E.tensor_tensor(
                               dub[:, kk * Tc:(kk + 1) * Tc],
                               dlt[:, kk * Tc:(kk + 1) * Tc],
                               xib[:, kk * Tc:(kk + 1) * Tc], ALU.mult),
                           reads=(f"dlt{q}", f"xib{q}"), writes=(f"dub{q}",),
                           waits=[frq["dlt"]] + (relw if kk == 0 else []),
                           clears=(f"dub{q}",) if kk == 0 else (),
                           inc="v1" if kk == KI - 1 else None)
            frq["dub"] = tdu

    def back_I(t, ty2):
        q = t % 2
        dly2 = dly2q[q]
        ld = {}
        for f in (0, 1):
            ld[f] = P.op("q", lambda E, f=f: E.dma_start(
                out=opf[:, f * 2048:(f + 1) * 2048], in_=ds["w_outproj"][f]),
                inc=f"qo{f}", inc_n=16, waits=[rot.get(("opf", f))])
        for f in range(KM):
            par = pi = f % 2
            tmm = None
            for k in range(KI):
                tmm = P.op("t", lambda E, k=k, pi=pi, par=par, dly2=dly2:
                           E.matmul(
                               ps[pi][:, 0:Tc],
                               opf[:, par * 2048 + k * 128:
                                   par * 2048 + (k + 1) * 128],
                               dly2[:, k * Tc:(k + 1) * Tc],
                               start=(k == 0), stop=(k == KI - 1)),
                           waits=([ld[f], ty2, C.psum_free.get(pi)]
                                  if k == 0 else []),
                           inc="t1" if k == KI - 1 else None)
            if f + 2 < KM:
                ld[f + 2] = P.op("q", lambda E, f=f, par=par: E.dma_start(
                    out=opf[:, par * 2048:(par + 1) * 2048],
                    in_=ds["w_outproj"][f + 2]),
                    inc=f"qo{par}", inc_n=16, waits=[tmm])
            else:
                rot[("opf", par)] = tmm
            te = P.op("s", lambda E, f=f, pi=pi: E.activation(
                moch[:, f * Tc:(f + 1) * Tc], ps[pi][:, 0:Tc], AF.Copy),
                writes=("moch",), inc="s1",
                waits=[tmm] + ([rot.get("moch_user")] if f == 0 else []),
                clears=("moch",) if f == 0 else ())
            C.psum_free[pi] = te
        tmo_mms = P.tok("t1")
        tmo = P.tok("s1")
        rel[q] = (ty2, tmo_mms)
        return tmo

    def back_JK(t, ty2, tmo):
        q = t % 2
        b, c = divmod(t, ncpb)
        cs = slice(c * Tc, (c + 1) * Tc)
        sres = sresq[q]
        first = True
        for k in range(KM):
            P.op("t", lambda E, k=k, first=first: E.matmul(
                ps[2][0:1, 0:Tc], sb["onesb"][:, 0:1],
                moch[:, k * Tc:(k + 1) * Tc], start=first,
                stop=(k == KM - 1)),
                inc="t1", waits=[tmo] + ([C.psum_free.get(2)]
                                         if first else []))
            tsq = P.op("s", lambda E, k=k: E.activation(
                sqb[:], moch[:, k * Tc:(k + 1) * Tc], AF.Square),
                writes=("sqb",), inc="s1",
                waits=[tmo, rot.get("sqb_user")], clears=("sqb",))
            t7 = P.op("t", lambda E, first=first, k=k: E.matmul(
                ps[3][0:1, 0:Tc], sb["onesb"][:, 0:1], sqb[:],
                start=first, stop=(k == KM - 1)), inc="t1",
                waits=[tsq] + ([C.psum_free.get(3)] if first else []))
            rot["sqb_user"] = t7
            first = False
        tmm = P.tok("t1")
        t4, t5 = ln_rows(C, ps[2], ps[3], 1.0 / DM, 1040, tmm, tc=Tc)
        C.psum_free[2] = P.tok("v1")
        C.psum_free[3] = C.psum_free[2]
        xnew = sres.ap().bitcast(F32)
        tadd = None
        tld = {}
        for k in range(4):
            tld[k] = P.op("q", _dma(kxt[:, k * Tc:(k + 1) * Tc],
                                    ds["resid"][b, k, :, cs]),
                          inc=f"qk{k}", inc_n=16,
                          waits=[rot.get(("kxt", k))])
        for k in range(KM):
            P.op("v", lambda E, k=k, xnew=xnew: E.tensor_tensor(
                xnew[:, k * Tc:(k + 1) * Tc],
                moch[:, k * Tc:(k + 1) * Tc], ps[5][:, 0:Tc], ALU.mult),
                reads=("moch",), writes=(f"xnK{k}",),
                waits=[t4, t5, ty2] if k == 0 else (),
                clears=(f"sres{q}",) if k == 0 else ())
        for k in range(KM):
            P.op("v", lambda E, k=k, xnew=xnew: E.tensor_tensor(
                xnew[:, k * Tc:(k + 1) * Tc],
                xnew[:, k * Tc:(k + 1) * Tc], ps[4][:, 0:Tc],
                ALU.subtract),
                reads=(f"xnK{k}",), writes=(f"xnK{k}",))
        for k in range(KM):
            par = k % 4
            tadd = P.op("v", lambda E, k=k, par=par, xnew=xnew:
                        E.tensor_tensor(
                            xnew[:, k * Tc:(k + 1) * Tc],
                            xnew[:, k * Tc:(k + 1) * Tc],
                            kxt[:, par * Tc:(par + 1) * Tc], ALU.add),
                        reads=(f"xnK{k}",), writes=(f"xnK{k}",), inc="v1",
                        waits=[tld[k]])
            rot[("kxt", par)] = tadd
            if k + 4 < KM:
                tld[k + 4] = P.op(
                    "q", _dma(kxt[:, par * Tc:(par + 1) * Tc],
                              ds["resid"][b, k + 4, :, cs]),
                    inc=f"qk{par}", inc_n=16, waits=[tadd])
        C.psum_free[4] = tadd
        C.psum_free[5] = tadd
        tst = P.op("s", lambda E, b=b, cs=cs, xnew=xnew: E.dma_start(
            out=ds["resid"][b, :, :, cs].rearrange("k p t -> p k t"),
            in_=xnew.rearrange("p (k t) -> p k t", k=KM)),
            inc="gr0", inc_n=16, waits=[tadd])
        rot["moch_user"] = tst
        rot[("sresK", q)] = tst

    pend = {}
    for s in range(steps + 1):
        tnorm = None
        if s < steps:
            if s not in pend:
                pend[s] = front_A_stats(s)
            tnorm = front_A_rows(s, pend.pop(s))
        ty2 = None
        for k in range(NST):
            if s >= 1:
                tk = back_H_k(s - 1, k)
                if tk is not None:
                    ty2 = tk
            if s < steps:
                if k < 8:
                    front_Bpart(s, k, tnorm)
                front_piece(s, k)
            if k == 14 and s + 1 < steps:
                pend[s + 1] = front_A_stats(s + 1)
        if s >= 1:
            tmo = back_I(s - 1, ty2)
            back_JK(s - 1, ty2, tmo)
    st.close()
    return P.tok("gr0")

def s_out(C, done_tok, src="resid"):
    P, nc, ds, sb, ps, L = C.P, C.nc, C.ds, C.sb, C.ps, C.L
    st = ExitStack()
    xf = st.enter_context(nc.sbuf_tensor("o_xf", [128, 2 * KM * 128], F32))
    ot = st.enter_context(nc.sbuf_tensor("o_ot", [128, 2 * DM], F32))
    rot = {}
    for b in range(BPC):
        for tt in range(L // 128):
            par = tt % 2
            tl = P.op("q", lambda E, b=b, tt=tt, par=par: E.dma_start(
                out=xf[:, par * KM * 128:(par + 1) * KM * 128]
                .rearrange("p (k t) -> p k t", t=128),
                in_=ds[src][b, :, :, tt * 128:(tt + 1) * 128]
                .rearrange("k p t -> p k t")),
                inc=f"qf{par}", inc_n=16,
                waits=[done_tok, rot.get(("xf", par))])
            for half in range(2):
                pi = half
                tmm = None
                for j in range(4):
                    k = half * 4 + j
                    tmm = P.op("t", lambda E, k=k, j=j, par=par, pi=pi:
                               E.transpose(
                                   ps[pi][:, j * 128:(j + 1) * 128],
                                   xf[:, par * KM * 128 + k * 128:
                                      par * KM * 128 + (k + 1) * 128],
                                   sb["idf"][:]),
                               waits=([tl, C.psum_free.get(pi)]
                                      if j == 0 else []),
                               inc="t1" if j == 3 else None)
                te = P.op("s", lambda E, par=par, half=half, pi=pi:
                          E.activation(
                              ot[:, par * DM + half * 512:
                                 par * DM + (half + 1) * 512],
                              ps[pi][:, 0:512], AF.Copy),
                          writes=("o_ot",), inc="s1",
                          waits=[tmm, rot.get(("ot", par))])
                C.psum_free[pi] = te
            tev = P.tok("s1")
            rot[("xf", par)] = tev
            ts = P.op("s", lambda E, b=b, tt=tt, par=par: E.dma_start(
                out=C.out_d[b, tt * 128:(tt + 1) * 128, :],
                in_=ot[:, par * DM:(par + 1) * DM]),
                inc=f"go{par}", inc_n=16, waits=[tev])
            rot[("ot", par)] = ts
    st.close()
    P.dma_wait_all("g", prefixes=("go",))


def emit_postln_add(C, srcname, srcch, b, cs, rot, tready):
    P, ps, sb, Tc = C.P, C.ps, C.sb, C.Tc
    xtl, sgt = C.xtl, C.sgt
    first = True
    for k in range(KM):
        t6 = P.op("t", lambda E, k=k, first=first: E.matmul(
            ps[6][0:1, 0:Tc], sb["onesf"][:, 0:1],
            srcch[:, k * Tc:(k + 1) * Tc], start=first, stop=(k == KM - 1)),
            inc="t1", waits=[tready] + ([C.psum_free.get(6)]
                                        if first else []))
        tsq = P.op("s", lambda E, k=k: E.activation(
            sgt[:, 0:Tc], srcch[:, k * Tc:(k + 1) * Tc], AF.Square),
            writes=(C.sgtname,), inc="s1",
            waits=[tready, rot.get("sgt_user")], clears=(C.sgtname,))
        t7 = P.op("t", lambda E, first=first, k=k: E.matmul(
            ps[7][0:1, 0:Tc], sb["onesf"][:, 0:1], sgt[:, 0:Tc],
            start=first, stop=(k == KM - 1)), inc="t1",
            waits=[tsq] + ([C.psum_free.get(7)] if first else []))
        rot["sgt_user"] = t7
        first = False
    tmm = P.tok("t1")
    t4, t5 = ln_rows(C, ps[6], ps[7], 1.0 / DM, 0, tmm)
    C.psum_free[6] = P.tok("v1")
    C.psum_free[7] = C.psum_free[6]
    tld = {}
    for k in range(4):
        tld[k] = P.op("q", _dma(xtl[:, k * Tc:(k + 1) * Tc],
                                C.ds["resid"][b, k, :, cs]),
                      inc=f"qz{k}", inc_n=16,
                      waits=[rot.get(("xtl", k))])
    for k in range(KM):
        P.op("v", lambda E, k=k: E.tensor_tensor(
            srcch[:, k * Tc:(k + 1) * Tc], srcch[:, k * Tc:(k + 1) * Tc],
            ps[5][:, 0:Tc], ALU.mult),
            reads=(srcname,), writes=(f"plnK{k}",),
            waits=[t4, t5] if k == 0 else ())
    for k in range(KM):
        P.op("v", lambda E, k=k: E.tensor_tensor(
            srcch[:, k * Tc:(k + 1) * Tc], srcch[:, k * Tc:(k + 1) * Tc],
            ps[4][:, 0:Tc], ALU.subtract),
            reads=(f"plnK{k}",), writes=(f"plnK{k}",))
    ta = None
    for k in range(KM):
        par = k % 4
        ta = P.op("v", lambda E, k=k, par=par: E.tensor_tensor(
            srcch[:, k * Tc:(k + 1) * Tc], srcch[:, k * Tc:(k + 1) * Tc],
            xtl[:, par * Tc:(par + 1) * Tc], ALU.add),
            reads=(f"plnK{k}",), writes=(f"plnK{k}",), inc="v1",
            waits=[tld[k]])
        rot[("xtl", par)] = ta
        if k + 4 < KM:
            tld[k + 4] = P.op("q", _dma(xtl[:, par * Tc:(par + 1) * Tc],
                                        C.ds["resid"][b, k + 4, :, cs]),
                              inc=f"qz{par}", inc_n=16, waits=[ta])
    tadd = P.tok("v1")
    C.psum_free[4] = tadd
    C.psum_free[5] = tadd
    tst = P.op("s", lambda E, b=b, cs=cs: E.dma_start(
        out=C.ds["resid"][b, :, :, cs].rearrange("k p t -> p k t"),
        in_=srcch.ap().rearrange("p (k t) -> p k t", k=KM)),
        inc="gr0", inc_n=16, waits=[tadd])
    return tadd, tst


def emit_qdq_pp(C, src, dst, inv_col, s_col, lo, hi, tmp, tmpname,
                reads=(), writes=(), waits=()):
    P = C.P
    M = 12582912.0
    P.op("v", lambda E: E.tensor_scalar(tmp, src, inv_col, M, ALU.mult,
                                        ALU.add),
         reads=reads, writes=(tmpname,), waits=waits)
    P.op("v", lambda E: E.tensor_scalar(tmp, tmp, M, float(lo),
                                        ALU.subtract, ALU.max),
         reads=(tmpname,), writes=(tmpname,))
    return P.op("v", lambda E: E.tensor_scalar(dst, tmp, float(hi), s_col,
                                               ALU.min, ALU.mult),
                reads=(tmpname,), writes=writes, inc="v1")

def emit_qdq_bc(C, src, dst, sinv_b, s_b, lo, hi, tmp, tmpname,
                reads=(), writes=(), waits=(), bcreads=()):
    P = C.P
    M = 12582912.0
    P.op("v", lambda E: E.tensor_tensor(tmp, src, sinv_b, ALU.mult),
         reads=tuple(reads) + tuple(bcreads), writes=(tmpname,), waits=waits)
    P.op("v", lambda E: E.tensor_scalar(tmp, tmp, M, M, ALU.add,
                                        ALU.subtract),
         reads=(tmpname,), writes=(tmpname,))
    P.op("v", lambda E: E.tensor_scalar(tmp, tmp, float(lo), float(hi),
                                        ALU.max, ALU.min),
         reads=(tmpname,), writes=(tmpname,))
    return P.op("v", lambda E: E.tensor_tensor(dst, tmp, s_b, ALU.mult),
                reads=(tmpname,) + tuple(bcreads), writes=writes, inc="v1")


def emit_armax(C, macc_list, scl, derived, ncore):
    P, nc, ds = C.P, C.nc, C.ds
    row = C.ccrow
    tz = P.op("g", lambda E: E.memset(row[0:1, :], 0.0), writes=("ccrow",),
              inc="g1", waits=[C.cc_prev])
    toks = []
    for i, (m, name) in enumerate(macc_list):
        tr = P.op("v", lambda E, m=m, i=i: E.reduce_max(
            C.mrow[:, 2 * i:2 * i + 1], m, AX.X), reads=(name,),
            writes=("mrow",), inc="v1",
            waits=[getattr(C, "mrow_prev", None)])
        tp = P.op("g", lambda E, i=i: E.partition_all_reduce(
            C.mrow[:, 2 * i + 1:2 * i + 2], C.mrow[:, 2 * i:2 * i + 1],
            128, ReduceOp.max),
            reads=("mrow",), writes=("mrow",), inc="g1", waits=[tr])
        tc = P.op("g", lambda E, i=i: E.tensor_copy(
            row[0:1, i:i + 1], C.mrow[0:1, 2 * i + 1:2 * i + 2]),
            reads=("mrow",), writes=("ccrow",), inc="g1",
            waits=[tp, tz])
        toks.append(tc)
    td = P.op("g", _dma(ds["cc_in"][:, :], row[0:1, 0:8]), inc="gcc",
              inc_n=16, waits=toks)
    tcc = P.op("g", lambda E, ncore=ncore: E.collective_compute(
        "AllReduce", ALU.max, replica_groups=[list(range(ncore))],
        ins=[ds["cc_in"].opt()], outs=[ds["cc_out"].opt()]),
        inc="g1", waits=[td])
    tl = P.op("g", _dma(row[0:1, 8:16], ds["cc_out"][:, :]), inc="gcc",
              inc_n=16, waits=[tcc])
    tout = None
    for i, _ in enumerate(macc_list):
        tout = P.op("g", lambda E, i=i: E.partition_broadcast(
            scl[:, i:i + 1], row[0:1, 8 + i:9 + i]),
            reads=("ccrow",), writes=("scl",), inc="g1", waits=[tl])
    tv = None
    for i, (qmax, inv_col, s_col) in enumerate(derived):
        P.op("v", lambda E, i=i, qmax=qmax, s_col=s_col:
             E.tensor_scalar(scl[:, s_col:s_col + 1],
                             scl[:, i:i + 1], 1.0 / qmax, 1e-12,
                             ALU.mult, ALU.max),
             reads=("scl",), writes=("scl",), waits=[tout])
        tv = P.op("v", lambda E, inv_col=inv_col, s_col=s_col:
                  E.reciprocal(scl[:, inv_col:inv_col + 1],
                               scl[:, s_col:s_col + 1]),
                  reads=("scl",), writes=("scl",), inc="v1")
    C.cc_prev = P.tok("g1")
    C.mrow_prev = C.cc_prev
    return tv


def s2_ffn0(C, ncore):
    P, nc, ds, sb, ps, Tc, L = C.P, C.nc, C.ds, C.sb, C.ps, C.Tc, C.L
    st = ExitStack()
    stw = ExitStack()
    def S(name, shape, dt, stack=None):
        return (stack or st).enter_context(nc.sbuf_tensor(name, shape, dt))
    wupg = stw.enter_context(
        nc.sbuf_tensor("f_wupg", [128, KM * HF], BF16, side="right"))
    wgat = stw.enter_context(
        nc.sbuf_tensor("f_wgat", [128, KM * HF], BF16, side="right"))
    xch = S("f_xch", [128, KM * Tc], F32)
    xqb = S("f_xqb", [128, KM * Tc], BF16)
    tile = S("f_tile", [128, 8 * Tc], F32)
    tmp2 = S("f_tmp2", [128, Tc], F32)
    sqb2 = S("f_sqb", [128, Tc], BF16)
    qt2 = S("f_qt2", [128, Tc], F32)
    macc = S("f_macc", [128, 32], F32)
    scl = S("f_scl", [128, 24], F32)
    mrow = S("f_mrow", [128, 4], F32)
    ccrow = S("f_ccrow", [1, 16], F32)
    xtl2 = S("f_xtl", [128, 4 * Tc], F32)
    C.xtl = xtl2
    C.sgt = tmp2
    C.sgtname = "f_tmp2"
    C.mrow, C.ccrow = mrow, ccrow
    C.scl = scl
    C.cc_prev = None
    C.qtmp2, C.qtmp2name = qt2, "f_qt2"
    rot = {}
    NCH = L // Tc
    XCHN = tuple(f"f_xch{k}" for k in range(KM))

    def absred(slot, src_ap, reads=(), waits=()):
        return P.op("v", lambda E, slot=slot, src_ap=src_ap: E.tensor_reduce(
            macc[:, slot:slot + 1], src_ap, AX.X, ALU.max,
            apply_absolute_value=True),
            reads=reads, writes=(f"f_mc{slot}",), inc="v1", waits=waits)

    def fold(s0, n, acc, accname, waits=()):
        P.op("v", lambda E, s0=s0, n=n: E.tensor_reduce(
            macc[:, 24:25], macc[:, s0:s0 + n], AX.X, ALU.max),
            reads=tuple(f"f_mc{s0+i}" for i in range(n)),
            writes=("f_mfold",), waits=waits)
        return P.op("v", lambda E, acc=acc: E.tensor_tensor(
            macc[:, acc:acc + 1], macc[:, acc:acc + 1], macc[:, 24:25],
            ALU.max),
            reads=("f_mfold", accname), writes=(accname,), inc="v1")

    twu = P.op("q", _dma(wupg.ap().rearrange("p (k c) -> p k c", c=HF),
                         ds["w_upT"].rearrange("(k p) c -> p k c", p=128)),
               inc="qw0", inc_n=16)
    twg = P.op("q", _dma(wgat.ap().rearrange("p (k c) -> p k c", c=HF),
                         ds["w_gateT"].rearrange("(k p) c -> p k c", p=128)),
               inc="qw1", inc_n=16)

    P.op("v", lambda E: E.memset(macc[:, 16:17], 0.0), writes=("f_ma0",))
    for b in range(BPC):
        for c in range(NCH):
            cs = slice(c * Tc, (c + 1) * Tc)
            first = True
            for k in range(KM):
                par = k % 2
                tl = P.op("q", _dma(xtl2[:, par * Tc:(par + 1) * Tc],
                                    ds["resid"][b, k, :, cs]),
                          inc=f"qt{par}", inc_n=16,
                          waits=[rot.get(("xtlA", par)),
                                 rot.get(("xtlA2", par))])
                tc1 = P.op("v", lambda E, k=k, par=par: E.tensor_copy(
                    xqb[:, k * Tc:(k + 1) * Tc],
                    xtl2[:, par * Tc:(par + 1) * Tc]),
                    writes=(f"f_xqb{k}",), inc="v1",
                    waits=[tl] + ([rot.get("xqb_user")] if k == 0 else []),
                    clears=tuple(f"f_xqb{i}" for i in range(KM))
                    if k == 0 else ())
                tc2 = P.op("s", lambda E, par=par: E.activation(
                    sqb2[:], xtl2[:, par * Tc:(par + 1) * Tc], AF.Square),
                    writes=("f_sqb",), inc="s1",
                    waits=[tl, rot.get("sqb_user")], clears=("f_sqb",))
                tc3 = P.op("v", lambda E, k=k, par=par: E.tensor_copy(
                    xch[:, k * Tc:(k + 1) * Tc],
                    xtl2[:, par * Tc:(par + 1) * Tc]),
                    writes=(f"f_xch{k}",), inc="v1",
                    waits=[tl] + ([rot.get("xch_user")] if k == 0 else []),
                    clears=XCHN + ("f_xchW",) if k == 0 else ())
                rot[("xtlA", par)] = tc3
                rot[("xtlA2", par)] = tc2
                P.op("t", lambda E, k=k, first=first: E.matmul(
                    ps[6][0:1, 0:Tc], sb["onesb"][:, 0:1],
                    xqb[:, k * Tc:(k + 1) * Tc], start=first,
                    stop=(k == KM - 1)),
                    inc="t1", waits=[tc1] + ([C.psum_free.get(6)]
                                             if first else []))
                t7 = P.op("t", lambda E, first=first, k=k: E.matmul(
                    ps[7][0:1, 0:Tc], sb["onesb"][:, 0:1], sqb2[:],
                    start=first, stop=(k == KM - 1)),
                    inc="t1", waits=[tc2] + ([C.psum_free.get(7)]
                                             if first else []))
                rot["sqb_user"] = t7
                first = False
            tmm = P.tok("t1")
            t4, t5 = ln_rows(C, ps[6], ps[7], 1.0 / DM, 0, tmm)
            C.psum_free[6] = P.tok("v1")
            C.psum_free[7] = C.psum_free[6]
            for k in range(KM):
                P.op("v", lambda E, k=k: E.tensor_tensor(
                    xch[:, k * Tc:(k + 1) * Tc], xch[:, k * Tc:(k + 1) * Tc],
                    ps[5][:, 0:Tc], ALU.mult),
                    reads=(f"f_xch{k}",), writes=(f"f_xch{k}",),
                    waits=[t4, t5] if k == 0 else ())
            tn = None
            for k in range(KM):
                tn = P.op("v", lambda E, k=k: E.tensor_tensor(
                    xch[:, k * Tc:(k + 1) * Tc], xch[:, k * Tc:(k + 1) * Tc],
                    ps[4][:, 0:Tc], ALU.subtract),
                    reads=(f"f_xch{k}",), writes=(f"f_xch{k}",),
                    inc="v1" if k == KM - 1 else None)
            for k in range(KM):
                absred(k, xch[:, k * Tc:(k + 1) * Tc],
                       reads=(f"f_xch{k}",))
            rot["xchW_user"] = fold(0, KM, 16, "f_ma0")
            C.psum_free[4] = tn
            C.psum_free[5] = tn
            tst = P.op("s", lambda E, b=b, cs=cs: E.dma_start(
                out=ds["xln1"][b, :, :, cs].rearrange("k p t -> p k t"),
                in_=xch.ap().rearrange("p (k t) -> p k t", k=KM)),
                inc="gx0", inc_n=16, waits=[tn])
            rot["xch_user"] = tst
            rot["xqb_user"] = tmm
    tm0 = emit_armax(C, [(macc[:, 16:17], "f_ma0")], scl,
                     [(32767.0, 8, 9)], ncore)

    P.dma_wait_all("q", prefixes=("gx", "gu", "gv"))
    P.op("v", lambda E: E.memset(macc[:, 17:19], 0.0),
         writes=("f_ma1", "f_ma2"))
    M = 12582912.0
    for b in range(BPC):
        for c in range(NCH):
            cs = slice(c * Tc, (c + 1) * Tc)
            lds = []
            for k in range(KM):
                lds.append(P.op("q", _dma(xch[:, k * Tc:(k + 1) * Tc],
                                          ds["xln1"][b, k, :, cs]),
                           inc=f"qt{k % 4}", inc_n=16,
                           waits=[rot.get("xchW_user")]))
            P.op("v", lambda E: E.tensor_scalar(
                xch[:], xch[:], scl[:, 8:9], M, ALU.mult, ALU.add),
                reads=("scl",), writes=("f_xchW",),
                waits=lds + [tm0], clears=XCHN + ("f_xchW",))
            P.op("v", lambda E: E.tensor_scalar(
                xch[:], xch[:], M, -32768.0, ALU.subtract, ALU.max),
                reads=("f_xchW",), writes=("f_xchW",))
            tqd = P.op("v", lambda E: E.tensor_scalar(
                xqb[:], xch[:], 32767.0, scl[:, 9:10], ALU.min, ALU.mult),
                reads=("f_xchW",), writes=("f_xqbW",), inc="v1",
                waits=[rot.get("xqb_user")],
                clears=tuple(f"f_xqb{i}" for i in range(KM)) + ("f_xqbW",))
            rot["xchW_user"] = tqd
            for which, (wst, tw, slot0, acc, accname, spill, sem) in \
                    enumerate([(wupg, twu, 0, 17, "f_ma1", "up_raw", "gu"),
                               (wgat, twg, 8, 18, "f_ma2", "gate_raw",
                                "gv")]):
                for f in range(KF):
                    par, pi = f % 2, (2 * which + f % 2) % 4
                    tmm = None
                    for k in range(KM):
                        tmm = P.op("t", lambda E, k=k, f=f, pi=pi, wst=wst:
                                   E.matmul(
                                       ps[pi][:, 0:Tc],
                                       wst[:, k * HF + f * 128:
                                           k * HF + (f + 1) * 128],
                                       xqb[:, k * Tc:(k + 1) * Tc],
                                       start=(k == 0), stop=(k == KM - 1)),
                                   waits=([tqd, tw, C.psum_free.get(pi),
                                           rot.get(("psr", pi))]
                                          if k == 0 else []),
                                   inc="t1" if k == KM - 1 else None)
                    te = P.op("s", lambda E, pi=pi, which=which, par=par:
                              E.activation(
                                  tile[:, (2 * which + par) * Tc:
                                       (2 * which + par + 1) * Tc],
                                  ps[pi][:, 0:Tc], AF.Copy),
                              writes=(f"f_tile{2 * which + par}",), inc="s1",
                              waits=[tmm, rot.get(("tile", which, par))],
                              clears=(f"f_tile{2 * which + par}",))
                    trd = absred(slot0 + f % 8, ps[pi][:, 0:Tc],
                                 waits=[tmm, te])
                    rot[("psr", pi)] = trd
                    C.psum_free[pi] = te
                    if f % 8 == 7:
                        fold(slot0, 8, acc, accname)
                    tsp = P.op("s", lambda E, b=b, f=f, cs=cs, which=which,
                               par=par, spill=spill: E.dma_start(
                                   out=ds[spill][b, f, :, cs],
                                   in_=tile[:, (2 * which + par) * Tc:
                                            (2 * which + par + 1) * Tc]),
                               inc=f"{sem}{par}", inc_n=16, waits=[te])
                    rot[("tile", which, par)] = tsp
            rot["xqb_user"] = P.tok("t1")
    P.barrier()
    stw.close()
    tm12 = emit_armax(C, [(macc[:, 17:18], "f_ma1"),
                          (macc[:, 18:19], "f_ma2")], scl,
                      [(32767.0, 10, 11), (32767.0, 12, 13)], ncore)

    P.dma_wait_all("q", prefixes=("gx", "gu", "gv"))
    P.op("v", lambda E: E.memset(macc[:, 19:20], 0.0), writes=("f_ma3",))
    TAN = ("f_tileA",)
    TBN = ("f_tileB",)
    for b in range(BPC):
        for c in range(NCH):
            cs = slice(c * Tc, (c + 1) * Tc)
            for g in range(KF // 4):
                f0 = 4 * g
                lds = []
                for j in range(4):
                    lds.append(P.op(
                        "q", lambda E, j=j, f0=f0, b=b, cs=cs: E.dma_start(
                            out=tile[:, j * Tc:(j + 1) * Tc],
                            in_=ds["gate_raw"][b, f0 + j, :, cs]),
                        inc=f"qt{j}", inc_n=16,
                        waits=[rot.get("t3a")]))
                tq0 = P.op("v", lambda E: E.tensor_scalar(
                    tile[:, 4 * Tc:8 * Tc], tile[:, 0:4 * Tc],
                    scl[:, 12:13], M, ALU.mult, ALU.add),
                    reads=("scl",), writes=TBN,
                    waits=lds + [tm12] + list(rot.get("t3b", ())),
                    clears=TAN + TBN, inc="v1")
                rot["t3a"] = tq0
                P.op("v", lambda E: E.tensor_scalar(
                    tile[:, 4 * Tc:8 * Tc], tile[:, 4 * Tc:8 * Tc], M,
                    -32768.0, ALU.subtract, ALU.max),
                    reads=TBN, writes=TBN)
                tgq = P.op("v", lambda E: E.tensor_scalar(
                    tile[:, 4 * Tc:8 * Tc], tile[:, 4 * Tc:8 * Tc],
                    32767.0, scl[:, 13:14], ALU.min, ALU.mult),
                    reads=TBN, writes=TBN, inc="v1")
                tsg = P.op("s", lambda E: E.activation(
                    xch[:, 0:4 * Tc], tile[:, 4 * Tc:8 * Tc], AF.Sigmoid),
                    writes=("f_sig",), inc="s1",
                    waits=[tgq, rot.get("sig_user")], clears=("f_sig",))
                P.op("v", lambda E: E.tensor_scalar(
                    xch[:, 0:4 * Tc], xch[:, 0:4 * Tc], 65536.0, M,
                    ALU.mult, ALU.add),
                    reads=(), writes=("f_sig",), waits=[tsg],
                    clears=("f_xchW",) + XCHN)
                P.op("v", lambda E: E.tensor_scalar(
                    xch[:, 0:4 * Tc], xch[:, 0:4 * Tc], M, 0.0,
                    ALU.subtract, ALU.max),
                    reads=("f_sig",), writes=("f_sig",))
                P.op("v", lambda E: E.tensor_scalar(
                    xch[:, 0:4 * Tc], xch[:, 0:4 * Tc], 65535.0,
                    1.0 / 65536.0, ALU.min, ALU.mult),
                    reads=("f_sig",), writes=("f_sig",))
                tg2 = P.op("v", lambda E: E.tensor_tensor(
                    tile[:, 4 * Tc:8 * Tc], tile[:, 4 * Tc:8 * Tc],
                    xch[:, 0:4 * Tc], ALU.mult),
                    reads=("f_sig",) + TBN, writes=TBN, inc="v1")
                rot["sig_user"] = tg2
                trd = None
                for j in range(4):
                    trd = absred((4 * g + j) % 8,
                                 tile[:, (4 + j) * Tc:(5 + j) * Tc],
                                 reads=TBN)
                if g % 2 == 1:
                    fold(0, 8, 19, "f_ma3")
                sps = []
                for j in range(4):
                    sps.append(P.op(
                        "s", lambda E, j=j, f0=f0, b=b, cs=cs: E.dma_start(
                            out=ds["g2pre"][b, f0 + j, :, cs],
                            in_=tile[:, (4 + j) * Tc:(5 + j) * Tc]),
                        inc=f"gu{j % 2}", inc_n=16, waits=[tg2]))
                rot["t3b"] = sps + [trd]
    tm3 = emit_armax(C, [(macc[:, 19:20], "f_ma3")], scl,
                     [(32767.0, 14, 15)], ncore)

    P.dma_wait_all("q", prefixes=("gx", "gu", "gv"))
    P.op("v", lambda E: E.memset(macc[:, 20:21], 0.0), writes=("f_ma4",))
    for b in range(BPC):
        for c in range(NCH):
            cs = slice(c * Tc, (c + 1) * Tc)
            for g in range(KF // 4):
                f0 = 4 * g
                ldA, ldB = [], []
                for j in range(4):
                    ldA.append(P.op(
                        "q", lambda E, j=j, f0=f0, b=b, cs=cs: E.dma_start(
                            out=tile[:, j * Tc:(j + 1) * Tc],
                            in_=ds["g2pre"][b, f0 + j, :, cs]),
                        inc=f"qt{j}", inc_n=16,
                        waits=list(rot.get("t4a", ())) + [rot.get("t3a")]))
                    ldB.append(P.op(
                        "q", lambda E, j=j, f0=f0, b=b, cs=cs: E.dma_start(
                            out=tile[:, (4 + j) * Tc:(5 + j) * Tc],
                            in_=ds["up_raw"][b, f0 + j, :, cs]),
                        inc=f"qo{j}", inc_n=16,
                        waits=[rot.get("t4b")] + list(rot.get("t3b", ()))))
                P.op("v", lambda E: E.tensor_scalar(
                    tile[:, 0:4 * Tc], tile[:, 0:4 * Tc],
                    scl[:, 14:15], M, ALU.mult, ALU.add),
                    reads=("scl",), writes=TAN, waits=ldA + [tm3],
                    clears=TAN + TBN)
                P.op("v", lambda E: E.tensor_scalar(
                    tile[:, 0:4 * Tc], tile[:, 0:4 * Tc], M, -32768.0,
                    ALU.subtract, ALU.max),
                    reads=TAN, writes=TAN)
                P.op("v", lambda E: E.tensor_scalar(
                    tile[:, 0:4 * Tc], tile[:, 0:4 * Tc], 32767.0,
                    scl[:, 15:16], ALU.min, ALU.mult),
                    reads=TAN, writes=TAN)
                P.op("v", lambda E: E.tensor_scalar(
                    tile[:, 4 * Tc:8 * Tc], tile[:, 4 * Tc:8 * Tc],
                    scl[:, 10:11], M, ALU.mult, ALU.add),
                    reads=TBN, writes=TBN, waits=ldB)
                P.op("v", lambda E: E.tensor_scalar(
                    tile[:, 4 * Tc:8 * Tc], tile[:, 4 * Tc:8 * Tc], M,
                    -32768.0, ALU.subtract, ALU.max),
                    reads=TBN, writes=TBN)
                P.op("v", lambda E: E.tensor_scalar(
                    tile[:, 4 * Tc:8 * Tc], tile[:, 4 * Tc:8 * Tc],
                    32767.0, scl[:, 11:12], ALU.min, ALU.mult),
                    reads=TBN, writes=TBN)
                to = P.op("v", lambda E: E.tensor_tensor(
                    tile[:, 0:4 * Tc], tile[:, 0:4 * Tc],
                    tile[:, 4 * Tc:8 * Tc], ALU.mult),
                    reads=TAN + TBN, writes=TAN, inc="v1")
                rot["t4b"] = to
                trd = None
                for j in range(4):
                    trd = absred((4 * g + j) % 8,
                                 tile[:, j * Tc:(j + 1) * Tc], reads=TAN)
                if g % 2 == 1:
                    fold(0, 8, 20, "f_ma4")
                sps = []
                for j in range(4):
                    sps.append(P.op(
                        "s", lambda E, j=j, f0=f0, b=b, cs=cs: E.dma_start(
                            out=ds["o_pre"][b, f0 + j, :, cs],
                            in_=tile[:, j * Tc:(j + 1) * Tc]),
                        inc=f"gu{j % 2}", inc_n=16, waits=[to]))
                rot["t4a"] = sps + [trd]
    tm4 = emit_armax(C, [(macc[:, 20:21], "f_ma4")], scl,
                     [(32767.0, 16, 17)], ncore)

    P.dma_wait_all("q", prefixes=("gx", "gu", "gv"))
    och = S("f_och", [128, KF * Tc], BF16)
    dnf = S("f_dnf", [128, 2 * KF * 128], BF16)
    dch = S("f_dch", [128, KM * Tc], F32)
    for b in range(BPC):
        for c in range(NCH):
            cs = slice(c * Tc, (c + 1) * Tc)
            for g in range(KF // 4):
                f0 = 4 * g
                lds = []
                for j in range(4):
                    lds.append(P.op(
                        "q", lambda E, j=j, f0=f0, b=b, cs=cs: E.dma_start(
                            out=tile[:, j * Tc:(j + 1) * Tc],
                            in_=ds["o_pre"][b, f0 + j, :, cs]),
                        inc=f"qt{j}", inc_n=16,
                        waits=[rot.get("t5a")] + list(rot.get("t4a", ()))))
                t5a = P.op("v", lambda E: E.tensor_scalar(
                    tile[:, 4 * Tc:8 * Tc], tile[:, 0:4 * Tc],
                    scl[:, 16:17], M, ALU.mult, ALU.add),
                    reads=("scl",), writes=TBN,
                    waits=lds + [tm4, rot.get("t5b"), rot.get("t4b")],
                    clears=TAN + TBN, inc="v1")
                rot["t5a"] = t5a
                P.op("v", lambda E: E.tensor_scalar(
                    tile[:, 4 * Tc:8 * Tc], tile[:, 4 * Tc:8 * Tc], M,
                    -32768.0, ALU.subtract, ALU.max),
                    reads=TBN, writes=TBN)
                tq = P.op("v", lambda E, f0=f0: E.tensor_scalar(
                    och[:, f0 * Tc:(f0 + 4) * Tc], tile[:, 4 * Tc:8 * Tc],
                    32767.0, scl[:, 17:18], ALU.min, ALU.mult),
                    reads=TBN, writes=("f_och",), inc="v1",
                    waits=[rot.get("och_user")] if g == 0 else (),
                    clears=("f_och",) if g == 0 else ())
                rot["t5b"] = tq
            tqd = P.tok("v1")
            for f in range(KM):
                par, pi = f % 2, f % 2
                tw = P.op("q", lambda E, f=f, par=par: E.dma_start(
                    out=dnf[:, par * KF * 128:(par + 1) * KF * 128],
                    in_=ds["w_downT"][f]),
                    inc=f"qp{par}", inc_n=16, waits=[rot.get(("dnf", par))])
                tmm = None
                for k in range(KF):
                    tmm = P.op("t", lambda E, k=k, par=par, pi=pi: E.matmul(
                        ps[pi][:, 0:Tc],
                        dnf[:, par * KF * 128 + k * 128:
                            par * KF * 128 + (k + 1) * 128],
                        och[:, k * Tc:(k + 1) * Tc],
                        start=(k == 0), stop=(k == KF - 1)),
                        waits=([tw, tqd, C.psum_free.get(pi)]
                               if k == 0 else []),
                        inc="t1" if k == KF - 1 else None)
                rot[("dnf", par)] = tmm
                te = P.op("s", lambda E, f=f, pi=pi: E.activation(
                    dch[:, f * Tc:(f + 1) * Tc], ps[pi][:, 0:Tc], AF.Copy),
                    writes=("f_dch",), inc="s1",
                    waits=[tmm] + ([rot.get("dch_user")] if f == 0 else []),
                    clears=("f_dch",) if f == 0 else ())
                C.psum_free[pi] = te
            tdn = P.tok("s1")
            rot["och_user"] = P.tok("t1")
            tadd, tst = emit_postln_add(C, "f_dch", dch, b, cs, rot, tdn)
            rot["dch_user"] = tst
    st.close()
    return P.tok("gr0")

def s3_attn(C):
    P, nc, ds, sb, ps, Tc, L = C.P, C.nc, C.ds, C.sb, C.ps, C.Tc, C.L
    starts = _win_starts(L)
    stP = ExitStack()
    _uid = [0]
    def S(name, shape, dt, stack=None):
        _uid[0] += 1
        return (stack or stP).enter_context(
            nc.sbuf_tensor(f"{name}_{_uid[0]}", shape, dt))
    qb = S("a_qb", [128, KM * L], BF16)
    kb = S("a_kb", [128, KM * L], BF16)
    vT = S("a_vT", [128, (L // 128) * DM], BF16)
    bqt = S("a_bqt", [128, 3 * KM], F32)
    bvrow = S("a_bvrow", [1, DM], F32)
    bvrep = S("a_bvrep", [128, DM], F32)
    cinv = S("a_cinv", [128, L], BF16)
    srow = S("a_srow", [128, 8], F32)
    xtl3 = S("a_xtl", [128, 4 * Tc], F32)
    C.xtl = xtl3
    sgt4 = S("a_sgt", [128, Tc], F32)
    C.sgt, C.sgtname = sgt4, "a_sgt"
    rot = {}
    NCH = L // Tc

    tb1 = P.op("q", _dma_nc(bqt[:, 0:KM],
                         C.din["L1_attn_bq"].rearrange("(k p) -> p k", p=128)),
               inc="qc", inc_n=16)
    tb2 = P.op("q", _dma_nc(bqt[:, KM:2 * KM],
                         C.din["L1_attn_bk"].rearrange("(k p) -> p k", p=128)),
               inc="qc", inc_n=16)
    tb3 = P.op("q", _dma_nc(bqt[:, 2 * KM:3 * KM],
                         C.din["L1_attn_bo"].rearrange("(k p) -> p k", p=128)),
               inc="qc", inc_n=16)
    tb4 = P.op("q", _dma(bvrow[0:1, :],
                         C.din["L1_attn_bv"].rearrange("(o c) -> o c", o=1)),
               inc="qc", inc_n=16)
    tm = P.op("t", lambda E: E.matmul(ps[4][:, 0:512], sb["onesf"][0:1, 0:128],
                                      bvrow[0:1, 0:512], start=True,
                                      stop=True),
              inc="t1", waits=[tb4, C.psum_free.get(4)])
    tm2 = P.op("t", lambda E: E.matmul(ps[5][:, 0:512],
                                       sb["onesf"][0:1, 0:128],
                                       bvrow[0:1, 512:1024], start=True,
                                       stop=True),
               inc="t1", waits=[tb4, C.psum_free.get(5)])
    te1 = P.op("s", lambda E: E.activation(bvrep[:, 0:512], ps[4][:, 0:512],
                                           AF.Copy),
               writes=("a_bvrep",), inc="s1", waits=[tm])
    te2 = P.op("s", lambda E: E.activation(bvrep[:, 512:1024],
                                           ps[5][:, 0:512], AF.Copy),
               writes=("a_bvrep",), inc="s1", waits=[tm2])
    C.psum_free[4] = te1
    C.psum_free[5] = te2
    counts = np.zeros(L, np.int32)
    for st0 in starts:
        counts[st0:st0 + 256] += 1
    runs, cur, r0 = [], counts[0], 0
    for i in range(1, L):
        if counts[i] != cur:
            runs.append((r0, i, cur)); r0, cur = i, counts[i]
    runs.append((r0, L, cur))
    tcv = None
    for (a, b_, cval) in runs:
        tcv = P.op("v", lambda E, a=a, b_=b_, cval=cval: E.memset(
            cinv[:, a:b_], 1.0 / float(cval)), writes=("a_cinv",), inc="v1")

    def _one_batch(b):
        stA = ExitStack()
        xln2 = S("a_xln2", [128, KM * L], BF16, stA)
        wqb = S("a_wqb", [128, KM * DM], BF16, stA)
        sqb3 = S("a_sqb", [128, Tc], BF16, stA)
        for c in range(NCH):
            cs = slice(c * Tc, (c + 1) * Tc)
            first = True
            for k in range(KM):
                par = k % 2
                tl = P.op("q", _dma(xtl3[:, par * Tc:(par + 1) * Tc],
                                    ds["resid"][b, k, :, cs]),
                          inc=f"qt{par}", inc_n=16,
                          waits=[rot.get(("xtl", par)),
                                 rot.get(("xtl2", par))])
                tc1 = P.op("v", lambda E, k=k, c=c, par=par: E.tensor_copy(
                    xln2[:, k * L + c * Tc:k * L + (c + 1) * Tc],
                    xtl3[:, par * Tc:(par + 1) * Tc]),
                    writes=("a_xln2",), inc="v1",
                    waits=[tl, rot.get("xln2_user")])
                tc2 = P.op("s", lambda E, par=par: E.activation(
                    sqb3[:], xtl3[:, par * Tc:(par + 1) * Tc], AF.Square),
                    writes=("a_sqb",), inc="s1",
                    waits=[tl, rot.get("sqb_user")])
                rot[("xtl", par)] = tc1
                rot[("xtl2", par)] = tc2
                t6 = P.op("t", lambda E, k=k, c=c, first=first: E.matmul(
                    ps[6][0:1, 0:Tc], sb["onesb"][:, 0:1],
                    xln2[:, k * L + c * Tc:k * L + (c + 1) * Tc],
                    start=first, stop=(k == KM - 1)),
                    inc="t1", waits=[tc1] + ([C.psum_free.get(6)]
                                             if first else []))
                t7 = P.op("t", lambda E, first=first, k=k: E.matmul(
                    ps[7][0:1, 0:Tc], sb["onesb"][:, 0:1], sqb3[:],
                    start=first, stop=(k == KM - 1)),
                    inc="t1", waits=[tc2] + ([C.psum_free.get(7)]
                                             if first else []))
                rot["sqb_user"] = t7
                first = False
            tmm = P.tok("t1")
            t4, t5 = ln_rows(C, ps[6], ps[7], 1.0 / DM, 0, tmm)
            C.psum_free[6] = P.tok("v1")
            C.psum_free[7] = C.psum_free[6]
            tn = None
            for k in range(KM):
                P.op("v", lambda E, k=k, c=c: E.tensor_tensor(
                    xln2[:, k * L + c * Tc:k * L + (c + 1) * Tc],
                    xln2[:, k * L + c * Tc:k * L + (c + 1) * Tc],
                    ps[5][:, 0:Tc], ALU.mult),
                    reads=("a_xln2",), writes=("a_xln2",), waits=[t4, t5])
                tn = P.op("v", lambda E, k=k, c=c: E.tensor_tensor(
                    xln2[:, k * L + c * Tc:k * L + (c + 1) * Tc],
                    xln2[:, k * L + c * Tc:k * L + (c + 1) * Tc],
                    ps[4][:, 0:Tc], ALU.subtract),
                    reads=("a_xln2",), writes=("a_xln2",),
                    inc="v1" if k == KM - 1 else None)
            C.psum_free[4] = tn
            C.psum_free[5] = tn
            rot["xln2_user"] = tn
        tln = P.tok("v1")

        for which, (wname, dst, dstname, bcol) in enumerate(
                [("w_q", qb, "a_qb", 0), ("w_k", kb, "a_kb", KM)]):
            tw = P.op("q", lambda E, wname=wname: E.dma_start(
                out=wqb.ap().rearrange("p (k c) -> p k c", c=DM),
                in_=ds[wname].rearrange("(k p) c -> p k c", p=128)),
                inc="qw0", inc_n=16, waits=[rot.get("wqb_user")])
            for f in range(KM):
                for c in range(NCH):
                    pi = (f * NCH + c) % 4
                    tmm = None
                    for k in range(KM):
                        tmm = P.op("t", lambda E, k=k, f=f, c=c, pi=pi:
                                   E.matmul(
                                       ps[pi][:, 0:Tc],
                                       wqb[:, k * DM + f * 128:
                                           k * DM + (f + 1) * 128],
                                       xln2[:, k * L + c * Tc:
                                            k * L + (c + 1) * Tc],
                                       start=(k == 0), stop=(k == KM - 1)),
                                   waits=([tw, tln, C.psum_free.get(pi)]
                                          if k == 0 else []),
                                   inc="t1" if k == KM - 1 else None)
                    te = P.op("v", lambda E, f=f, c=c, pi=pi, dst=dst,
                              bcol=bcol: E.tensor_scalar(
                                  dst[:, f * L + c * Tc:f * L + (c + 1) * Tc],
                                  ps[pi][:, 0:Tc],
                                  bqt[:, bcol + f:bcol + f + 1], None,
                                  ALU.add),
                              writes=(dstname,), inc="v1",
                              waits=[tmm, rot.get((dstname, "user"))])
                    C.psum_free[pi] = te
            rot["wqb_user"] = P.tok("t1")
        tw = P.op("q", lambda E: E.dma_start(
            out=wqb.ap().rearrange("p (k c) -> p k c", c=DM),
            in_=ds["w_v"].rearrange("(k p) c -> p k c", p=128)),
            inc="qw0", inc_n=16, waits=[rot.get("wqb_user")])
        for tt in range(L // 128):
            for h in range(2):
                pi = (tt * 2 + h) % 4
                tmm = None
                for k in range(KM):
                    tmm = P.op("t", lambda E, k=k, tt=tt, h=h, pi=pi:
                               E.matmul(
                                   ps[pi][:, 0:512],
                                   xln2[:, k * L + tt * 128:
                                        k * L + (tt + 1) * 128],
                                   wqb[:, k * DM + h * 512:
                                       k * DM + (h + 1) * 512],
                                   start=(k == 0), stop=(k == KM - 1)),
                               waits=([tw, tln, C.psum_free.get(pi)]
                                      if k == 0 else []),
                               inc="t1" if k == KM - 1 else None)
                te = P.op("v", lambda E, tt=tt, h=h, pi=pi: E.tensor_tensor(
                    vT[:, tt * DM + h * 512:tt * DM + (h + 1) * 512],
                    ps[pi][:, 0:512], bvrep[:, h * 512:(h + 1) * 512],
                    ALU.add),
                    reads=("a_bvrep",), writes=("a_vT",), inc="v1",
                    waits=[tmm, rot.get("vT_user")])
                C.psum_free[pi] = te
        tqkv = P.tok("v1")
        stA.close()
        P.barrier()

        stB = ExitStack()
        aout = S("a_aout", [128, KM * L], BF16, stB)
        pf = S("a_pf", [128, 256], F32, stB)
        pnb = S("a_pnb", [128, 256], BF16, stB)
        pT = S("a_pT", [128, 2 * 256], BF16, stB)
        aobf = S("a_aobf", [128, KM * Tc], BF16, stB)
        woch = S("a_woch", [128, KM * Tc], F32, stB)
        wo_res = S("a_wores", [128, KM * 128], BF16, stB)
        tz = None
        for k in range(KM):
            tz = P.op("v", lambda E, k=k: E.memset(
                aout[:, k * L:(k + 1) * L], 0.0), writes=("a_aout",),
                inc="v1" if k == KM - 1 else None)
        for st0 in starts:
            for qt in range(2):
                pi = qt
                tmm = None
                for k in range(KM):
                    tmm = P.op("t", lambda E, k=k, st0=st0, qt=qt, pi=pi:
                               E.matmul(
                                   ps[pi][:, 0:256],
                                   qb[:, k * L + st0 + qt * 128:
                                      k * L + st0 + (qt + 1) * 128],
                                   kb[:, k * L + st0:k * L + st0 + 256],
                                   start=(k == 0), stop=(k == KM - 1)),
                               waits=([tqkv, C.psum_free.get(pi)]
                                      if k == 0 else []),
                               inc="t1" if k == KM - 1 else None)
                tr = P.op("v", lambda E, pi=pi: E.reduce_max(
                    srow[:, 0:1], ps[pi][:, 0:256], AX.X, negate=True),
                    writes=("a_srow",), waits=[tmm,
                                               rot.get("srow_user")])
                tsc = P.op("v", lambda E: E.tensor_scalar(
                    srow[:, 1:2], srow[:, 0:1], 0.125, None, ALU.mult),
                    reads=("a_srow",), writes=("a_srow",), inc="v1")
                tex = P.op("s", lambda E, pi=pi: E.activation(
                    pf[:], ps[pi][:, 0:256], AF.Exp, scale=0.125,
                    bias=srow[:, 1:2], accum_out=srow[:, 2:3]),
                    reads=("a_srow",), writes=("a_pf", "a_srow"), inc="s1",
                    waits=[tsc, rot.get("pf_user")])
                trc = P.op("v", lambda E: E.reciprocal(srow[:, 3:4],
                                                       srow[:, 2:3]),
                           reads=("a_srow",), writes=("a_srow",),
                           waits=[tex])
                tpn = P.op("v", lambda E: E.tensor_scalar(
                    pnb[:], pf[:], srow[:, 3:4], None, ALU.mult),
                    reads=("a_pf", "a_srow"), writes=("a_pnb",), inc="v1",
                    waits=[rot.get("pnb_user")])
                rot["srow_user"] = tpn
                rot["pf_user"] = tpn
                C.psum_free[pi] = tpn
                ttr = None
                for j in range(2):
                    ttr = P.op("t", lambda E, j=j, qt=qt: E.transpose(
                        ps[2].ap().bitcast(BF16)[0:128, j * 128:(j + 1) * 128],
                        pnb[:, j * 128:(j + 1) * 128], sb["idb"][:]),
                        waits=([tpn, C.psum_free.get(2)] if j == 0 else []),
                        inc="t1" if j == 1 else None)
                tpe = P.op("s", lambda E, qt=qt: E.activation(
                    pT.ap().bitcast(F32)
                    [:, (0 * 256 + qt * 128) // 2:(0 * 256 + qt * 128) // 2 + 64],
                    ps[2][:, 0:64], AF.Copy),
                    writes=("a_pT",), inc="s1",
                    waits=[ttr, rot.get("pT_user")])
                tpe2 = P.op("s", lambda E, qt=qt: E.activation(
                    pT.ap().bitcast(F32)
                    [:, (1 * 256 + qt * 128) // 2:(1 * 256 + qt * 128) // 2 + 64],
                    ps[2][:, 64:128], AF.Copy),
                    writes=("a_pT",), inc="s1")
                C.psum_free[2] = tpe2
            tpT = P.tok("s1")
            rot["pT_user"] = None
            for f in range(KM):
                pi = 4 + f % 2
                tmm = None
                for kt in range(2):
                    tmm = P.op("t", lambda E, kt=kt, f=f, st0=st0, pi=pi:
                               E.matmul(
                                   ps[pi][:, 0:256],
                                   vT[:, (st0 // 128 + kt) * DM + f * 128:
                                      (st0 // 128 + kt) * DM + (f + 1) * 128],
                                   pT[:, kt * 256:(kt + 1) * 256],
                                   start=(kt == 0), stop=(kt == 1)),
                               waits=([tpT, C.psum_free.get(pi)]
                                      if kt == 0 else []),
                               inc="t1" if kt == 1 else None)
                ta = P.op("v", lambda E, f=f, st0=st0, pi=pi: E.tensor_tensor(
                    aout[:, f * L + st0:f * L + st0 + 256],
                    ps[pi][:, 0:256],
                    aout[:, f * L + st0:f * L + st0 + 256], ALU.add),
                    reads=("a_aout",), writes=("a_aout",), inc="v1",
                    waits=[tmm, tz])
                C.psum_free[pi] = ta
            rot["pT_user"] = P.tok("t1")
        taw = P.tok("v1")
        if C.dbg_aout:
            P.op("s", lambda E, b=b: E.dma_start(
                out=C.out_d[b].rearrange("(a p) d -> p a d", p=128),
                in_=aout.ap().rearrange("p (a d) -> p a d", d=DM)),
                inc=f"go{b}", inc_n=16, waits=[taw])

        for c in range(NCH):
            cs = slice(c * Tc, (c + 1) * Tc)
            tcb = None
            for k in range(KM):
                tcb = P.op("v", lambda E, k=k, c=c, cs=cs: E.tensor_tensor(
                    aobf[:, k * Tc:(k + 1) * Tc],
                    aout[:, k * L + c * Tc:k * L + (c + 1) * Tc],
                    cinv[:, cs], ALU.mult),
                    reads=("a_aout", "a_cinv"), writes=("a_aobf",),
                    waits=[taw] + ([rot.get("aobf_user")] if k == 0 else []),
                    inc="v1" if k == KM - 1 else None)
            for f in range(KM):
                pi = f % 4
                tw = P.op("q", lambda E, f=f: E.dma_start(
                    out=wo_res.ap(), in_=ds["w_o"][f]),
                    inc="qo0", inc_n=16, waits=[rot.get("wores_user")])
                tmm = None
                for k in range(KM):
                    tmm = P.op("t", lambda E, k=k, f=f, pi=pi: E.matmul(
                        ps[pi][:, 0:Tc],
                        wo_res[:, k * 128:(k + 1) * 128],
                        aobf[:, k * Tc:(k + 1) * Tc],
                        start=(k == 0), stop=(k == KM - 1)),
                        waits=([tw, tcb, C.psum_free.get(pi)]
                               if k == 0 else []),
                        inc="t1" if k == KM - 1 else None)
                rot["wores_user"] = tmm
                te = P.op("v", lambda E, f=f, pi=pi: E.tensor_scalar(
                    woch[:, f * Tc:(f + 1) * Tc], ps[pi][:, 0:Tc],
                    bqt[:, 2 * KM + f:2 * KM + f + 1], None, ALU.add),
                    writes=("a_woch",), inc="v1",
                    waits=[tmm, rot.get("woch_user")])
                C.psum_free[pi] = te
            rot["aobf_user"] = P.tok("t1")
            two = P.tok("v1")
            tadd, tst = emit_postln_add(C, "a_woch", woch, b, cs, rot, two)
            rot["woch_user"] = tst
        stB.close()
        P.barrier()

    for b in range(BPC):
        _one_batch(b)

    stP.close()
    return P.tok("gr0")


def s4_ffn1(C):
    P, nc, ds, sb, ps, L = C.P, C.nc, C.ds, C.sb, C.ps, C.L
    Tc = 256
    W = Tc + 2
    st = ExitStack()
    def S(name, shape, dt):
        return st.enter_context(nc.sbuf_tensor(name, shape, dt))
    w1T = S("n_w1T", [128, KM * DH], BF16)
    xln = S("n_xln", [128, KM * W], F32)
    xqb = S("n_xqb", [128, KM * W], BF16)
    sqb4 = S("n_sqb", [128, W], BF16)
    h2b = S("n_h2b", [128, KH * Tc], BF16)
    hr = S("n_hr", [128, 2 * W], F32)
    hmax = S("n_hmax", [128, 2 * W], F32)
    scb = S("n_scb", [128, 2 * W], F32)
    tmpw = S("n_tmpw", [128, W], F32)
    tmpw2 = S("n_tmpw2", [128, W], F32)
    nab = S("n_abs", [128, W], F32)
    w2f = S("n_w2f", [128, 2 * KH * 128], BF16)
    dch = S("n_dch", [128, KM * Tc], F32)
    cw1 = S("n_cw1", [128, KH * 3], F32)
    cb1 = S("n_cb1", [128, KH], F32)
    xtl4 = S("n_xtl", [128, 2 * W], F32)
    C.xtl = xtl4
    C.qtmp2, C.qtmp2name = tmpw2, "n_tmpw2"
    sgt5 = S("n_sgt", [128, Tc], F32)
    C.sgt, C.sgtname = sgt5, "n_sgt"
    rot = {}
    NCH = L // Tc

    tw1 = P.op("q", _dma(w1T.ap().rearrange("p (k c) -> p k c", c=DH),
                         ds["w_w1T"].rearrange("(k p) c -> p k c", p=128)),
               inc="qw0", inc_n=16)
    tcw = P.op("q", _dma(cw1.ap().rearrange("p (k c) -> p k c", c=3),
                         C.din["L1_ffn_conv_w"]
                         .rearrange("(k p) o c -> p k (o c)", p=128)),
               inc="qc", inc_n=16)
    tcb = P.op("q", _dma_nc(cb1[:],
                         C.din["L1_ffn_conv_b"]
                         .rearrange("(k p) -> p k", p=128)),
               inc="qc", inc_n=16)

    for b in range(BPC):
        for c in range(NCH):
            c0 = c * Tc
            hs = max(c0 - 1, 0)
            he = min(c0 + Tc + 1, L)
            Wl = he - hs
            off = 1 if c == 0 else 0
            first = True
            for k in range(KM):
                par = k % 2
                tl = P.op("q", lambda E, b=b, k=k, hs=hs, he=he, par=par:
                          E.dma_start(
                              out=xtl4[:, par * W:par * W + (he - hs)],
                              in_=ds["resid"][b, k, :, hs:he]),
                          inc=f"qt{par}", inc_n=16,
                          waits=[rot.get(("xtl", par)),
                                 rot.get(("xtl2", par))])
                tc1 = P.op("v", lambda E, k=k, par=par, off=off, Wl=Wl:
                           E.tensor_copy(
                               xqb[:, k * W + off:k * W + off + Wl],
                               xtl4[:, par * W:par * W + Wl]),
                           writes=("n_xqb",), inc="v1",
                           waits=[tl, rot.get("xqb_user")])
                tc2 = P.op("s", lambda E, par=par, Wl=Wl: E.activation(
                    sqb4[:, 0:Wl], xtl4[:, par * W:par * W + Wl],
                    AF.Square),
                    writes=("n_sqb",), inc="s1",
                    waits=[tl, rot.get("sqb_user")])
                tc3 = P.op("v", lambda E, k=k, par=par, off=off, Wl=Wl:
                           E.tensor_copy(
                               xln[:, k * W + off:k * W + off + Wl],
                               xtl4[:, par * W:par * W + Wl]),
                           writes=("n_xln",), inc="v1",
                           waits=[tl, rot.get("xln_user")])
                rot[("xtl", par)] = tc3
                rot[("xtl2", par)] = tc2
                t6 = P.op("t", lambda E, k=k, first=first, off=off, Wl=Wl:
                          E.matmul(
                              ps[6][0:1, 0:Wl], sb["onesb"][:, 0:1],
                              xqb[:, k * W + off:k * W + off + Wl],
                              start=first, stop=(k == KM - 1)),
                          inc="t1", waits=[tc1] + ([C.psum_free.get(6)]
                                                   if first else []))
                t7 = P.op("t", lambda E, first=first, k=k, Wl=Wl: E.matmul(
                    ps[7][0:1, 0:Wl], sb["onesb"][:, 0:1], sqb4[:, 0:Wl],
                    start=first, stop=(k == KM - 1)),
                    inc="t1", waits=[tc2] + ([C.psum_free.get(7)]
                                             if first else []))
                rot["sqb_user"] = t7
                first = False
            tmm = P.tok("t1")
            oldTc = C.Tc
            C.Tc = Wl
            t4, t5 = ln_rows(C, ps[6], ps[7], 1.0 / DM, 0, tmm)
            C.Tc = oldTc
            C.psum_free[6] = P.tok("v1")
            C.psum_free[7] = C.psum_free[6]
            tz = P.op("v", lambda E: E.memset(hmax[:, 0:W], 0.0),
                      writes=("n_hmax",), inc="v1",
                      waits=[rot.get("hmax_user")])
            tn = None
            for k in range(KM):
                P.op("v", lambda E, k=k, off=off, Wl=Wl: E.tensor_tensor(
                    xln[:, k * W + off:k * W + off + Wl],
                    xln[:, k * W + off:k * W + off + Wl],
                    ps[5][:, 0:Wl], ALU.mult),
                    reads=("n_xln",), writes=("n_xln",), waits=[t4, t5])
                tnk = P.op("v", lambda E, k=k, off=off, Wl=Wl: E.tensor_tensor(
                    xln[:, k * W + off:k * W + off + Wl],
                    xln[:, k * W + off:k * W + off + Wl],
                    ps[4][:, 0:Wl], ALU.subtract),
                    reads=("n_xln",), writes=("n_xln",), inc="v1")
                tab = P.op("s", lambda E, k=k, off=off, Wl=Wl: E.activation(
                    nab[:, 0:Wl], xln[:, k * W + off:k * W + off + Wl],
                    AF.Abs),
                    reads=("n_xln",), writes=("n_abs",), inc="s1",
                    waits=[tnk, rot.get("nab_user")])
                tn = P.op("v", lambda E, off=off, Wl=Wl: E.tensor_tensor(
                    hmax[:, off:off + Wl], hmax[:, off:off + Wl],
                    nab[:, 0:Wl], ALU.max),
                    reads=("n_abs", "n_hmax"), writes=("n_hmax",),
                    inc="v1", waits=[tz, tab])
                rot["nab_user"] = tn
            C.psum_free[4] = tn
            C.psum_free[5] = tn
            rot["xln_user"] = tn
            tpr = P.op("g", lambda E, off=off, Wl=Wl: E.partition_all_reduce(
                hmax[:, W + off:W + off + Wl], hmax[:, off:off + Wl], 128,
                ReduceOp.max),
                reads=("n_hmax",), writes=("n_hmax",), inc="g1", waits=[tn])
            P.op("v", lambda E, off=off, Wl=Wl: E.tensor_scalar(
                hmax[:, W + off:W + off + Wl], hmax[:, W + off:W + off + Wl],
                1e-5, None, ALU.max),
                reads=("n_hmax",), writes=("n_hmax",), waits=[tpr])
            P.op("v", lambda E, off=off, Wl=Wl: E.reciprocal(
                scb[:, off:off + Wl], hmax[:, W + off:W + off + Wl]),
                reads=("n_hmax",), writes=("n_scb",),
                waits=[rot.get("scb_user")])
            P.op("v", lambda E, off=off, Wl=Wl: E.tensor_scalar(
                scb[:, off:off + Wl], scb[:, off:off + Wl], 127.0, None,
                ALU.mult),
                reads=("n_scb",), writes=("n_scb",))
            tsc = P.op("v", lambda E, off=off, Wl=Wl: E.tensor_scalar(
                scb[:, W + off:W + off + Wl],
                hmax[:, W + off:W + off + Wl], 1.0 / 127.0, None, ALU.mult),
                reads=("n_hmax",), writes=("n_scb",), inc="v1")
            rot["hmax_user"] = tsc
            tq = None
            for k in range(KM):
                if off == 1:
                    P.op("v", lambda E, k=k: E.memset(
                        xqb[:, k * W:k * W + 1], 0.0), writes=("n_xqb",))
                if he == L:
                    P.op("v", lambda E, k=k: E.memset(
                        xqb[:, k * W + off + Wl:(k + 1) * W], 0.0),
                        writes=("n_xqb",))
                tq = emit_qdq_bc(
                    C, xln[:, k * W + off:k * W + off + Wl],
                    xqb[:, k * W + off:k * W + off + Wl],
                    scb[:, off:off + Wl], scb[:, W + off:W + off + Wl],
                    -128.0, 127.0, tmpw[:, 0:Wl], "n_tmpw",
                    reads=("n_xln",), writes=("n_xqb",),
                    waits=[tsc], bcreads=("n_scb",))
            tzh = P.op("v", lambda E: E.memset(hmax[:, 0:Tc], 0.0),
                       writes=("n_hmax",), inc="v1", waits=[tsc])
            Tcc = min(Tc, L - c0)
            for f in range(KH):
                par, pi = f % 2, f % 2
                tmm = None
                for k in range(KM):
                    tmm = P.op("t", lambda E, k=k, f=f, pi=pi, W=W: E.matmul(
                        ps[pi][:, 0:W], w1T[:, k * DH + f * 128:
                                            k * DH + (f + 1) * 128],
                        xqb[:, k * W:(k + 1) * W],
                        start=(k == 0), stop=(k == KM - 1)),
                        waits=([tq, tw1, C.psum_free.get(pi)]
                               if k == 0 else []),
                        inc="t1" if k == KM - 1 else None)
                te = P.op("s", lambda E, pi=pi, par=par, W=W: E.activation(
                    hr[:, par * W:(par + 1) * W], ps[pi][:, 0:W], AF.Copy),
                    writes=("n_hr",), inc="s1",
                    waits=[tmm, rot.get(("hr", par))])
                C.psum_free[pi] = te
                tcv2 = None
                P.op("v", lambda E, f=f, par=par, Tcc=Tcc, W=W:
                     E.tensor_scalar(
                         sgt5[:, 0:Tcc], hr[:, par * W:par * W + Tcc],
                         cw1[:, f * 3:f * 3 + 1], None, ALU.mult),
                     reads=("n_hr", "n_cw1"), writes=("n_sgt",),
                     waits=[te, tcb, rot.get("sgt_user")])
                for kk in range(1, 3):
                    tcv2 = P.op("v", lambda E, f=f, par=par, kk=kk, Tcc=Tcc,
                                W=W: E.scalar_tensor_tensor(
                                    sgt5[:, 0:Tcc],
                                    hr[:, par * W + kk:par * W + kk + Tcc],
                                    cw1[:, f * 3 + kk:f * 3 + kk + 1],
                                    sgt5[:, 0:Tcc], ALU.mult, ALU.add),
                                reads=("n_hr", "n_sgt", "n_cw1"),
                                writes=("n_sgt",),
                                inc="v1" if kk == 2 else None)
                rot[("hr", par)] = tcv2
                ts = P.op("s", lambda E, f=f, Tcc=Tcc: E.activation(
                    h2b[:, f * Tc:f * Tc + Tcc], sgt5[:, 0:Tcc], AF.Silu,
                    bias=cb1[:, f:f + 1]),
                    reads=("n_sgt",), writes=("n_h2b",), inc="s1",
                    waits=[tcv2, tcb, rot.get("h2b_user")])
                rot["sgt_user"] = ts
                tab2 = P.op("s", lambda E, f=f, Tcc=Tcc: E.activation(
                    nab[:, 0:Tcc], h2b[:, f * Tc:f * Tc + Tcc], AF.Abs),
                    reads=("n_h2b",), writes=("n_abs",), inc="s1",
                    waits=[ts, rot.get("nab_user")])
                tmx = P.op("v", lambda E, Tcc=Tcc: E.tensor_tensor(
                    hmax[:, 0:Tcc], hmax[:, 0:Tcc], nab[:, 0:Tcc], ALU.max),
                    reads=("n_abs", "n_hmax"), writes=("n_hmax",),
                    inc="v1", waits=[tzh, tab2])
                rot["nab_user"] = tmx
            th = P.tok("v1")
            tpr = P.op("g", lambda E, Tcc=Tcc: E.partition_all_reduce(
                hmax[:, W:W + Tcc], hmax[:, 0:Tcc], 128, ReduceOp.max),
                reads=("n_hmax",), writes=("n_hmax",), inc="g1", waits=[th])
            P.op("v", lambda E, Tcc=Tcc: E.tensor_scalar(
                hmax[:, W:W + Tcc], hmax[:, W:W + Tcc], 1e-5, None,
                ALU.max),
                reads=("n_hmax",), writes=("n_hmax",), waits=[tpr])
            P.op("v", lambda E, Tcc=Tcc: E.reciprocal(
                scb[:, 0:Tcc], hmax[:, W:W + Tcc]),
                reads=("n_hmax",), writes=("n_scb",), waits=[tsc])
            P.op("v", lambda E, Tcc=Tcc: E.tensor_scalar(
                scb[:, 0:Tcc], scb[:, 0:Tcc], 127.0, None, ALU.mult),
                reads=("n_scb",), writes=("n_scb",))
            tsc2 = P.op("v", lambda E, Tcc=Tcc: E.tensor_scalar(
                scb[:, W:W + Tcc], hmax[:, W:W + Tcc], 1.0 / 127.0, None,
                ALU.mult),
                reads=("n_hmax",), writes=("n_scb",), inc="v1")
            rot["hmax_user"] = tsc2
            rot["scb_user"] = tsc2
            tq2 = None
            for f in range(KH):
                tq2 = emit_qdq_bc(
                    C, h2b[:, f * Tc:f * Tc + Tcc],
                    h2b[:, f * Tc:f * Tc + Tcc],
                    scb[:, 0:Tcc], scb[:, W:W + Tcc], -128.0, 127.0,
                    tmpw[:, 0:Tcc], "n_tmpw",
                    reads=("n_h2b",), writes=("n_h2b",),
                    waits=[tsc2], bcreads=("n_scb",))
            rot["h2b_user"] = tq2
            cs = slice(c0, c0 + Tcc)
            for f in range(KM):
                par, pi = f % 2, f % 2
                tw2 = P.op("q", lambda E, f=f, par=par: E.dma_start(
                    out=w2f[:, par * KH * 128:(par + 1) * KH * 128],
                    in_=ds["w_w2T"][f]),
                    inc=f"qp{par}", inc_n=16, waits=[rot.get(("w2f", par))])
                tmm = None
                for k in range(KH):
                    tmm = P.op("t", lambda E, k=k, par=par, pi=pi, Tcc=Tcc:
                               E.matmul(
                                   ps[pi][:, 0:Tcc],
                                   w2f[:, par * KH * 128 + k * 128:
                                       par * KH * 128 + (k + 1) * 128],
                                   h2b[:, k * Tc:k * Tc + Tcc],
                                   start=(k == 0), stop=(k == KH - 1)),
                               waits=([tw2, tq2, C.psum_free.get(pi)]
                                      if k == 0 else []),
                               inc="t1" if k == KH - 1 else None)
                rot[("w2f", par)] = tmm
                te = P.op("s", lambda E, f=f, pi=pi, Tcc=Tcc: E.activation(
                    dch[:, f * Tc:f * Tc + Tcc], ps[pi][:, 0:Tcc], AF.Copy),
                    writes=("n_dch",), inc="s1",
                    waits=[tmm, rot.get("dch_user")])
                C.psum_free[pi] = te
            tdn = P.tok("s1")
            oldTc = C.Tc
            C.Tc = Tcc
            dchv = dch.ap().rearrange("p (k t) -> p k t", t=Tc)[:, :, 0:Tcc]
            tadd, tst = emit_postln_add_v(C, "n_dch", dch, Tc, b, cs, rot,
                                          tdn, dst="xln1")
            C.Tc = oldTc
            rot["dch_user"] = tst
            rot["xqb_user"] = P.tok("t1")
    st.close()
    return P.tok("gr0")


def emit_postln_add_v(C, srcname, srcch, stride, b, cs, rot, tready,
                      dst="resid"):
    P, ps, sb = C.P, C.ps, C.sb
    Tc = C.Tc
    xtl, sgt = C.xtl, C.sgt
    first = True
    for k in range(KM):
        t6 = P.op("t", lambda E, k=k, first=first: E.matmul(
            ps[6][0:1, 0:Tc], sb["onesf"][:, 0:1],
            srcch[:, k * stride:k * stride + Tc], start=first,
            stop=(k == KM - 1)),
            inc="t1", waits=[tready] + ([C.psum_free.get(6)]
                                        if first else []))
        tsq = P.op("s", lambda E, k=k: E.activation(
            sgt[:, 0:Tc], srcch[:, k * stride:k * stride + Tc], AF.Square),
            writes=(C.sgtname,), inc="s1",
            waits=[tready, rot.get("sgt_user")])
        t7 = P.op("t", lambda E, first=first, k=k: E.matmul(
            ps[7][0:1, 0:Tc], sb["onesf"][:, 0:1], sgt[:, 0:Tc],
            start=first, stop=(k == KM - 1)), inc="t1",
            waits=[tsq] + ([C.psum_free.get(7)] if first else []))
        rot["sgt_user"] = t7
        first = False
    tmm = P.tok("t1")
    t4, t5 = ln_rows(C, ps[6], ps[7], 1.0 / DM, 0, tmm)
    C.psum_free[6] = P.tok("v1")
    C.psum_free[7] = C.psum_free[6]
    for k in range(KM):
        par = k % 2
        HW = xtl.shape[1] // 2
        tl = P.op("q", lambda E, k=k, par=par, b=b, cs=cs, HW=HW:
                  E.dma_start(out=xtl[:, par * HW:par * HW + Tc],
                              in_=C.ds["resid"][b, k, :, cs]),
                  inc=f"qz{par}", inc_n=16,
                  waits=[rot.get(("xtl", par)), rot.get(("xtl2", par))])
        P.op("v", lambda E, k=k: E.tensor_tensor(
            srcch[:, k * stride:k * stride + Tc],
            srcch[:, k * stride:k * stride + Tc], ps[5][:, 0:Tc], ALU.mult),
            reads=(srcname,), writes=(srcname,), waits=[t4, t5])
        P.op("v", lambda E, k=k: E.tensor_tensor(
            srcch[:, k * stride:k * stride + Tc],
            srcch[:, k * stride:k * stride + Tc], ps[4][:, 0:Tc],
            ALU.subtract),
            reads=(srcname,), writes=(srcname,))
        ta = P.op("v", lambda E, k=k, par=par, HW=HW: E.tensor_tensor(
            srcch[:, k * stride:k * stride + Tc],
            srcch[:, k * stride:k * stride + Tc],
            xtl[:, par * HW:par * HW + Tc], ALU.add),
            reads=(srcname,), writes=(srcname,), inc="v1", waits=[tl])
        rot[("xtl", par)] = ta
        rot[("xtl2", par)] = ta
    tadd = P.tok("v1")
    C.psum_free[4] = tadd
    C.psum_free[5] = tadd
    tst = P.op("s", lambda E, b=b, cs=cs, dst=dst: E.dma_start(
        out=C.ds[dst][b, :, :, cs].rearrange("k p t -> p k t"),
        in_=srcch.ap().rearrange("p (k t) -> p k t", t=stride)
        [:, :, 0:Tc]),
        inc="gr0", inc_n=16, waits=[tadd])
    return tadd, tst


def build_nc(L=2048, Tc=512, dbg=None, ncore=NCORE):
    global _LASTP
    nc = bass.Bass("TRN2", target_bir_lowering=False)
    P = Prog(nc)
    _LASTP = P
    es = P.es

    din = {}
    for name, shape in [
        ("L0_attn_x", [BPC, L, DM]), ("L0_attn_adj", [DI, DI]),
        ("L0_attn_in_proj_w", [DM, 2 * DI]), ("L0_attn_conv_w", [DI, 1, 4]),
        ("L0_attn_conv_b", [DI]), ("L0_attn_x_proj_w", [DI, DTR + 2 * NST]),
        ("L0_attn_dt_proj_w", [DTR, DI]), ("L0_attn_dt_proj_b", [DI]),
        ("L0_attn_A_log", [DI, NST]), ("L0_attn_D", [DI]),
        ("L0_attn_out_proj_w", [DI, DM]),
        ("L0_ffn_w_gate", [HF, DM]), ("L0_ffn_w_up", [HF, DM]),
        ("L0_ffn_w_down", [DM, HF]),
        ("L1_attn_Wq", [DM, DM]), ("L1_attn_bq", [DM]),
        ("L1_attn_Wk", [DM, DM]), ("L1_attn_bk", [DM]),
        ("L1_attn_Wv", [DM, DM]), ("L1_attn_bv", [DM]),
        ("L1_attn_Wo", [DM, DM]), ("L1_attn_bo", [DM]),
        ("L1_ffn_w1", [DH, DM]), ("L1_ffn_conv_w", [DH, 1, 3]),
        ("L1_ffn_conv_b", [DH]), ("L1_ffn_w2", [DM, DH]),
    ]:
        din[name] = nc.dram_tensor(name, shape, F32, kind="ExternalInput").ap()
    out_d = nc.dram_tensor("out", [BPC, L, DM], F32, kind="ExternalOutput").ap()

    ds = {}
    for name, shape, dt in [
        ("w_inproj", [2 * KI, 128, DM], BF16),
        ("w_adjn", [KI, 128, DI], BF16),
        ("w_outproj", [KM, 128, DI], BF16), ("w_xproj", [DI, 96], BF16),
        ("w_dtproj", [DTR, DI], BF16),
        ("w_upT", [DM, HF], BF16), ("w_gateT", [DM, HF], BF16),
        ("w_downT", [KM, 128, HF], BF16),
        ("w_q", [DM, DM], BF16), ("w_k", [DM, DM], BF16),
        ("w_v", [DM, DM], BF16), ("w_o", [KM, 128, DM], BF16),
        ("w_w1T", [DM, DH], BF16), ("w_w2T", [KM, 128, DH], BF16),
        ("resid", [BPC, KM, 128, L], F32),
        ("xln1", [BPC, KM, 128, L], F32),
        ("up_raw", [BPC, KF, 128, L], F32),
        ("gate_raw", [BPC, KF, 128, L], F32),
        ("g2pre", [BPC, KF, 128, L], F32),
        ("o_pre", [BPC, KF, 128, L], F32),
        ("cc_in", [1, 8], F32), ("cc_out", [1, 8], F32),
    ]:
        ds[name] = nc.dram_tensor(name, shape, dt).ap()

    sb = {}
    def sbuf(name, shape, dt):
        sb[name] = es.enter_context(nc.sbuf_tensor(name, shape, dt))
        return sb[name]
    sbuf("idf", [128, 128], F32)
    sbuf("idb", [128, 128], BF16)
    sbuf("onesf", [128, 128], F32)
    sbuf("onesb", [128, 128], BF16)
    sbuf("selb", [32, 32 * 128], BF16)
    init_es = ExitStack()
    sb["self_sel"] = init_es.enter_context(
        nc.sbuf_tensor("self_sel", [32, 32 * 128], F32, side="right"))
    sbuf("negA", [128, KI * NST], F32)
    sbuf("Dsb", [128, KI], F32)
    sbuf("cwsb", [128, KI * 4], F32)
    sbuf("cbsb", [128, KI], F32)
    sbuf("dtbn", [128, KI], F32)
    sbuf("hcar", [128, KI * NST], F32)
    sbuf("tails", [128, KI * 3], F32)
    sbuf("xprj", [128, KI * 96], BF16)
    sbuf("dtprj", [64, DI], BF16)
    sbuf("rows", [1, 4 * 520], F32)
    ps = [es.enter_context(nc.psum_tensor(f"ps{i}", [128, 512], F32))
          for i in range(8)]

    C = SimpleNamespace(P=P, nc=nc, din=din, ds=ds, sb=sb, ps=ps, out_d=out_d,
                        L=L, Tc=Tc, rows=sb["rows"], psum_free={})

    P.op("g", lambda E: E.iota(sb["idf"][:], pattern=[[1, 128]], base=0,
                               channel_multiplier=-1,
                               allow_small_or_imprecise_dtypes=True),
         writes=("idf",))
    P.op("g", lambda E: E.tensor_scalar(sb["idf"][:], sb["idf"][:], 0.0,
                                        None, ALU.is_equal),
         reads=("idf",), writes=("idf",))
    P.op("g", lambda E: E.iota(sb["self_sel"][:], pattern=[[1, 32], [0, 128]],
                               base=0, channel_multiplier=-1,
                               allow_small_or_imprecise_dtypes=True),
         writes=("self_sel",))
    P.op("g", lambda E: E.tensor_scalar(sb["selb"][:], sb["self_sel"][:],
                                        0.0, None, ALU.is_equal),
         reads=("self_sel",), writes=("selb",))
    P.op("g", lambda E: E.tensor_copy(sb["idb"][:], sb["idf"][:]),
         reads=("idf",), writes=("idb",))
    P.op("g", lambda E: E.memset(sb["onesf"][:], 1.0), writes=("onesf",))
    tinit = P.op("g", lambda E: E.memset(sb["onesb"][:], 1.0),
                 writes=("onesb",), inc="g0")
    P.raw("g", lambda E: E.load_library(_attn_lib))
    for e in ("t", "v", "s", "q"):
        P.waitt(e, tinit)

    init_es.close()
    for k in range(KI):
        P.op("q", _dma(sb["negA"][:, k * NST:(k + 1) * NST],
                       din["L0_attn_A_log"][k * 128:(k + 1) * 128, :]),
             inc="qc", inc_n=16)
        P.op("q", _dma(sb["cwsb"][:, k * 4:(k + 1) * 4],
                       din["L0_attn_conv_w"][k * 128:(k + 1) * 128, 0, :]),
             inc="qc", inc_n=16)
        P.op("q", _dma(sb["Dsb"][:, k:k + 1],
                       din["L0_attn_D"][k * 128:(k + 1) * 128]
                       .rearrange("(p o) -> p o", o=1)), inc="qc", inc_n=16)
        P.op("q", _dma(sb["cbsb"][:, k:k + 1],
                       din["L0_attn_conv_b"][k * 128:(k + 1) * 128]
                       .rearrange("(p o) -> p o", o=1)), inc="qc", inc_n=16)
        P.op("q", _dma(sb["dtbn"][:, k:k + 1],
                       din["L0_attn_dt_proj_b"][k * 128:(k + 1) * 128]
                       .rearrange("(p o) -> p o", o=1)), inc="qc", inc_n=16)
    qc = P.tok("qc")
    tna = P.op("s", lambda E: E.activation(sb["negA"][:], sb["negA"][:],
                                           AF.Exp),
               reads=("negA",), writes=("negA",), inc="s1", waits=[qc])
    P.op("v", lambda E: E.tensor_scalar(sb["negA"][:], sb["negA"][:], -1.0,
                                        None, ALU.mult),
         reads=("negA",), writes=("negA",), waits=[tna])
    P.op("v", lambda E: E.tensor_scalar(sb["dtbn"][:], sb["dtbn"][:], -1.0,
                                        None, ALU.mult),
         reads=("dtbn",), writes=("dtbn",), waits=[qc], inc="v1")

    st = ExitStack()
    ti = st.enter_context(nc.sbuf_tensor("i_ti", [128, 2 * DM], F32))
    to = st.enter_context(nc.sbuf_tensor("i_to", [128, 2 * KM * 128], F32))
    rot = {}
    for b in range(BPC):
        for tt in range(L // 128):
            par = tt % 2
            tl = P.op("q", lambda E, b=b, tt=tt, par=par: E.dma_start(
                out=ti[:, par * DM:(par + 1) * DM],
                in_=din["L0_attn_x"][b, tt * 128:(tt + 1) * 128, :]),
                inc=f"qi{par}", inc_n=16, waits=[rot.get(("ti", par))])
            for half in range(2):
                pi = half
                tmm = None
                for j in range(4):
                    k = half * 4 + j
                    tmm = P.op("t", lambda E, k=k, j=j, par=par, pi=pi:
                               E.transpose(
                                   ps[pi][:, j * 128:(j + 1) * 128],
                                   ti[:, par * DM + k * 128:
                                      par * DM + (k + 1) * 128],
                                   sb["idf"][:]),
                               waits=([tl, C.psum_free.get(pi)]
                                      if j == 0 else []),
                               inc="t1" if j == 3 else None)
                te = P.op("s", lambda E, par=par, half=half, pi=pi:
                          E.activation(
                              to[:, par * KM * 128 + half * 512:
                                 par * KM * 128 + (half + 1) * 512],
                              ps[pi][:, 0:512], AF.Copy),
                          writes=("i_to",), inc="s1",
                          waits=[tmm, rot.get(("to", par))])
                C.psum_free[pi] = te
            tev = P.tok("s1")
            rot[("ti", par)] = tev
            tst = P.op("s", lambda E, b=b, tt=tt, par=par: E.dma_start(
                out=ds["resid"][b, :, :, tt * 128:(tt + 1) * 128]
                .rearrange("k p t -> p k t"),
                in_=to[:, par * KM * 128:(par + 1) * KM * 128]
                .rearrange("p (k t) -> p k t", t=128)),
                inc=f"gi{par}", inc_n=16, waits=[tev])
            rot[("to", par)] = tst
    st.close()
    P.barrier()
    x_staged = None

    need = {"x1": ("mamba",), "x2": ("mamba", "ffn0"),
            "x3": ("mamba", "ffn0", "attn1"),
            "aout": ("mamba", "ffn0", "attn1")}.get(
                dbg, ("mamba", "ffn0", "attn1", "ffn1"))
    C.dbg_aout = False
    C.w_done = s0_weights(C, need)
    P.barrier()
    s1_done = s1_mamba(C)
    def _finish():
        block = es.enter_context(nc.Block())
        P.build(block)
        es.close()
        mybir.codegen_inst_isa_subclasses(nc)
        return nc
    if dbg == "x1":
        P.barrier()
        s_out(C, s1_done)
        return _finish()
    P.barrier()
    s2_done = s2_ffn0(C, ncore)
    if dbg == "x2":
        P.barrier()
        s_out(C, s2_done)
        return _finish()
    P.barrier()
    C.dbg_aout = (dbg == "aout")
    s3_done = s3_attn(C)
    if dbg == "aout":
        P.dma_wait_all("g", prefixes=("go",))
        return _finish()
    if dbg == "x3":
        P.barrier()
        s_out(C, s3_done)
        return _finish()
    P.barrier()
    s4_done = s4_ffn1(C)
    P.barrier()
    s_out(C, s4_done, src="xln1")
    return _finish()


def kernel(**inputs):
    full = {k: np.ascontiguousarray(np.asarray(v, dtype=np.float32))
            for k, v in inputs.items()}
    nc = build_nc(L=2048, Tc=512, dbg=None)
    in_maps = []
    for c in range(NCORE):
        m = dict(full)
        m["L0_attn_x"] = full["L0_attn_x"][c * BPC:(c + 1) * BPC]
        in_maps.append(m)
    res = run_bass_kernel_spmd(nc, in_maps, core_ids=list(range(NCORE)))
    return np.concatenate([r["out"] for r in res.results], axis=0)


if __name__ == "__main__":
    print("kernel module OK")


# version 15 (final)
# speedup vs baseline: 1.0494x; 1.0024x over previous
import numpy as np
from contextlib import ExitStack
from types import SimpleNamespace

import concourse.bass as bass
import concourse.mybir as mybir
from concourse.bass_utils import run_bass_kernel_spmd
from concourse.library_config import attn as _attn_lib
from concourse.bass_isa import ReduceOp

F32 = mybir.dt.float32
BF16 = mybir.dt.bfloat16
AF = mybir.ActivationFunctionType
ALU = mybir.AluOpType
AX = mybir.AxisListType

DM, DI, NST, DTR, HF, DH = 1024, 2048, 16, 64, 3072, 4096
KM, KI, KF, KH = DM // 128, DI // 128, HF // 128, DH // 128
NCORE, BPC = 8, 2
EPS = 1e-5


class Prog:
    ENGS = ("t", "v", "s", "g", "q")

    def __init__(self, nc):
        self.nc = nc
        self.streams = {e: [] for e in self.ENGS}
        self.cnt = {}
        self.sem_h = {}
        self.dirty = {e: set() for e in self.ENGS}
        self.rset = {e: set() for e in self.ENGS}
        self.waited = {e: {} for e in self.ENGS}
        self.dma_sems = set()
        self.es = ExitStack()
        self.labels = {e: [] for e in self.ENGS}
        self.meta = {e: [] for e in self.ENGS}

    def sem(self, name):
        if name not in self.sem_h:
            self.sem_h[name] = self.es.enter_context(self.nc.semaphore(name))
            self.cnt[name] = 0
        return self.sem_h[name]

    def wait(self, eng, name, val):
        if val <= 0 or self.waited[eng].get(name, 0) >= val:
            return
        self.waited[eng][name] = val
        h = self.sem(name)
        self.streams[eng].append(lambda E, h=h, v=val: E.wait_ge(h, v))
        self.meta[eng].append(("w", name, val))

    def waitt(self, eng, tok):
        if tok is not None:
            self.wait(eng, tok[0], tok[1])

    def drain(self, eng):
        if self.dirty[eng] or self.rset[eng]:
            self.streams[eng].append(lambda E: E.drain())
            self.labels[eng].append(("drain", 0))
            self.dirty[eng].clear()
            self.rset[eng].clear()

    def op(self, eng, fn, reads=(), writes=(), inc=None, inc_n=1, waits=(),
           clears=()):
        merged = {}
        for tk in waits:
            if tk is not None and tk[1] > merged.get(tk[0], 0):
                merged[tk[0]] = tk[1]
        for nm, v in merged.items():
            self.waitt(eng, (nm, v))
        for nm in clears:
            self.dirty[eng].discard(nm)
            self.rset[eng].discard(nm)
        if (any(r in self.dirty[eng] for r in reads)
                or any(w in self.dirty[eng] or w in self.rset[eng]
                       for w in writes)):
            self.drain(eng)
        if inc is not None:
            self.sem(inc)
            if inc_n == 16:
                self.dma_sems.add(inc)
            self.cnt[inc] += inc_n
            h = self.sem_h[inc]

            def run(E, fn=fn, h=h, n=inc_n):
                fn(E).then_inc(h, n)
        else:
            def run(E, fn=fn):
                fn(E)
        self.streams[eng].append(run)
        import sys as _sys
        f = _sys._getframe(1)
        if f.f_code.co_name == "<lambda>":
            f = f.f_back
        self.labels[eng].append(("op", f.f_lineno))
        self.meta[eng].append(("i", inc, inc_n, f.f_lineno)
                              if inc is not None else ("i", None, 0,
                                                       f.f_lineno))
        if self.labels[eng][-2:-1] == [("drain", 0)]:
            self.labels[eng][-2] = ("drain", f.f_lineno)
        for w in writes:
            self.dirty[eng].add(w)
        for r in reads:
            self.rset[eng].add(r)
        return (inc, self.cnt[inc]) if inc is not None else None

    def raw(self, eng, fn):
        self.streams[eng].append(fn)
        self.labels[eng].append(("raw", 0))

    def tok(self, name):
        self.sem(name)
        return (name, self.cnt[name])

    def dma_wait_all(self, eng, prefixes=None):
        for name in sorted(self.dma_sems):
            if prefixes is None or any(name.startswith(p) for p in prefixes):
                self.wait(eng, name, self.cnt[name])

    def barrier(self, engines=None):
        for eng in (engines or self.ENGS):
            for name in sorted(self.cnt):
                self.wait(eng, name, self.cnt[name])

    def build(self, block):
        for attr, e in (("tensor", "t"), ("vector", "v"), ("scalar", "s"),
                        ("gpsimd", "g"), ("sync", "q")):
            stream = self.streams[e]

            def make(stream):
                def body(E):
                    for f in stream:
                        f(E)
                return body
            getattr(block, attr)(make(stream))


def _dma(out, in_):
    return lambda E: E.dma_start(out=out, in_=in_)


def _dma_nc(out, in_):
    return lambda E: E.dma_start(out=out, in_=in_,
                                 allow_slow_non_contiguous=True)


def _win_starts(L, ws=256):
    stride, starts = ws // 2, []
    for i in range(0, L, stride):
        end = min(i + ws, L)
        starts.append(max(0, end - ws))
        if end == L:
            break
    return starts


def s0_weights(C, need=("mamba", "ffn0", "attn1", "ffn1")):
    P, nc, din, ds, sb, ps = C.P, C.nc, C.din, C.ds, C.sb, C.ps
    st = ExitStack()
    wf = st.enter_context(nc.sbuf_tensor("s0_wf", [128, 2 * 4096], F32))
    wb = st.enter_context(nc.sbuf_tensor("s0_wb", [128, 2 * 4096], BF16))
    sca = st.enter_context(nc.sbuf_tensor("s0_sca", [128, 2 * 192], F32))
    tb = st.enter_context(nc.sbuf_tensor("s0_tb", [128, 2 * 512], BF16))
    row = st.enter_context(nc.sbuf_tensor("s0_row", [128, 8], F32))
    wmod = st.enter_context(nc.sbuf_tensor("s0_wm", [128, 4096], F32))
    ldt, vdt, gqt = {}, {}, {}

    def loadw(par, dst_cols, src_ap, w):
        P.op("q", _dma(wf[:, dst_cols], src_ap), inc=f"qw{par}", inc_n=16,
             waits=[vdt.get((par, "wf"))])
        return P.tok(f"qw{par}")

    def plain_cast(src, dst, ktiles, cols, scale=None, rows=128,
                   tiled=0):
        for k in range(ktiles):
            par = k % 2
            tl = P.op("q", _dma(wf[0:rows, par * 4096:par * 4096 + cols],
                                src[k * rows:(k + 1) * rows, :]),
                      inc=f"qw{par}", inc_n=16, waits=[vdt.get((par, "wf"))])
            if scale is None:
                t = P.op("v", lambda E, par=par, cols=cols, rows=rows:
                         E.tensor_copy(
                             wb[0:rows, par * 4096:par * 4096 + cols],
                             wf[0:rows, par * 4096:par * 4096 + cols]),
                         reads=("s0_wf",), writes=("s0_wb",), inc="v0",
                         waits=[tl, gqt.get((par, "wb"))])
            else:
                t = P.op("v", lambda E, par=par, cols=cols, s=scale,
                         rows=rows: E.tensor_scalar(
                             wb[0:rows, par * 4096:par * 4096 + cols],
                             wf[0:rows, par * 4096:par * 4096 + cols],
                             s, None, ALU.mult),
                         reads=("s0_wf",), writes=("s0_wb",), inc="v0",
                         waits=[tl, gqt.get((par, "wb"))])
            vdt[(par, "wf")] = t
            if tiled:
                P.op("s", lambda E, k=k, par=par, cols=cols, dst=dst,
                     tiled=tiled: E.dma_start(
                         out=dst[:, :, k * 128:(k + 1) * 128]
                         .rearrange("f p c -> p f c"),
                         in_=wb[:, par * 4096:par * 4096 + cols]
                         .rearrange("p (f c) -> p f c", c=128)),
                     inc=f"gw{par}", inc_n=16, waits=[t])
            else:
                P.op("s", lambda E, k=k, par=par, cols=cols, rows=rows,
                     dst=dst: E.dma_start(
                         out=dst[k * rows:(k + 1) * rows, :],
                         in_=wb[0:rows, par * 4096:par * 4096 + cols]),
                     inc=f"gw{par}", inc_n=16, waits=[t])
            gqt[(par, "wb")] = P.tok(f"gw{par}")

    if "mamba" in need:
        plain_cast(din["L0_attn_in_proj_w"], ds["w_inproj"], KM, 2 * DI,
                   tiled=2 * KI)
        plain_cast(din["L0_attn_adj"], ds["w_adjn"], KI, DI, scale=-1.0,
                   tiled=KI)
        plain_cast(din["L0_attn_out_proj_w"], ds["w_outproj"], KI, DM,
                   tiled=KM)
        plain_cast(din["L0_attn_x_proj_w"], ds["w_xproj"], KI, DTR + 2 * NST)
        plain_cast(din["L0_attn_dt_proj_w"], ds["w_dtproj"], 1, DI, rows=64)
    if "attn1" in need:
        for wn, dst in (("L1_attn_Wq", "w_q"), ("L1_attn_Wk", "w_k"),
                        ("L1_attn_Wv", "w_v")):
            plain_cast(din[wn], ds[dst], KM, DM)
        plain_cast(din["L1_attn_Wo"], ds["w_o"], KM, DM, tiled=KM)

    idb, onesf = sb["idb"], sb["onesf"]

    def transpose_store(par, cols, dst, ktile, psum_i, ready, tiled=False):
        nblk = cols // 128
        for g0 in range(0, nblk, 4):
            gn = min(4, nblk - g0)
            tprev = C.psum_free.get(psum_i)
            tmm = None
            for j in range(gn):
                tmm = P.op("t", lambda E, par=par, g0=g0, j=j, psum_i=psum_i:
                           E.transpose(
                               ps[psum_i].ap().bitcast(BF16)
                               [0:128, j * 128:(j + 1) * 128],
                               wb[:, par * 4096 + (g0 + j) * 128:
                                  par * 4096 + (g0 + j + 1) * 128],
                               idb[:]),
                           waits=[tprev, ready] if j == 0 else [],
                           inc="t0" if j == gn - 1 else None)
            te = P.op("v", lambda E, psum_i=psum_i, gn=gn: E.tensor_copy(
                tb[:, psum_i % 2 * 512: psum_i % 2 * 512 + gn * 128],
                ps[psum_i].ap().bitcast(BF16)[:, 0:gn * 128]),
                writes=("s0_tb",), inc="v0",
                waits=[tmm, gqt.get((psum_i % 2, "tb"))])
            C.psum_free[psum_i] = te
            if tiled:
                P.op("s", lambda E, g0=g0, gn=gn, ktile=ktile, dst=dst,
                     psum_i=psum_i: E.dma_start(
                         out=dst[ktile, :, g0 * 128:(g0 + gn) * 128],
                         in_=tb[:, psum_i % 2 * 512:
                                psum_i % 2 * 512 + gn * 128]),
                     inc=f"gts{psum_i % 2}", inc_n=16, waits=[te])
            else:
                P.op("s", lambda E, g0=g0, gn=gn, ktile=ktile, dst=dst,
                     psum_i=psum_i: E.dma_start(
                         out=dst[g0 * 128:(g0 + gn) * 128,
                                 ktile * 128:(ktile + 1) * 128]
                         .rearrange("(g p) c -> p g c", p=128),
                         in_=tb[:, psum_i % 2 * 512:
                                psum_i % 2 * 512 + gn * 128]
                         .rearrange("p (g c) -> p g c", c=128)),
                     inc=f"gts{psum_i % 2}", inc_n=16, waits=[te])
            gqt[(psum_i % 2, "tb")] = P.tok(f"gts{psum_i % 2}")
        vdt[(par, "wbT")] = P.tok("t0")

    def lpbq_weight(src, dstT, O):
        KT, I = O // 128, src.shape[1]
        NB = I // 32
        for k in range(KT):
            par = k % 2
            tl = loadw(par, slice(par * 4096, par * 4096 + I),
                       src[k * 128:(k + 1) * 128, :], I)
            wfs = wf[:, par * 4096:par * 4096 + I]
            wbs = wb[:, par * 4096:par * 4096 + I]
            scs = sca[:, par * 192:par * 192 + NB]
            sns = sca[:, par * 192 + NB:par * 192 + 2 * NB]
            P.op("v", lambda E, wfs=wfs, scs=scs, NB=NB: E.tensor_reduce(
                scs, wfs.rearrange("p (n k) -> p n k", k=32), AX.X,
                ALU.max, apply_absolute_value=True),
                reads=("s0_wf",), writes=("s0_sca",), waits=[tl])
            P.op("v", lambda E, scs=scs: E.tensor_scalar(
                scs, scs, 1.0 / 7.0, 1e-12, ALU.mult, ALU.max),
                reads=("s0_sca",), writes=("s0_sca",))
            P.op("v", lambda E, scs=scs, sns=sns: E.tensor_copy(sns, scs),
                 reads=("s0_sca",), writes=("s0_sca",))
            P.op("v", lambda E, scs=scs: E.reciprocal(scs, scs),
                 reads=("s0_sca",), writes=("s0_sca",))
            def bc(a, NB=NB):
                return a.unsqueeze(2).broadcast_to([128, NB, 32])
            P.op("v", lambda E, wfs=wfs, scs=scs, NB=NB: E.tensor_tensor(
                wfs.rearrange("p (n k) -> p n k", k=32), wfs
                .rearrange("p (n k) -> p n k", k=32), bc(scs), ALU.mult),
                reads=("s0_wf", "s0_sca"), writes=("s0_wf",))
            P.op("v", lambda E, wfs=wfs: E.tensor_scalar(
                wfs, wfs, 12582912.0, 12582912.0, ALU.add, ALU.subtract),
                reads=("s0_wf",), writes=("s0_wf",))
            P.op("v", lambda E, wfs=wfs: E.tensor_scalar(
                wfs, wfs, -8.0, 7.0, ALU.max, ALU.min),
                reads=("s0_wf",), writes=("s0_wf",))
            tq = P.op("v", lambda E, wfs=wfs, sns=sns, wbs=wbs, NB=NB:
                      E.tensor_tensor(
                          wbs.rearrange("p (n k) -> p n k", k=32),
                          wfs.rearrange("p (n k) -> p n k", k=32),
                          bc(sns), ALU.mult),
                      reads=("s0_wf", "s0_sca"), writes=("s0_wb",), inc="v0",
                      waits=[gqt.get((par, "wb")), vdt.get((par, "wbT"))])
            vdt[(par, "wf")] = tq
            transpose_store(par, I, dstT, k, 4 + (k % 2), tq,
                            tiled=(dstT.shape[0] <= 16))

    if "ffn0" in need:
        lpbq_weight(din["L0_ffn_w_up"], ds["w_upT"], HF)
        lpbq_weight(din["L0_ffn_w_gate"], ds["w_gateT"], HF)
        lpbq_weight(din["L0_ffn_w_down"], ds["w_downT"], DM)

    def tern_weight(src, dstT, O, I):
        KT = O // 128
        tprev = C.psum_free.get(6)
        nmm = KT * (I // 512)
        imm = 0
        for k in range(KT):
            par = k % 2
            tl = loadw(par, slice(par * 4096, par * 4096 + I),
                       src[k * 128:(k + 1) * 128, :], I)
            ta = P.op("s", lambda E, par=par, I=I: E.activation(
                wb[:, par * 4096:par * 4096 + I],
                wf[:, par * 4096:par * 4096 + I], AF.Abs),
                reads=("s0_wf",), writes=("s0_wb",), inc="s0",
                waits=[tl, vdt.get((par, "awb")), gqt.get((par, "wb")),
                       vdt.get((par, "wbT"))])
            for g in range(I // 512):
                P.op("t", lambda E, par=par, g=g, first=(imm == 0),
                     last=(imm == nmm - 1):
                     E.matmul(ps[6][0:1, 0:512], sb["onesb"][:, 0:1],
                              wb[:, par * 4096 + g * 512:
                                 par * 4096 + (g + 1) * 512],
                              start=first, stop=last),
                     inc="t0", waits=[ta, tprev])
                imm += 1
            vdt[(par, "awb")] = P.tok("t0")
            vdt[(par, "wf")] = vdt[(par, "awb")]
        tm = P.tok("t0")
        P.op("v", lambda E: E.reduce_sum(row[0:1, 0:1], ps[6][0:1, 0:512],
                                         AX.X), writes=("s0_row",),
             waits=[tm])
        P.op("v", lambda E, n=O * I: E.tensor_scalar(
            row[0:1, 1:2], row[0:1, 0:1], 1.0 / n, 1e-5, ALU.mult, ALU.max),
            reads=("s0_row",), writes=("s0_row",))
        P.op("v", lambda E: E.reciprocal(row[0:1, 2:3], row[0:1, 1:2]),
             reads=("s0_row",), writes=("s0_row",))
        tr = P.op("v", lambda E: E.tensor_scalar(
            row[0:1, 3:4], row[0:1, 1:2], 0.5, None, ALU.mult),
            reads=("s0_row",), writes=("s0_row",), inc="v0")
        tg = P.op("g", lambda E: E.partition_broadcast(row[:, 1:4],
                                                       row[0:1, 1:4]),
                  reads=("s0_row",), writes=("s0_row",), inc="g0",
                  waits=[tr])
        C.psum_free[6] = tg
        for k in range(KT):
            par = k % 2
            tl = loadw(par, slice(par * 4096, par * 4096 + I),
                       src[k * 128:(k + 1) * 128, :], I)
            wfs = wf[:, par * 4096:par * 4096 + I]
            wbs = wb[:, par * 4096:par * 4096 + I]
            ta = P.op("s", lambda E, wfs=wfs, wbs=wbs: E.activation(
                wbs, wfs, AF.Abs), reads=("s0_wf",), writes=("s0_wb",),
                inc="s0", waits=[tl, tg, vdt.get((par, "twb")),
                                 gqt.get((par, "wb")),
                                 vdt.get((par, "wbT"))])
            tsg = P.op("s", lambda E, wfs=wfs: E.sign(wfs, wfs),
                       reads=("s0_wf",), writes=("s0_wf",), inc="s0")
            tmsk = P.op("v", lambda E, wbs=wbs: E.tensor_scalar(
                wbs, wbs, row[:, 3:4], None, ALU.is_ge),
                reads=("s0_wb", "s0_row"), writes=("s0_wb",), waits=[ta],
                inc="v0")
            tq = P.op("v", lambda E, wbs=wbs, wfs=wfs: E.scalar_tensor_tensor(
                wbs, wbs, row[:, 1:2], wfs, ALU.mult, ALU.mult),
                reads=("s0_wb", "s0_wf", "s0_row"), writes=("s0_wb",),
                inc="v0", waits=[tsg])
            vdt[(par, "wf")] = tq
            vdt[(par, "twb")] = tq
            transpose_store(par, I, dstT, k, 4 + (k % 2), tq,
                            tiled=(dstT.shape[0] <= 16))

    if "ffn1" in need:
        tern_weight(din["L1_ffn_w1"], ds["w_w1T"], DH, DM)
        tern_weight(din["L1_ffn_w2"], ds["w_w2T"], DM, DH)
    st.close()
    return [P.tok(s) for s in sorted(P.dma_sems)
            if s.startswith(("gw", "gts"))]


def ln_rows(C, ps_sum, ps_sq, n_inv, rowc, tmm, tc=None):
    P, ps, rows = C.P, C.ps, C.rows
    Tc = tc if tc is not None else C.Tc
    m = rows[0:1, rowc:rowc + Tc]
    e2 = rows[0:1, rowc + Tc:rowc + 2 * Tc]
    r = rows[0:1, rowc + 2 * Tc:rowc + 3 * Tc]
    mr = rows[0:1, rowc + 3 * Tc:rowc + 4 * Tc]
    P.op("v", lambda E: E.tensor_scalar(m, ps_sum[0:1, 0:Tc], n_inv, None,
                                        ALU.mult),
         writes=("rows",), waits=[tmm])
    P.op("v", lambda E: E.tensor_scalar(e2, ps_sq[0:1, 0:Tc], n_inv, None,
                                        ALU.mult),
         reads=(), writes=("rows",))
    P.op("v", lambda E: E.tensor_tensor(r, m, m, ALU.mult),
         reads=("rows",), writes=("rows",))
    P.op("v", lambda E: E.scalar_tensor_tensor(r, e2, EPS, r, ALU.add,
                                               ALU.subtract),
         reads=("rows",), writes=("rows",))
    tv = P.op("v", lambda E: E.reciprocal(r, r), reads=("rows",),
              writes=("rows",), inc="v1")
    ts = P.op("s", lambda E: E.activation(r, r, AF.Sqrt), reads=("rows",),
              writes=("rows",), inc="s1", waits=[tv])
    tv2 = P.op("v", lambda E: E.tensor_tensor(mr, m, r, ALU.mult),
               reads=("rows",), writes=("rows",), inc="v1", waits=[ts])
    onesf = C.sb["onesf"]
    t4 = P.op("t", lambda E: E.matmul(ps[4][:, 0:Tc], onesf[0:1, 0:128], mr,
                                      start=True, stop=True), inc="t1",
              waits=[tv2, C.psum_free.get(4)])
    t5 = P.op("t", lambda E: E.matmul(ps[5][:, 0:Tc], onesf[0:1, 0:128], r,
                                      start=True, stop=True), inc="t1",
              waits=[tv2, C.psum_free.get(5)])
    return t4, t5


def s1_mamba(C):
    P, nc, ds, sb, ps, L = C.P, C.nc, C.ds, C.sb, C.ps, C.L
    Tc = 256
    ncpb = L // Tc
    steps = BPC * ncpb
    st = ExitStack()

    def S(name, shape, dt):
        return st.enter_context(nc.sbuf_tensor(name, shape, dt))
    xlnb = S("m_xlnb", [128, KM * Tc], BF16)
    sqb = S("m_sqb", [128, Tc], BF16)
    moch = S("m_moch", [128, KM * Tc], BF16)
    xtl = S("m_xtl", [128, 2 * Tc], F32)
    kxt = S("m_kxt", [128, 4 * Tc], F32)
    wipf = S("m_wipf", [128, 2 * 1024], BF16)
    adjf = S("m_adjf", [128, 2 * 2048], BF16)
    opf = S("m_opf", [128, 2 * 2048], BF16)
    xirch = S("m_xirch", [128, KI * (Tc + 3)], BF16)
    sgt = S("m_sgt", [128, 2 * Tc], F32)
    xdbd = S("m_xdbd", [64, Tc], BF16)
    bcb = S("m_bcb", [32, Tc], BF16)
    hpl = S("m_hpl", [128, NST * Tc], BF16)
    ab = S("m_ab", [128, 8 * Tc], BF16)
    bb = S("m_bb", [128, NST * Tc], BF16)
    yb = S("m_yb", [128, Tc], F32)
    hplF = hpl.ap().bitcast(F32)
    bbF = bb.ap().bitcast(F32)
    HPLN = tuple(f"hpl{n}" for n in range(NST))
    BBN = tuple(f"bb{n}" for n in range(NST))
    XLN = tuple(f"xlnb{k}" for k in range(KM))
    xibq = [S(f"m_xib{q}", [128, KI * Tc], BF16) for q in (0, 1)]
    sresq = [S(f"m_sres{q}", [128, KI * Tc], BF16) for q in (0, 1)]
    dltq = [S(f"m_dlt{q}", [128, KI * Tc], BF16) for q in (0, 1)]
    dubq = [S(f"m_dub{q}", [128, KI * Tc], BF16) for q in (0, 1)]
    dly2q = [S(f"m_dly2{q}", [128, KI * Tc], BF16) for q in (0, 1)]
    Brepq = [S(f"m_Brep{q}", [128, NST * Tc], BF16) for q in (0, 1)]
    Crepq = [S(f"m_Crep{q}", [128, NST * Tc], BF16) for q in (0, 1)]
    negA, Dsb, cwsb, cbsb, dtbn = (sb[x] for x in
                                   ("negA", "Dsb", "cwsb", "cbsb", "dtbn"))
    hcar, tails, xprj, dtprj, selb = (sb[x] for x in
                                      ("hcar", "tails", "xprj", "dtprj",
                                       "selb"))
    tx = P.op("q", _dma(xprj.ap().rearrange("p (k c) -> p k c", c=96),
                        ds["w_xproj"].rearrange("(k p) c -> p k c", p=128)),
              inc="qw0", inc_n=16, waits=list(C.w_done))
    td_ = P.op("q", _dma(dtprj[:], ds["w_dtproj"][:, :]), inc="qw1",
               inc_n=16)
    C.s1_resident = (tx, td_)

    rot = {}
    fr = [{}, {}]
    rel = [None, None]

    def front_A_stats(s):
        b, c = divmod(s, ncpb)
        cs = slice(c * Tc, (c + 1) * Tc)
        first = True
        for k in range(KM):
            par = k % 2
            tl = P.op("q", _dma(xtl[:, par * Tc:(par + 1) * Tc],
                                ds["resid"][b, k, :, cs]),
                      inc=f"qt{par}", inc_n=16,
                      waits=[rot.get(("xtl", par)), rot.get(("xtl2", par))])
            tc1 = P.op("v", lambda E, k=k, par=par: E.tensor_copy(
                xlnb[:, k * Tc:(k + 1) * Tc],
                xtl[:, par * Tc:(par + 1) * Tc]),
                writes=(f"xlnb{k}",), inc="v1",
                waits=[tl] + ([rot.get("xlnb_user")] if k == 0 else []),
                clears=XLN if k == 0 else ())
            tc2 = P.op("s", lambda E, par=par: E.activation(
                sqb[:], xtl[:, par * Tc:(par + 1) * Tc], AF.Square),
                writes=("sqb",), inc="s1",
                waits=[tl, rot.get("sqb_user")], clears=("sqb",))
            rot[("xtl", par)] = tc1
            rot[("xtl2", par)] = tc2
            P.op("t", lambda E, k=k, first=first: E.matmul(
                ps[6][0:1, 0:Tc], sb["onesb"][:, 0:1],
                xlnb[:, k * Tc:(k + 1) * Tc], start=first, stop=(k == KM - 1)),
                inc="t1", waits=[tc1] + ([C.psum_free.get(6)]
                                         if first else []))
            t7 = P.op("t", lambda E, first=first, k=k: E.matmul(
                ps[7][0:1, 0:Tc], sb["onesb"][:, 0:1], sqb[:],
                start=first, stop=(k == KM - 1)),
                inc="t1", waits=[tc2] + ([C.psum_free.get(7)]
                                         if first else []))
            rot["sqb_user"] = t7
            first = False
        return P.tok("t1")

    def front_A_rows(s, tmm):
        t4, t5 = ln_rows(C, ps[6], ps[7], 1.0 / DM, 0, tmm, tc=Tc)
        C.psum_free[6] = P.tok("v1")
        C.psum_free[7] = C.psum_free[6]
        tnorm = None
        for k in range(KM):
            P.op("v", lambda E, k=k: E.tensor_tensor(
                xlnb[:, k * Tc:(k + 1) * Tc],
                xlnb[:, k * Tc:(k + 1) * Tc], ps[5][:, 0:Tc], ALU.mult),
                reads=(f"xlnb{k}",), writes=(f"xlnb{k}",), waits=[t4, t5])
        for k in range(KM):
            tnorm = P.op("v", lambda E, k=k: E.tensor_tensor(
                xlnb[:, k * Tc:(k + 1) * Tc],
                xlnb[:, k * Tc:(k + 1) * Tc], ps[4][:, 0:Tc], ALU.subtract),
                reads=(f"xlnb{k}",), writes=(f"xlnb{k}",),
                inc="v1" if k == KM - 1 else None)
        C.psum_free[4] = tnorm
        C.psum_free[5] = tnorm
        return tnorm

    def front_Bpart(s, part, tnorm):
        q = s % 2
        xib, sres = xibq[q], sresq[q]
        relw = list(rel[q]) if rel[q] else []
        for f in range(4 * part, 4 * part + 4):
            par = pi = f % 2
            tw = P.op("q", lambda E, f=f, par=par: E.dma_start(
                out=wipf[:, par * 1024:(par + 1) * 1024],
                in_=ds["w_inproj"][f]),
                inc=f"qp{par}", inc_n=16, waits=[rot.get(("wipf", par))])
            tmm = None
            for k in range(KM):
                tmm = P.op("t", lambda E, k=k, par=par, pi=pi: E.matmul(
                    ps[pi][:, 0:Tc],
                    wipf[:, par * 1024 + k * 128:par * 1024 + (k + 1) * 128],
                    xlnb[:, k * Tc:(k + 1) * Tc],
                    start=(k == 0), stop=(k == KM - 1)),
                    waits=([tw, tnorm, C.psum_free.get(pi)]
                           if k == 0 else []),
                    inc="t1" if k == KM - 1 else None)
            rot[("wipf", par)] = tmm
            if f < KI:
                te = P.op("s", lambda E, f=f, pi=pi: E.activation(
                    xirch[:, f * (Tc + 3) + 3:(f + 1) * (Tc + 3)],
                    ps[pi][:, 0:Tc], AF.Copy),
                    writes=("xirch",), inc="s1",
                    waits=[tmm] + ([rot.get("xirch_user")] if f == 0 else []),
                    clears=("xirch",) if f == 0 else ())
                fr[q][("xiev", f)] = te
            else:
                f2 = f - KI
                te = P.op("s", lambda E, f2=f2, pi=pi, sres=sres:
                          E.activation(
                              sres[:, f2 * Tc:(f2 + 1) * Tc],
                              ps[pi][:, 0:Tc], AF.Silu),
                          writes=(f"sres{q}",), inc="s1",
                          waits=([tmm]
                                 + ([rot.get(("sresK", q))] + relw
                                    if f2 == 0 else [])),
                          clears=(f"sres{q}",) if f2 == 0 else ())
                if f2 == KI - 1:
                    fr[q]["sres"] = te
            C.psum_free[pi] = te
        if part == 7:
            rot["xlnb_user"] = tmm
        if part == 0:
            for f2 in (0, 1):
                fr[q][("adjld", f2)] = P.op(
                    "q", lambda E, f2=f2: E.dma_start(
                        out=adjf[:, f2 * 2048:(f2 + 1) * 2048],
                        in_=ds["w_adjn"][f2]),
                    inc=f"qa{f2}", inc_n=16, waits=[rot.get(("adjf", f2))])

    def back_H_k(t, k):
        q = t % 2
        b, c = divmod(t, ncpb)
        xib, sres, dlt, dub = xibq[q], sresq[q], dltq[q], dubq[q]
        dly2, Brep, Crep = dly2q[q], Brepq[q], Crepq[q]
        frq = fr[q]
        if k == 0 and c == 0:
            P.op("v", lambda E: E.memset(hcar[:], 0.0), writes=("hcar",))
        def emit_exps(n0):
            last = None
            for n in range(n0, n0 + 8):
                r = n % 8
                last = P.op("s", lambda E, k=k, n=n, r=r, dlt=dlt:
                            E.activation(
                                ab[:, r * Tc:(r + 1) * Tc],
                                dlt[:, k * Tc:(k + 1) * Tc], AF.Exp,
                                scale=negA[:, k * NST + n:k * NST + n + 1]),
                            reads=(f"dlt{q}",), writes=(f"ab{r}",), inc="s1",
                            waits=[rot.get(("ab", r))]
                            + ([frq.get("dlt")]
                               if k == 0 and n == 0 else []),
                            clears=(f"ab{r}",))
            return last

        ta7 = emit_exps(0)
        for n in range(NST):
            P.op("v", lambda E, k=k, n=n, dub=dub, Brep=Brep:
                 E.tensor_tensor(
                     bb[:, n * Tc:(n + 1) * Tc],
                     dub[:, k * Tc:(k + 1) * Tc],
                     Brep[:, n * Tc:(n + 1) * Tc], ALU.mult),
                 reads=(f"dub{q}", f"Brep{q}"), writes=(f"bb{n}",),
                 waits=[frq.get("dub"), frq.get("Brep")]
                 if k == 0 and n == 0 else ())
        ta15 = None
        for n in range(NST):
            r = n % 8
            tsc = P.op("v", lambda E, k=k, n=n, r=r:
                       E.tensor_tensor_scan(
                           hpl[:, n * Tc:(n + 1) * Tc],
                           ab[:, r * Tc:(r + 1) * Tc],
                           bb[:, n * Tc:(n + 1) * Tc],
                           hcar[:, k * NST + n:k * NST + n + 1],
                           ALU.mult, ALU.add),
                       reads=(f"bb{n}", "hcar"), writes=(f"hpl{n}",),
                       inc="v1",
                       waits=[ta7] if n == 0 else ([ta15] if n == 8 else ()))
            rot[("ab", r)] = tsc
            if n == 7:
                rot[("ab", "half")] = tsc
                ta15 = emit_exps(8)
        P.op("v", lambda E, k=k: E.tensor_copy(
            hcar[:, k * NST:(k + 1) * NST],
            hpl.ap().rearrange("p (n t) -> p n t", n=NST)[:, :, Tc - 1]),
            reads=HPLN, writes=("hcar",))
        for n in range(NST):
            P.op("v", lambda E, n=n, Crep=Crep: E.tensor_tensor(
                bb[:, n * Tc:(n + 1) * Tc],
                hpl[:, n * Tc:(n + 1) * Tc],
                Crep[:, n * Tc:(n + 1) * Tc], ALU.mult),
                reads=(f"hpl{n}", f"Crep{q}"), writes=(f"bb{n}",),
                waits=[frq.get("Crep")] if k == 0 and n == 0 else ())
        for j in range(8):
            P.op("v", lambda E, j=j: E.tensor_tensor(
                hplF[:, j * Tc:(j + 1) * Tc],
                bb[:, 2 * j * Tc:(2 * j + 1) * Tc],
                bb[:, (2 * j + 1) * Tc:(2 * j + 2) * Tc], ALU.add),
                reads=(f"bb{2*j}", f"bb{2*j+1}"),
                writes=(f"hpl{2*j}", f"hpl{2*j+1}"))
        for j in range(4):
            P.op("v", lambda E, j=j: E.tensor_tensor(
                bbF[:, j * Tc:(j + 1) * Tc],
                hplF[:, 2 * j * Tc:(2 * j + 1) * Tc],
                hplF[:, (2 * j + 1) * Tc:(2 * j + 2) * Tc], ALU.add),
                reads=tuple(f"hpl{4*j+i}" for i in range(4)),
                writes=(f"bb{2*j}", f"bb{2*j+1}"))
        for j in range(2):
            P.op("v", lambda E, j=j: E.tensor_tensor(
                hplF[:, j * Tc:(j + 1) * Tc],
                bbF[:, 2 * j * Tc:(2 * j + 1) * Tc],
                bbF[:, (2 * j + 1) * Tc:(2 * j + 2) * Tc], ALU.add),
                reads=tuple(f"bb{4*j+i}" for i in range(4)),
                writes=(f"hpl{2*j}", f"hpl{2*j+1}"))
        P.op("v", lambda E: E.tensor_tensor(
            yb[:], hplF[:, 0:Tc], hplF[:, Tc:2 * Tc], ALU.add),
            reads=("hpl0", "hpl1", "hpl2", "hpl3"), writes=("yb",))
        P.op("v", lambda E, k=k, xib=xib: E.scalar_tensor_tensor(
            yb[:], xib[:, k * Tc:(k + 1) * Tc], Dsb[:, k:k + 1],
            yb[:], ALU.mult, ALU.add),
            reads=("yb", f"xib{q}"), writes=("yb",),
            waits=[frq.get("xib")] if k == 0 else ())
        ty2 = P.op("v", lambda E, k=k, dly2=dly2, sres=sres: E.tensor_tensor(
            dly2[:, k * Tc:(k + 1) * Tc], yb[:],
            sres[:, k * Tc:(k + 1) * Tc], ALU.mult),
            reads=("yb", f"sres{q}"), writes=(f"dly2{q}",), inc="v1",
            waits=[frq.get("sres"), frq.get("deltap_mms")] if k == 0 else ())
        return ty2 if k == NST - 1 else None

    def front_piece(s, k):
        q = s % 2
        b, c = divmod(s, ncpb)
        frq = fr[q]
        relw = list(rel[q]) if rel[q] else []
        xib, sres, dlt, dub = xibq[q], sresq[q], dltq[q], dubq[q]
        dly2, Brep, Crep = dly2q[q], Brepq[q], Crepq[q]
        if k < 8:
            if k == 0 and c == 0:
                P.op("v", lambda E: E.memset(tails[:], 0.0),
                     writes=tuple(f"tails{f}" for f in range(KI)),
                     waits=[rot.get("tails_user")])
            for f in (2 * k, 2 * k + 1):
                xs = xirch[:, f * (Tc + 3):(f + 1) * (Tc + 3)]
                P.op("v", lambda E, xs=xs, f=f: E.tensor_copy(
                    xs[:, 0:3], tails[:, f * 3:(f + 1) * 3]),
                    reads=(f"tails{f}",), writes=("xirchT",), inc="v1",
                    waits=[frq[("xiev", f)]])
            for f in (2 * k, 2 * k + 1):
                xs = xirch[:, f * (Tc + 3):(f + 1) * (Tc + 3)]
                par = f % 2
                P.op("v", lambda E, xs=xs, f=f, par=par: E.tensor_scalar(
                    sgt[:, par * Tc:(par + 1) * Tc], xs[:, 0:Tc],
                    cwsb[:, f * 4:f * 4 + 1], None, ALU.mult),
                    reads=("xirchT",), writes=(f"sgt{par}",),
                    waits=[rot.get(("sgt", par))], clears=(f"sgt{par}",))
                tcv = None
                for kk in range(1, 4):
                    tcv = P.op("v", lambda E, xs=xs, f=f, kk=kk, par=par:
                               E.scalar_tensor_tensor(
                                   sgt[:, par * Tc:(par + 1) * Tc],
                                   xs[:, kk:kk + Tc],
                                   cwsb[:, f * 4 + kk:f * 4 + kk + 1],
                                   sgt[:, par * Tc:(par + 1) * Tc],
                                   ALU.mult, ALU.add),
                               reads=(f"sgt{par}",), writes=(f"sgt{par}",),
                               inc="v1" if kk == 3 else None)
                tt1 = P.op("v", lambda E, xs=xs, f=f: E.tensor_copy(
                    tails[:, f * 3:(f + 1) * 3], xs[:, Tc:Tc + 3]),
                    reads=(), writes=(), inc="v1")
                rot["tails_user"] = tt1
                ts = P.op("s", lambda E, f=f, par=par, xib=xib: E.activation(
                    xib[:, f * Tc:(f + 1) * Tc],
                    sgt[:, par * Tc:(par + 1) * Tc], AF.Silu,
                    bias=cbsb[:, f:f + 1]),
                    reads=(f"sgt{par}",), writes=(f"xib{q}",), inc="s1",
                    waits=[tcv] + ([rot.get(("sresK", q))] + relw
                                   if f == 0 else []),
                    clears=(f"xib{q}",) if f == 0 else ())
                rot[("sgt", par)] = ts
                if f == KI - 1:
                    frq["xib"] = ts
            if k == 7:
                rot["xirch_user"] = P.tok("v1")
        elif k == 8:
            txib = frq["xib"]
            tA = tB = None
            for kk in range(KI):
                tA = P.op("t", lambda E, kk=kk, xib=xib: E.matmul(
                    ps[2][0:64, 0:Tc], xprj[:, kk * 96:kk * 96 + 64],
                    xib[:, kk * Tc:(kk + 1) * Tc],
                    start=(kk == 0), stop=(kk == KI - 1)),
                    waits=([txib, C.psum_free.get(2), C.s1_resident[0]]
                           if kk == 0 else []),
                    inc="t1" if kk == KI - 1 else None)
            for kk in range(KI):
                tB = P.op("t", lambda E, kk=kk, xib=xib: E.matmul(
                    ps[3][0:32, 0:Tc], xprj[:, kk * 96 + 64:(kk + 1) * 96],
                    xib[:, kk * Tc:(kk + 1) * Tc],
                    start=(kk == 0), stop=(kk == KI - 1)),
                    waits=[C.psum_free.get(3)] if kk == 0 else [],
                    inc="t1" if kk == KI - 1 else None)
            tev1 = P.op("s", lambda E: E.activation(xdbd[:], ps[2][0:64, 0:Tc],
                                                    AF.Copy),
                        writes=("xdbd",), inc="s1",
                        waits=[tA, rot.get("xdbd_user")])
            tev2 = P.op("s", lambda E: E.activation(bcb[:], ps[3][0:32, 0:Tc],
                                                    AF.Copy),
                        writes=("bcb",), inc="s1",
                        waits=[tB, rot.get("bcb_user")])
            C.psum_free[2] = tev1
            C.psum_free[3] = tev2
            frq["tev1"] = tev1
            frq["tev2"] = tev2
        elif k == 9:
            tev1 = frq["tev1"]
            for f in range(KI):
                pi = par = f % 2
                tmm = P.op("t", lambda E, f=f, pi=pi: E.matmul(
                    ps[pi][:, 0:Tc], dtprj[:, f * 128:(f + 1) * 128],
                    xdbd[:], start=True, stop=True),
                    waits=[tev1, C.psum_free.get(pi), C.s1_resident[1]],
                    inc="t1")
                tsg = P.op("s", lambda E, pi=pi, f=f, par=par: E.activation(
                    sgt[:, par * Tc:(par + 1) * Tc], ps[pi][:, 0:Tc],
                    AF.Sigmoid, scale=-1.0, bias=dtbn[:, f:f + 1]),
                    writes=(f"sgt{par}",), inc="s1",
                    waits=[tmm, rot.get(("sgt", par))],
                    clears=(f"sgt{par}",))
                C.psum_free[pi] = tsg
                tln = P.op("s", lambda E, f=f, par=par, dly2=dly2:
                           E.activation(
                               dly2[:, f * Tc:(f + 1) * Tc],
                               sgt[:, par * Tc:(par + 1) * Tc], AF.Ln),
                           reads=(f"sgt{par}",), writes=(f"dly2{q}",),
                           inc="s1", waits=relw if f == 0 else (),
                           clears=(f"dly2{q}",) if f == 0 else ())
                rot[("sgt", par)] = tln
            frq["dL"] = P.tok("s1")
        elif k in (10, 11):
            tdl = frq["dL"]
            for f in range(8 * (k - 10), 8 * (k - 9)):
                par = pi = f % 2
                tmm = None
                for kk in range(KI):
                    tmm = P.op("t", lambda E, kk=kk, par=par, pi=pi,
                               dly2=dly2: E.matmul(
                                   ps[pi][:, 0:Tc],
                                   adjf[:, par * 2048 + kk * 128:
                                        par * 2048 + (kk + 1) * 128],
                                   dly2[:, kk * Tc:(kk + 1) * Tc],
                                   start=(kk == 0), stop=(kk == KI - 1)),
                               waits=([frq[("adjld", f)], tdl,
                                       C.psum_free.get(pi)]
                                      if kk == 0 else []),
                               inc="t1" if kk == KI - 1 else None)
                if f + 2 < KI:
                    frq[("adjld", f + 2)] = P.op(
                        "q", lambda E, f=f, par=par: E.dma_start(
                            out=adjf[:, par * 2048:(par + 1) * 2048],
                            in_=ds["w_adjn"][f + 2]),
                        inc=f"qa{par}", inc_n=16, waits=[tmm])
                else:
                    rot[("adjf", par)] = tmm
                te = P.op("s", lambda E, f=f, pi=pi, dlt=dlt: E.activation(
                    dlt[:, f * Tc:(f + 1) * Tc], ps[pi][:, 0:Tc], AF.Copy),
                    writes=(f"dlt{q}",), inc="s1",
                    waits=[tmm] + (relw if f == 0 else []),
                    clears=(f"dlt{q}",) if f == 0 else ())
                C.psum_free[pi] = te
            if k == 11:
                frq["dlt"] = P.tok("s1")
                frq["deltap_mms"] = P.tok("t1")
        elif k == 12:
            tev2 = frq["tev2"]
            for n in range(NST):
                pi = n % 2
                tm1 = P.op("t", lambda E, n=n, pi=pi: E.matmul(
                    ps[pi][:, 0:Tc], selb[:, n * 128:(n + 1) * 128], bcb[:],
                    start=True, stop=True), inc="t1",
                    waits=[tev2, C.psum_free.get(pi)])
                te1 = P.op("s", lambda E, n=n, pi=pi, Brep=Brep: E.activation(
                    Brep[:, n * Tc:(n + 1) * Tc], ps[pi][:, 0:Tc], AF.Copy),
                    writes=(f"Brep{q}",), inc="s1",
                    waits=[tm1] + (relw if n == 0 else []),
                    clears=(f"Brep{q}",) if n == 0 else ())
                C.psum_free[pi] = te1
                tm2 = P.op("t", lambda E, n=n, pi=pi: E.matmul(
                    ps[pi][:, 0:Tc], selb[:, (16 + n) * 128:(17 + n) * 128],
                    bcb[:], start=True, stop=True), inc="t1",
                    waits=[C.psum_free.get(pi)])
                te2 = P.op("s", lambda E, n=n, pi=pi, Crep=Crep: E.activation(
                    Crep[:, n * Tc:(n + 1) * Tc], ps[pi][:, 0:Tc], AF.Copy),
                    writes=(f"Crep{q}",), inc="s1",
                    waits=[tm2] + (relw if n == 0 else []),
                    clears=(f"Crep{q}",) if n == 0 else ())
                C.psum_free[pi] = te2
            frq["Brep"] = P.tok("s1")
            frq["Crep"] = frq["Brep"]
            rot["bcb_user"] = frq["Brep"]
            rot["xdbd_user"] = frq["Brep"]
        elif k == 13:
            tdu = None
            for kk in range(KI):
                tdu = P.op("v", lambda E, kk=kk, dub=dub, dlt=dlt, xib=xib:
                           E.tensor_tensor(
                               dub[:, kk * Tc:(kk + 1) * Tc],
                               dlt[:, kk * Tc:(kk + 1) * Tc],
                               xib[:, kk * Tc:(kk + 1) * Tc], ALU.mult),
                           reads=(f"dlt{q}", f"xib{q}"), writes=(f"dub{q}",),
                           waits=[frq["dlt"]] + (relw if kk == 0 else []),
                           clears=(f"dub{q}",) if kk == 0 else (),
                           inc="v1" if kk == KI - 1 else None)
            frq["dub"] = tdu

    def back_I(t, ty2):
        q = t % 2
        dly2 = dly2q[q]
        ld = {}
        for f in (0, 1):
            ld[f] = P.op("q", lambda E, f=f: E.dma_start(
                out=opf[:, f * 2048:(f + 1) * 2048], in_=ds["w_outproj"][f]),
                inc=f"qo{f}", inc_n=16, waits=[rot.get(("opf", f))])
        for f in range(KM):
            par = pi = f % 2
            tmm = None
            for k in range(KI):
                tmm = P.op("t", lambda E, k=k, pi=pi, par=par, dly2=dly2:
                           E.matmul(
                               ps[pi][:, 0:Tc],
                               opf[:, par * 2048 + k * 128:
                                   par * 2048 + (k + 1) * 128],
                               dly2[:, k * Tc:(k + 1) * Tc],
                               start=(k == 0), stop=(k == KI - 1)),
                           waits=([ld[f], ty2, C.psum_free.get(pi)]
                                  if k == 0 else []),
                           inc="t1" if k == KI - 1 else None)
            if f + 2 < KM:
                ld[f + 2] = P.op("q", lambda E, f=f, par=par: E.dma_start(
                    out=opf[:, par * 2048:(par + 1) * 2048],
                    in_=ds["w_outproj"][f + 2]),
                    inc=f"qo{par}", inc_n=16, waits=[tmm])
            else:
                rot[("opf", par)] = tmm
            te = P.op("s", lambda E, f=f, pi=pi: E.activation(
                moch[:, f * Tc:(f + 1) * Tc], ps[pi][:, 0:Tc], AF.Copy),
                writes=("moch",), inc="s1",
                waits=[tmm] + ([rot.get("moch_user")] if f == 0 else []),
                clears=("moch",) if f == 0 else ())
            C.psum_free[pi] = te
        tmo_mms = P.tok("t1")
        tmo = P.tok("s1")
        rel[q] = (ty2, tmo_mms)
        return tmo

    def back_JK(t, ty2, tmo):
        q = t % 2
        b, c = divmod(t, ncpb)
        cs = slice(c * Tc, (c + 1) * Tc)
        sres = sresq[q]
        first = True
        for k in range(KM):
            P.op("t", lambda E, k=k, first=first: E.matmul(
                ps[2][0:1, 0:Tc], sb["onesb"][:, 0:1],
                moch[:, k * Tc:(k + 1) * Tc], start=first,
                stop=(k == KM - 1)),
                inc="t1", waits=[tmo] + ([C.psum_free.get(2)]
                                         if first else []))
            tsq = P.op("s", lambda E, k=k: E.activation(
                sqb[:], moch[:, k * Tc:(k + 1) * Tc], AF.Square),
                writes=("sqb",), inc="s1",
                waits=[tmo, rot.get("sqb_user")], clears=("sqb",))
            t7 = P.op("t", lambda E, first=first, k=k: E.matmul(
                ps[3][0:1, 0:Tc], sb["onesb"][:, 0:1], sqb[:],
                start=first, stop=(k == KM - 1)), inc="t1",
                waits=[tsq] + ([C.psum_free.get(3)] if first else []))
            rot["sqb_user"] = t7
            first = False
        tmm = P.tok("t1")
        t4, t5 = ln_rows(C, ps[2], ps[3], 1.0 / DM, 1040, tmm, tc=Tc)
        C.psum_free[2] = P.tok("v1")
        C.psum_free[3] = C.psum_free[2]
        xnew = sres.ap().bitcast(F32)
        tadd = None
        tld = {}
        for k in range(4):
            tld[k] = P.op("q", _dma(kxt[:, k * Tc:(k + 1) * Tc],
                                    ds["resid"][b, k, :, cs]),
                          inc=f"qk{k}", inc_n=16,
                          waits=[rot.get(("kxt", k))])
        for k in range(KM):
            P.op("v", lambda E, k=k, xnew=xnew: E.tensor_tensor(
                xnew[:, k * Tc:(k + 1) * Tc],
                moch[:, k * Tc:(k + 1) * Tc], ps[5][:, 0:Tc], ALU.mult),
                reads=("moch",), writes=(f"xnK{k}",),
                waits=[t4, t5, ty2] if k == 0 else (),
                clears=(f"sres{q}",) if k == 0 else ())
        for k in range(KM):
            P.op("v", lambda E, k=k, xnew=xnew: E.tensor_tensor(
                xnew[:, k * Tc:(k + 1) * Tc],
                xnew[:, k * Tc:(k + 1) * Tc], ps[4][:, 0:Tc],
                ALU.subtract),
                reads=(f"xnK{k}",), writes=(f"xnK{k}",))
        for k in range(KM):
            par = k % 4
            tadd = P.op("v", lambda E, k=k, par=par, xnew=xnew:
                        E.tensor_tensor(
                            xnew[:, k * Tc:(k + 1) * Tc],
                            xnew[:, k * Tc:(k + 1) * Tc],
                            kxt[:, par * Tc:(par + 1) * Tc], ALU.add),
                        reads=(f"xnK{k}",), writes=(f"xnK{k}",), inc="v1",
                        waits=[tld[k]])
            rot[("kxt", par)] = tadd
            if k + 4 < KM:
                tld[k + 4] = P.op(
                    "q", _dma(kxt[:, par * Tc:(par + 1) * Tc],
                              ds["resid"][b, k + 4, :, cs]),
                    inc=f"qk{par}", inc_n=16, waits=[tadd])
        C.psum_free[4] = tadd
        C.psum_free[5] = tadd
        tst = P.op("s", lambda E, b=b, cs=cs, xnew=xnew: E.dma_start(
            out=ds["resid"][b, :, :, cs].rearrange("k p t -> p k t"),
            in_=xnew.rearrange("p (k t) -> p k t", k=KM)),
            inc="gr0", inc_n=16, waits=[tadd])
        rot["moch_user"] = tst
        rot[("sresK", q)] = tst

    pend = {}
    for s in range(steps + 1):
        tnorm = None
        if s < steps:
            if s not in pend:
                pend[s] = front_A_stats(s)
            tnorm = front_A_rows(s, pend.pop(s))
        ty2 = None
        for k in range(NST):
            if s >= 1:
                tk = back_H_k(s - 1, k)
                if tk is not None:
                    ty2 = tk
            if s < steps:
                if k < 8:
                    front_Bpart(s, k, tnorm)
                front_piece(s, k)
            if k == 14 and s + 1 < steps:
                pend[s + 1] = front_A_stats(s + 1)
        if s >= 1:
            tmo = back_I(s - 1, ty2)
            back_JK(s - 1, ty2, tmo)
    st.close()
    return P.tok("gr0")

def s_out(C, done_tok, src="resid"):
    P, nc, ds, sb, ps, L = C.P, C.nc, C.ds, C.sb, C.ps, C.L
    st = ExitStack()
    xf = st.enter_context(nc.sbuf_tensor("o_xf", [128, 2 * KM * 128], F32))
    ot = st.enter_context(nc.sbuf_tensor("o_ot", [128, 2 * DM], F32))
    rot = {}
    for b in range(BPC):
        for tt in range(L // 128):
            par = tt % 2
            tl = P.op("q", lambda E, b=b, tt=tt, par=par: E.dma_start(
                out=xf[:, par * KM * 128:(par + 1) * KM * 128]
                .rearrange("p (k t) -> p k t", t=128),
                in_=ds[src][b, :, :, tt * 128:(tt + 1) * 128]
                .rearrange("k p t -> p k t")),
                inc=f"qf{par}", inc_n=16,
                waits=[done_tok, rot.get(("xf", par))])
            for half in range(2):
                pi = half
                tmm = None
                for j in range(4):
                    k = half * 4 + j
                    tmm = P.op("t", lambda E, k=k, j=j, par=par, pi=pi:
                               E.transpose(
                                   ps[pi][:, j * 128:(j + 1) * 128],
                                   xf[:, par * KM * 128 + k * 128:
                                      par * KM * 128 + (k + 1) * 128],
                                   sb["idf"][:]),
                               waits=([tl, C.psum_free.get(pi)]
                                      if j == 0 else []),
                               inc="t1" if j == 3 else None)
                te = P.op("s", lambda E, par=par, half=half, pi=pi:
                          E.activation(
                              ot[:, par * DM + half * 512:
                                 par * DM + (half + 1) * 512],
                              ps[pi][:, 0:512], AF.Copy),
                          writes=("o_ot",), inc="s1",
                          waits=[tmm, rot.get(("ot", par))])
                C.psum_free[pi] = te
            tev = P.tok("s1")
            rot[("xf", par)] = tev
            ts = P.op("s", lambda E, b=b, tt=tt, par=par: E.dma_start(
                out=C.out_d[b, tt * 128:(tt + 1) * 128, :],
                in_=ot[:, par * DM:(par + 1) * DM]),
                inc=f"go{par}", inc_n=16, waits=[tev])
            rot[("ot", par)] = ts
    st.close()
    P.dma_wait_all("g", prefixes=("go",))


def emit_postln_add(C, srcname, srcch, b, cs, rot, tready):
    P, ps, sb, Tc = C.P, C.ps, C.sb, C.Tc
    xtl, sgt = C.xtl, C.sgt
    first = True
    for k in range(KM):
        t6 = P.op("t", lambda E, k=k, first=first: E.matmul(
            ps[6][0:1, 0:Tc], sb["onesf"][:, 0:1],
            srcch[:, k * Tc:(k + 1) * Tc], start=first, stop=(k == KM - 1)),
            inc="t1", waits=[tready] + ([C.psum_free.get(6)]
                                        if first else []))
        tsq = P.op("s", lambda E, k=k: E.activation(
            sgt[:, 0:Tc], srcch[:, k * Tc:(k + 1) * Tc], AF.Square),
            writes=(C.sgtname,), inc="s1",
            waits=[tready, rot.get("sgt_user")], clears=(C.sgtname,))
        t7 = P.op("t", lambda E, first=first, k=k: E.matmul(
            ps[7][0:1, 0:Tc], sb["onesf"][:, 0:1], sgt[:, 0:Tc],
            start=first, stop=(k == KM - 1)), inc="t1",
            waits=[tsq] + ([C.psum_free.get(7)] if first else []))
        rot["sgt_user"] = t7
        first = False
    tmm = P.tok("t1")
    t4, t5 = ln_rows(C, ps[6], ps[7], 1.0 / DM, 0, tmm)
    C.psum_free[6] = P.tok("v1")
    C.psum_free[7] = C.psum_free[6]
    tld = {}
    for k in range(4):
        tld[k] = P.op("q", _dma(xtl[:, k * Tc:(k + 1) * Tc],
                                C.ds["resid"][b, k, :, cs]),
                      inc=f"qz{k}", inc_n=16,
                      waits=[rot.get(("xtl", k))])
    for k in range(KM):
        P.op("v", lambda E, k=k: E.tensor_tensor(
            srcch[:, k * Tc:(k + 1) * Tc], srcch[:, k * Tc:(k + 1) * Tc],
            ps[5][:, 0:Tc], ALU.mult),
            reads=(srcname,), writes=(f"plnK{k}",),
            waits=[t4, t5] if k == 0 else ())
    for k in range(KM):
        P.op("v", lambda E, k=k: E.tensor_tensor(
            srcch[:, k * Tc:(k + 1) * Tc], srcch[:, k * Tc:(k + 1) * Tc],
            ps[4][:, 0:Tc], ALU.subtract),
            reads=(f"plnK{k}",), writes=(f"plnK{k}",))
    ta = None
    for k in range(KM):
        par = k % 4
        ta = P.op("v", lambda E, k=k, par=par: E.tensor_tensor(
            srcch[:, k * Tc:(k + 1) * Tc], srcch[:, k * Tc:(k + 1) * Tc],
            xtl[:, par * Tc:(par + 1) * Tc], ALU.add),
            reads=(f"plnK{k}",), writes=(f"plnK{k}",), inc="v1",
            waits=[tld[k]])
        rot[("xtl", par)] = ta
        if k + 4 < KM:
            tld[k + 4] = P.op("q", _dma(xtl[:, par * Tc:(par + 1) * Tc],
                                        C.ds["resid"][b, k + 4, :, cs]),
                              inc=f"qz{par}", inc_n=16, waits=[ta])
    tadd = P.tok("v1")
    C.psum_free[4] = tadd
    C.psum_free[5] = tadd
    tst = P.op("s", lambda E, b=b, cs=cs: E.dma_start(
        out=C.ds["resid"][b, :, :, cs].rearrange("k p t -> p k t"),
        in_=srcch.ap().rearrange("p (k t) -> p k t", k=KM)),
        inc="gr0", inc_n=16, waits=[tadd])
    return tadd, tst


def emit_qdq_pp(C, src, dst, inv_col, s_col, lo, hi, tmp, tmpname,
                reads=(), writes=(), waits=()):
    P = C.P
    M = 12582912.0
    P.op("v", lambda E: E.tensor_scalar(tmp, src, inv_col, M, ALU.mult,
                                        ALU.add),
         reads=reads, writes=(tmpname,), waits=waits)
    P.op("v", lambda E: E.tensor_scalar(tmp, tmp, M, float(lo),
                                        ALU.subtract, ALU.max),
         reads=(tmpname,), writes=(tmpname,))
    return P.op("v", lambda E: E.tensor_scalar(dst, tmp, float(hi), s_col,
                                               ALU.min, ALU.mult),
                reads=(tmpname,), writes=writes, inc="v1")

def emit_qdq_bc(C, src, dst, sinv_b, s_b, lo, hi, tmp, tmpname,
                reads=(), writes=(), waits=(), bcreads=()):
    P = C.P
    M = 12582912.0
    P.op("v", lambda E: E.tensor_tensor(tmp, src, sinv_b, ALU.mult),
         reads=tuple(reads) + tuple(bcreads), writes=(tmpname,), waits=waits)
    P.op("v", lambda E: E.tensor_scalar(tmp, tmp, M, M, ALU.add,
                                        ALU.subtract),
         reads=(tmpname,), writes=(tmpname,))
    P.op("v", lambda E: E.tensor_scalar(tmp, tmp, float(lo), float(hi),
                                        ALU.max, ALU.min),
         reads=(tmpname,), writes=(tmpname,))
    return P.op("v", lambda E: E.tensor_tensor(dst, tmp, s_b, ALU.mult),
                reads=(tmpname,) + tuple(bcreads), writes=writes, inc="v1")


def emit_armax(C, macc_list, scl, derived, ncore):
    P, nc, ds = C.P, C.nc, C.ds
    row = C.ccrow
    tz = P.op("g", lambda E: E.memset(row[0:1, :], 0.0), writes=("ccrow",),
              inc="g1", waits=[C.cc_prev])
    toks = []
    for i, (m, name) in enumerate(macc_list):
        tr = P.op("v", lambda E, m=m, i=i: E.reduce_max(
            C.mrow[:, 2 * i:2 * i + 1], m, AX.X), reads=(name,),
            writes=("mrow",), inc="v1",
            waits=[getattr(C, "mrow_prev", None)])
        tp = P.op("g", lambda E, i=i: E.partition_all_reduce(
            C.mrow[:, 2 * i + 1:2 * i + 2], C.mrow[:, 2 * i:2 * i + 1],
            128, ReduceOp.max),
            reads=("mrow",), writes=("mrow",), inc="g1", waits=[tr])
        tc = P.op("g", lambda E, i=i: E.tensor_copy(
            row[0:1, i:i + 1], C.mrow[0:1, 2 * i + 1:2 * i + 2]),
            reads=("mrow",), writes=("ccrow",), inc="g1",
            waits=[tp, tz])
        toks.append(tc)
    td = P.op("g", _dma(ds["cc_in"][:, :], row[0:1, 0:8]), inc="gcc",
              inc_n=16, waits=toks)
    tcc = P.op("g", lambda E, ncore=ncore: E.collective_compute(
        "AllReduce", ALU.max, replica_groups=[list(range(ncore))],
        ins=[ds["cc_in"].opt()], outs=[ds["cc_out"].opt()]),
        inc="g1", waits=[td])
    tl = P.op("g", _dma(row[0:1, 8:16], ds["cc_out"][:, :]), inc="gcc",
              inc_n=16, waits=[tcc])
    tout = None
    for i, _ in enumerate(macc_list):
        tout = P.op("g", lambda E, i=i: E.partition_broadcast(
            scl[:, i:i + 1], row[0:1, 8 + i:9 + i]),
            reads=("ccrow",), writes=("scl",), inc="g1", waits=[tl])
    tv = None
    for i, (qmax, inv_col, s_col) in enumerate(derived):
        P.op("v", lambda E, i=i, qmax=qmax, s_col=s_col:
             E.tensor_scalar(scl[:, s_col:s_col + 1],
                             scl[:, i:i + 1], 1.0 / qmax, 1e-12,
                             ALU.mult, ALU.max),
             reads=("scl",), writes=("scl",), waits=[tout])
        tv = P.op("v", lambda E, inv_col=inv_col, s_col=s_col:
                  E.reciprocal(scl[:, inv_col:inv_col + 1],
                               scl[:, s_col:s_col + 1]),
                  reads=("scl",), writes=("scl",), inc="v1")
    C.cc_prev = P.tok("g1")
    C.mrow_prev = C.cc_prev
    return tv


def s2_ffn0(C, ncore):
    P, nc, ds, sb, ps, Tc, L = C.P, C.nc, C.ds, C.sb, C.ps, C.Tc, C.L
    st = ExitStack()
    stw = ExitStack()
    def S(name, shape, dt, stack=None):
        return (stack or st).enter_context(nc.sbuf_tensor(name, shape, dt))
    wupg = stw.enter_context(
        nc.sbuf_tensor("f_wupg", [128, KM * HF], BF16, side="right"))
    wgat = stw.enter_context(
        nc.sbuf_tensor("f_wgat", [128, KM * HF], BF16, side="right"))
    xch = S("f_xch", [128, KM * Tc], F32)
    xqb = S("f_xqb", [128, KM * Tc], BF16)
    tile = S("f_tile", [128, 8 * Tc], F32)
    tmp2 = S("f_tmp2", [128, Tc], F32)
    sqb2 = S("f_sqb", [128, Tc], BF16)
    qt2 = S("f_qt2", [128, Tc], F32)
    macc = S("f_macc", [128, 32], F32)
    scl = S("f_scl", [128, 24], F32)
    mrow = S("f_mrow", [128, 4], F32)
    ccrow = S("f_ccrow", [1, 16], F32)
    xtl2 = S("f_xtl", [128, 4 * Tc], F32)
    C.xtl = xtl2
    C.sgt = tmp2
    C.sgtname = "f_tmp2"
    C.mrow, C.ccrow = mrow, ccrow
    C.scl = scl
    C.cc_prev = None
    C.qtmp2, C.qtmp2name = qt2, "f_qt2"
    rot = {}
    NCH = L // Tc
    XCHN = tuple(f"f_xch{k}" for k in range(KM))

    def absred(slot, src_ap, reads=(), waits=()):
        return P.op("v", lambda E, slot=slot, src_ap=src_ap: E.tensor_reduce(
            macc[:, slot:slot + 1], src_ap, AX.X, ALU.max,
            apply_absolute_value=True),
            reads=reads, writes=(f"f_mc{slot}",), inc="v1", waits=waits)

    def fold(s0, n, acc, accname, waits=()):
        P.op("v", lambda E, s0=s0, n=n: E.tensor_reduce(
            macc[:, 24:25], macc[:, s0:s0 + n], AX.X, ALU.max),
            reads=tuple(f"f_mc{s0+i}" for i in range(n)),
            writes=("f_mfold",), waits=waits)
        return P.op("v", lambda E, acc=acc: E.tensor_tensor(
            macc[:, acc:acc + 1], macc[:, acc:acc + 1], macc[:, 24:25],
            ALU.max),
            reads=("f_mfold", accname), writes=(accname,), inc="v1")

    twu = P.op("q", _dma(wupg.ap().rearrange("p (k c) -> p k c", c=HF),
                         ds["w_upT"].rearrange("(k p) c -> p k c", p=128)),
               inc="qw0", inc_n=16)
    twg = P.op("q", _dma(wgat.ap().rearrange("p (k c) -> p k c", c=HF),
                         ds["w_gateT"].rearrange("(k p) c -> p k c", p=128)),
               inc="qw1", inc_n=16)

    P.op("v", lambda E: E.memset(macc[:, 16:17], 0.0), writes=("f_ma0",))
    for b in range(BPC):
        for c in range(NCH):
            cs = slice(c * Tc, (c + 1) * Tc)
            first = True
            for k in range(KM):
                par = k % 2
                tl = P.op("q", _dma(xtl2[:, par * Tc:(par + 1) * Tc],
                                    ds["resid"][b, k, :, cs]),
                          inc=f"qt{par}", inc_n=16,
                          waits=[rot.get(("xtlA", par)),
                                 rot.get(("xtlA2", par))])
                tc1 = P.op("v", lambda E, k=k, par=par: E.tensor_copy(
                    xqb[:, k * Tc:(k + 1) * Tc],
                    xtl2[:, par * Tc:(par + 1) * Tc]),
                    writes=(f"f_xqb{k}",), inc="v1",
                    waits=[tl] + ([rot.get("xqb_user")] if k == 0 else []),
                    clears=tuple(f"f_xqb{i}" for i in range(KM))
                    if k == 0 else ())
                tc2 = P.op("s", lambda E, par=par: E.activation(
                    sqb2[:], xtl2[:, par * Tc:(par + 1) * Tc], AF.Square),
                    writes=("f_sqb",), inc="s1",
                    waits=[tl, rot.get("sqb_user")], clears=("f_sqb",))
                tc3 = P.op("v", lambda E, k=k, par=par: E.tensor_copy(
                    xch[:, k * Tc:(k + 1) * Tc],
                    xtl2[:, par * Tc:(par + 1) * Tc]),
                    writes=(f"f_xch{k}",), inc="v1",
                    waits=[tl] + ([rot.get("xch_user")] if k == 0 else []),
                    clears=XCHN + ("f_xchW",) if k == 0 else ())
                rot[("xtlA", par)] = tc3
                rot[("xtlA2", par)] = tc2
                P.op("t", lambda E, k=k, first=first: E.matmul(
                    ps[6][0:1, 0:Tc], sb["onesb"][:, 0:1],
                    xqb[:, k * Tc:(k + 1) * Tc], start=first,
                    stop=(k == KM - 1)),
                    inc="t1", waits=[tc1] + ([C.psum_free.get(6)]
                                             if first else []))
                t7 = P.op("t", lambda E, first=first, k=k: E.matmul(
                    ps[7][0:1, 0:Tc], sb["onesb"][:, 0:1], sqb2[:],
                    start=first, stop=(k == KM - 1)),
                    inc="t1", waits=[tc2] + ([C.psum_free.get(7)]
                                             if first else []))
                rot["sqb_user"] = t7
                first = False
            tmm = P.tok("t1")
            t4, t5 = ln_rows(C, ps[6], ps[7], 1.0 / DM, 0, tmm)
            C.psum_free[6] = P.tok("v1")
            C.psum_free[7] = C.psum_free[6]
            for k in range(KM):
                P.op("v", lambda E, k=k: E.tensor_tensor(
                    xch[:, k * Tc:(k + 1) * Tc], xch[:, k * Tc:(k + 1) * Tc],
                    ps[5][:, 0:Tc], ALU.mult),
                    reads=(f"f_xch{k}",), writes=(f"f_xch{k}",),
                    waits=[t4, t5] if k == 0 else ())
            tn = None
            for k in range(KM):
                tn = P.op("v", lambda E, k=k: E.tensor_tensor(
                    xch[:, k * Tc:(k + 1) * Tc], xch[:, k * Tc:(k + 1) * Tc],
                    ps[4][:, 0:Tc], ALU.subtract),
                    reads=(f"f_xch{k}",), writes=(f"f_xch{k}",),
                    inc="v1" if k == KM - 1 else None)
            for k in range(KM):
                absred(k, xch[:, k * Tc:(k + 1) * Tc],
                       reads=(f"f_xch{k}",))
            rot["xchW_user"] = fold(0, KM, 16, "f_ma0")
            C.psum_free[4] = tn
            C.psum_free[5] = tn
            tst = P.op("s", lambda E, b=b, cs=cs: E.dma_start(
                out=ds["xln1"][b, :, :, cs].rearrange("k p t -> p k t"),
                in_=xch.ap().rearrange("p (k t) -> p k t", k=KM)),
                inc="gx0", inc_n=16, waits=[tn])
            rot["xch_user"] = tst
            rot["xqb_user"] = tmm
    tm0 = emit_armax(C, [(macc[:, 16:17], "f_ma0")], scl,
                     [(32767.0, 8, 9)], ncore)

    P.dma_wait_all("q", prefixes=("gx", "gu", "gv"))
    P.op("v", lambda E: E.memset(macc[:, 17:19], 0.0),
         writes=("f_ma1", "f_ma2"))
    M = 12582912.0
    for b in range(BPC):
        for c in range(NCH):
            cs = slice(c * Tc, (c + 1) * Tc)
            lds = []
            for k in range(KM):
                lds.append(P.op("q", _dma(xch[:, k * Tc:(k + 1) * Tc],
                                          ds["xln1"][b, k, :, cs]),
                           inc=f"qt{k % 4}", inc_n=16,
                           waits=[rot.get("xchW_user")]))
            P.op("v", lambda E: E.tensor_scalar(
                xch[:], xch[:], scl[:, 8:9], M, ALU.mult, ALU.add),
                reads=("scl",), writes=("f_xchW",),
                waits=lds + [tm0], clears=XCHN + ("f_xchW",))
            P.op("v", lambda E: E.tensor_scalar(
                xch[:], xch[:], M, -32768.0, ALU.subtract, ALU.max),
                reads=("f_xchW",), writes=("f_xchW",))
            tqd = P.op("v", lambda E: E.tensor_scalar(
                xqb[:], xch[:], 32767.0, scl[:, 9:10], ALU.min, ALU.mult),
                reads=("f_xchW",), writes=("f_xqbW",), inc="v1",
                waits=[rot.get("xqb_user")],
                clears=tuple(f"f_xqb{i}" for i in range(KM)) + ("f_xqbW",))
            rot["xchW_user"] = tqd
            for which, (wst, tw, slot0, acc, accname, spill, sem) in \
                    enumerate([(wupg, twu, 0, 17, "f_ma1", "up_raw", "gu"),
                               (wgat, twg, 8, 18, "f_ma2", "gate_raw",
                                "gv")]):
                for f in range(KF):
                    par, pi = f % 2, (2 * which + f % 2) % 4
                    tmm = None
                    for k in range(KM):
                        tmm = P.op("t", lambda E, k=k, f=f, pi=pi, wst=wst:
                                   E.matmul(
                                       ps[pi][:, 0:Tc],
                                       wst[:, k * HF + f * 128:
                                           k * HF + (f + 1) * 128],
                                       xqb[:, k * Tc:(k + 1) * Tc],
                                       start=(k == 0), stop=(k == KM - 1)),
                                   waits=([tqd, tw, C.psum_free.get(pi)]
                                          if k == 0 else []),
                                   inc="t1" if k == KM - 1 else None)
                    te = P.op("s", lambda E, pi=pi, which=which, par=par:
                              E.activation(
                                  tile[:, (2 * which + par) * Tc:
                                       (2 * which + par + 1) * Tc],
                                  ps[pi][:, 0:Tc], AF.Copy),
                              writes=(f"f_tile{2 * which + par}",), inc="s1",
                              waits=[tmm] + list(
                                  rot.get(("tile", which, par), ())),
                              clears=(f"f_tile{2 * which + par}",))
                    trd = absred(slot0 + f % 8,
                                 tile[:, (2 * which + par) * Tc:
                                      (2 * which + par + 1) * Tc],
                                 reads=(f"f_tile{2 * which + par}",),
                                 waits=[te])
                    C.psum_free[pi] = te
                    if f % 8 == 7:
                        fold(slot0, 8, acc, accname)
                    tsp = P.op("s", lambda E, b=b, f=f, cs=cs, which=which,
                               par=par, spill=spill: E.dma_start(
                                   out=ds[spill][b, f, :, cs],
                                   in_=tile[:, (2 * which + par) * Tc:
                                            (2 * which + par + 1) * Tc]),
                               inc=f"{sem}{par}", inc_n=16, waits=[te])
                    rot[("tile", which, par)] = (tsp, trd)
            rot["xqb_user"] = P.tok("t1")
    P.barrier()
    stw.close()
    tm12 = emit_armax(C, [(macc[:, 17:18], "f_ma1"),
                          (macc[:, 18:19], "f_ma2")], scl,
                      [(32767.0, 10, 11), (32767.0, 12, 13)], ncore)

    P.dma_wait_all("q", prefixes=("gx", "gu", "gv"))
    P.op("v", lambda E: E.memset(macc[:, 19:20], 0.0), writes=("f_ma3",))
    TAN = ("f_tileA",)
    TBN = ("f_tileB",)
    for b in range(BPC):
        for c in range(NCH):
            cs = slice(c * Tc, (c + 1) * Tc)
            for g in range(KF // 4):
                f0 = 4 * g
                lds = []
                for j in range(4):
                    lds.append(P.op(
                        "q", lambda E, j=j, f0=f0, b=b, cs=cs: E.dma_start(
                            out=tile[:, j * Tc:(j + 1) * Tc],
                            in_=ds["gate_raw"][b, f0 + j, :, cs]),
                        inc=f"qt{j}", inc_n=16,
                        waits=[rot.get("t3a")]))
                tq0 = P.op("v", lambda E: E.tensor_scalar(
                    tile[:, 4 * Tc:8 * Tc], tile[:, 0:4 * Tc],
                    scl[:, 12:13], M, ALU.mult, ALU.add),
                    reads=("scl",), writes=TBN,
                    waits=lds + [tm12] + list(rot.get("t3b", ())),
                    clears=TAN + TBN, inc="v1")
                rot["t3a"] = tq0
                P.op("v", lambda E: E.tensor_scalar(
                    tile[:, 4 * Tc:8 * Tc], tile[:, 4 * Tc:8 * Tc], M,
                    -32768.0, ALU.subtract, ALU.max),
                    reads=TBN, writes=TBN)
                tgq = P.op("v", lambda E: E.tensor_scalar(
                    tile[:, 4 * Tc:8 * Tc], tile[:, 4 * Tc:8 * Tc],
                    32767.0, scl[:, 13:14], ALU.min, ALU.mult),
                    reads=TBN, writes=TBN, inc="v1")
                tsg = P.op("s", lambda E: E.activation(
                    xch[:, 0:4 * Tc], tile[:, 4 * Tc:8 * Tc], AF.Sigmoid),
                    writes=("f_sig",), inc="s1",
                    waits=[tgq, rot.get("sig_user")], clears=("f_sig",))
                P.op("v", lambda E: E.tensor_scalar(
                    xch[:, 0:4 * Tc], xch[:, 0:4 * Tc], 65536.0, M,
                    ALU.mult, ALU.add),
                    reads=(), writes=("f_sig",), waits=[tsg],
                    clears=("f_xchW",) + XCHN)
                P.op("v", lambda E: E.tensor_scalar(
                    xch[:, 0:4 * Tc], xch[:, 0:4 * Tc], M, 0.0,
                    ALU.subtract, ALU.max),
                    reads=("f_sig",), writes=("f_sig",))
                P.op("v", lambda E: E.tensor_scalar(
                    xch[:, 0:4 * Tc], xch[:, 0:4 * Tc], 65535.0,
                    1.0 / 65536.0, ALU.min, ALU.mult),
                    reads=("f_sig",), writes=("f_sig",))
                tg2 = P.op("v", lambda E: E.tensor_tensor(
                    tile[:, 4 * Tc:8 * Tc], tile[:, 4 * Tc:8 * Tc],
                    xch[:, 0:4 * Tc], ALU.mult),
                    reads=("f_sig",) + TBN, writes=TBN, inc="v1")
                rot["sig_user"] = tg2
                trd = None
                for j in range(4):
                    trd = absred((4 * g + j) % 8,
                                 tile[:, (4 + j) * Tc:(5 + j) * Tc],
                                 reads=TBN)
                if g % 2 == 1:
                    fold(0, 8, 19, "f_ma3")
                sps = []
                for j in range(4):
                    sps.append(P.op(
                        "s", lambda E, j=j, f0=f0, b=b, cs=cs: E.dma_start(
                            out=ds["g2pre"][b, f0 + j, :, cs],
                            in_=tile[:, (4 + j) * Tc:(5 + j) * Tc]),
                        inc=f"gu{j % 2}", inc_n=16, waits=[tg2]))
                rot["t3b"] = sps + [trd]
    tm3 = emit_armax(C, [(macc[:, 19:20], "f_ma3")], scl,
                     [(32767.0, 14, 15)], ncore)

    P.dma_wait_all("q", prefixes=("gx", "gu", "gv"))
    P.op("v", lambda E: E.memset(macc[:, 20:21], 0.0), writes=("f_ma4",))
    for b in range(BPC):
        for c in range(NCH):
            cs = slice(c * Tc, (c + 1) * Tc)
            for g in range(KF // 4):
                f0 = 4 * g
                ldA, ldB = [], []
                for j in range(4):
                    ldA.append(P.op(
                        "q", lambda E, j=j, f0=f0, b=b, cs=cs: E.dma_start(
                            out=tile[:, j * Tc:(j + 1) * Tc],
                            in_=ds["g2pre"][b, f0 + j, :, cs]),
                        inc=f"qt{j}", inc_n=16,
                        waits=list(rot.get("t4a", ())) + [rot.get("t3a")]))
                    ldB.append(P.op(
                        "q", lambda E, j=j, f0=f0, b=b, cs=cs: E.dma_start(
                            out=tile[:, (4 + j) * Tc:(5 + j) * Tc],
                            in_=ds["up_raw"][b, f0 + j, :, cs]),
                        inc=f"qo{j}", inc_n=16,
                        waits=[rot.get("t4b")] + list(rot.get("t3b", ()))))
                P.op("v", lambda E: E.tensor_scalar(
                    tile[:, 0:4 * Tc], tile[:, 0:4 * Tc],
                    scl[:, 14:15], M, ALU.mult, ALU.add),
                    reads=("scl",), writes=TAN, waits=ldA + [tm3],
                    clears=TAN + TBN)
                P.op("v", lambda E: E.tensor_scalar(
                    tile[:, 0:4 * Tc], tile[:, 0:4 * Tc], M, -32768.0,
                    ALU.subtract, ALU.max),
                    reads=TAN, writes=TAN)
                P.op("v", lambda E: E.tensor_scalar(
                    tile[:, 0:4 * Tc], tile[:, 0:4 * Tc], 32767.0,
                    scl[:, 15:16], ALU.min, ALU.mult),
                    reads=TAN, writes=TAN)
                P.op("v", lambda E: E.tensor_scalar(
                    tile[:, 4 * Tc:8 * Tc], tile[:, 4 * Tc:8 * Tc],
                    scl[:, 10:11], M, ALU.mult, ALU.add),
                    reads=TBN, writes=TBN, waits=ldB)
                P.op("v", lambda E: E.tensor_scalar(
                    tile[:, 4 * Tc:8 * Tc], tile[:, 4 * Tc:8 * Tc], M,
                    -32768.0, ALU.subtract, ALU.max),
                    reads=TBN, writes=TBN)
                P.op("v", lambda E: E.tensor_scalar(
                    tile[:, 4 * Tc:8 * Tc], tile[:, 4 * Tc:8 * Tc],
                    32767.0, scl[:, 11:12], ALU.min, ALU.mult),
                    reads=TBN, writes=TBN)
                to = P.op("v", lambda E: E.tensor_tensor(
                    tile[:, 0:4 * Tc], tile[:, 0:4 * Tc],
                    tile[:, 4 * Tc:8 * Tc], ALU.mult),
                    reads=TAN + TBN, writes=TAN, inc="v1")
                rot["t4b"] = to
                trd = None
                for j in range(4):
                    trd = absred((4 * g + j) % 8,
                                 tile[:, j * Tc:(j + 1) * Tc], reads=TAN)
                if g % 2 == 1:
                    fold(0, 8, 20, "f_ma4")
                sps = []
                for j in range(4):
                    sps.append(P.op(
                        "s", lambda E, j=j, f0=f0, b=b, cs=cs: E.dma_start(
                            out=ds["o_pre"][b, f0 + j, :, cs],
                            in_=tile[:, j * Tc:(j + 1) * Tc]),
                        inc=f"gu{j % 2}", inc_n=16, waits=[to]))
                rot["t4a"] = sps + [trd]
    tm4 = emit_armax(C, [(macc[:, 20:21], "f_ma4")], scl,
                     [(32767.0, 16, 17)], ncore)

    P.dma_wait_all("q", prefixes=("gx", "gu", "gv"))
    och = S("f_och", [128, KF * Tc], BF16)
    dnf = S("f_dnf", [128, 2 * KF * 128], BF16)
    dch = S("f_dch", [128, KM * Tc], F32)
    for b in range(BPC):
        for c in range(NCH):
            cs = slice(c * Tc, (c + 1) * Tc)
            for g in range(KF // 4):
                f0 = 4 * g
                lds = []
                for j in range(4):
                    lds.append(P.op(
                        "q", lambda E, j=j, f0=f0, b=b, cs=cs: E.dma_start(
                            out=tile[:, j * Tc:(j + 1) * Tc],
                            in_=ds["o_pre"][b, f0 + j, :, cs]),
                        inc=f"qt{j}", inc_n=16,
                        waits=[rot.get("t5a")] + list(rot.get("t4a", ()))))
                t5a = P.op("v", lambda E: E.tensor_scalar(
                    tile[:, 4 * Tc:8 * Tc], tile[:, 0:4 * Tc],
                    scl[:, 16:17], M, ALU.mult, ALU.add),
                    reads=("scl",), writes=TBN,
                    waits=lds + [tm4, rot.get("t5b"), rot.get("t4b")],
                    clears=TAN + TBN, inc="v1")
                rot["t5a"] = t5a
                P.op("v", lambda E: E.tensor_scalar(
                    tile[:, 4 * Tc:8 * Tc], tile[:, 4 * Tc:8 * Tc], M,
                    -32768.0, ALU.subtract, ALU.max),
                    reads=TBN, writes=TBN)
                tq = P.op("v", lambda E, f0=f0: E.tensor_scalar(
                    och[:, f0 * Tc:(f0 + 4) * Tc], tile[:, 4 * Tc:8 * Tc],
                    32767.0, scl[:, 17:18], ALU.min, ALU.mult),
                    reads=TBN, writes=("f_och",), inc="v1",
                    waits=[rot.get("och_user")] if g == 0 else (),
                    clears=("f_och",) if g == 0 else ())
                rot["t5b"] = tq
            tqd = P.tok("v1")
            for f in range(KM):
                par, pi = f % 2, f % 2
                tw = P.op("q", lambda E, f=f, par=par: E.dma_start(
                    out=dnf[:, par * KF * 128:(par + 1) * KF * 128],
                    in_=ds["w_downT"][f]),
                    inc=f"qp{par}", inc_n=16, waits=[rot.get(("dnf", par))])
                tmm = None
                for k in range(KF):
                    tmm = P.op("t", lambda E, k=k, par=par, pi=pi: E.matmul(
                        ps[pi][:, 0:Tc],
                        dnf[:, par * KF * 128 + k * 128:
                            par * KF * 128 + (k + 1) * 128],
                        och[:, k * Tc:(k + 1) * Tc],
                        start=(k == 0), stop=(k == KF - 1)),
                        waits=([tw, tqd, C.psum_free.get(pi)]
                               if k == 0 else []),
                        inc="t1" if k == KF - 1 else None)
                rot[("dnf", par)] = tmm
                te = P.op("s", lambda E, f=f, pi=pi: E.activation(
                    dch[:, f * Tc:(f + 1) * Tc], ps[pi][:, 0:Tc], AF.Copy),
                    writes=("f_dch",), inc="s1",
                    waits=[tmm] + ([rot.get("dch_user")] if f == 0 else []),
                    clears=("f_dch",) if f == 0 else ())
                C.psum_free[pi] = te
            tdn = P.tok("s1")
            rot["och_user"] = P.tok("t1")
            tadd, tst = emit_postln_add(C, "f_dch", dch, b, cs, rot, tdn)
            rot["dch_user"] = tst
    st.close()
    return P.tok("gr0")

def s3_attn(C):
    P, nc, ds, sb, ps, Tc, L = C.P, C.nc, C.ds, C.sb, C.ps, C.Tc, C.L
    starts = _win_starts(L)
    stP = ExitStack()
    _uid = [0]
    def S(name, shape, dt, stack=None):
        _uid[0] += 1
        return (stack or stP).enter_context(
            nc.sbuf_tensor(f"{name}_{_uid[0]}", shape, dt))
    qb = S("a_qb", [128, KM * L], BF16)
    kb = S("a_kb", [128, KM * L], BF16)
    vT = S("a_vT", [128, (L // 128) * DM], BF16)
    bqt = S("a_bqt", [128, 3 * KM], F32)
    bvrow = S("a_bvrow", [1, DM], F32)
    bvrep = S("a_bvrep", [128, DM], F32)
    cinv = S("a_cinv", [128, L], BF16)
    srow = S("a_srow", [128, 8], F32)
    xtl3 = S("a_xtl", [128, 4 * Tc], F32)
    C.xtl = xtl3
    sgt4 = S("a_sgt", [128, Tc], F32)
    C.sgt, C.sgtname = sgt4, "a_sgt"
    rot = {}
    NCH = L // Tc

    tb1 = P.op("q", _dma_nc(bqt[:, 0:KM],
                         C.din["L1_attn_bq"].rearrange("(k p) -> p k", p=128)),
               inc="qc", inc_n=16)
    tb2 = P.op("q", _dma_nc(bqt[:, KM:2 * KM],
                         C.din["L1_attn_bk"].rearrange("(k p) -> p k", p=128)),
               inc="qc", inc_n=16)
    tb3 = P.op("q", _dma_nc(bqt[:, 2 * KM:3 * KM],
                         C.din["L1_attn_bo"].rearrange("(k p) -> p k", p=128)),
               inc="qc", inc_n=16)
    tb4 = P.op("q", _dma(bvrow[0:1, :],
                         C.din["L1_attn_bv"].rearrange("(o c) -> o c", o=1)),
               inc="qc", inc_n=16)
    tm = P.op("t", lambda E: E.matmul(ps[4][:, 0:512], sb["onesf"][0:1, 0:128],
                                      bvrow[0:1, 0:512], start=True,
                                      stop=True),
              inc="t1", waits=[tb4, C.psum_free.get(4)])
    tm2 = P.op("t", lambda E: E.matmul(ps[5][:, 0:512],
                                       sb["onesf"][0:1, 0:128],
                                       bvrow[0:1, 512:1024], start=True,
                                       stop=True),
               inc="t1", waits=[tb4, C.psum_free.get(5)])
    te1 = P.op("s", lambda E: E.activation(bvrep[:, 0:512], ps[4][:, 0:512],
                                           AF.Copy),
               writes=("a_bvrep",), inc="s1", waits=[tm])
    te2 = P.op("s", lambda E: E.activation(bvrep[:, 512:1024],
                                           ps[5][:, 0:512], AF.Copy),
               writes=("a_bvrep",), inc="s1", waits=[tm2])
    C.psum_free[4] = te1
    C.psum_free[5] = te2
    counts = np.zeros(L, np.int32)
    for st0 in starts:
        counts[st0:st0 + 256] += 1
    runs, cur, r0 = [], counts[0], 0
    for i in range(1, L):
        if counts[i] != cur:
            runs.append((r0, i, cur)); r0, cur = i, counts[i]
    runs.append((r0, L, cur))
    tcv = None
    for (a, b_, cval) in runs:
        tcv = P.op("v", lambda E, a=a, b_=b_, cval=cval: E.memset(
            cinv[:, a:b_], 1.0 / float(cval)), writes=("a_cinv",), inc="v1")

    def _one_batch(b):
        stA = ExitStack()
        xln2 = S("a_xln2", [128, KM * L], BF16, stA)
        wqb = S("a_wqb", [128, KM * DM], BF16, stA)
        sqb3 = S("a_sqb", [128, Tc], BF16, stA)
        for c in range(NCH):
            cs = slice(c * Tc, (c + 1) * Tc)
            first = True
            for k in range(KM):
                par = k % 2
                tl = P.op("q", _dma(xtl3[:, par * Tc:(par + 1) * Tc],
                                    ds["resid"][b, k, :, cs]),
                          inc=f"qt{par}", inc_n=16,
                          waits=[rot.get(("xtl", par)),
                                 rot.get(("xtl2", par))])
                tc1 = P.op("v", lambda E, k=k, c=c, par=par: E.tensor_copy(
                    xln2[:, k * L + c * Tc:k * L + (c + 1) * Tc],
                    xtl3[:, par * Tc:(par + 1) * Tc]),
                    writes=("a_xln2",), inc="v1",
                    waits=[tl, rot.get("xln2_user")])
                tc2 = P.op("s", lambda E, par=par: E.activation(
                    sqb3[:], xtl3[:, par * Tc:(par + 1) * Tc], AF.Square),
                    writes=("a_sqb",), inc="s1",
                    waits=[tl, rot.get("sqb_user")])
                rot[("xtl", par)] = tc1
                rot[("xtl2", par)] = tc2
                t6 = P.op("t", lambda E, k=k, c=c, first=first: E.matmul(
                    ps[6][0:1, 0:Tc], sb["onesb"][:, 0:1],
                    xln2[:, k * L + c * Tc:k * L + (c + 1) * Tc],
                    start=first, stop=(k == KM - 1)),
                    inc="t1", waits=[tc1] + ([C.psum_free.get(6)]
                                             if first else []))
                t7 = P.op("t", lambda E, first=first, k=k: E.matmul(
                    ps[7][0:1, 0:Tc], sb["onesb"][:, 0:1], sqb3[:],
                    start=first, stop=(k == KM - 1)),
                    inc="t1", waits=[tc2] + ([C.psum_free.get(7)]
                                             if first else []))
                rot["sqb_user"] = t7
                first = False
            tmm = P.tok("t1")
            t4, t5 = ln_rows(C, ps[6], ps[7], 1.0 / DM, 0, tmm)
            C.psum_free[6] = P.tok("v1")
            C.psum_free[7] = C.psum_free[6]
            tn = None
            for k in range(KM):
                P.op("v", lambda E, k=k, c=c: E.tensor_tensor(
                    xln2[:, k * L + c * Tc:k * L + (c + 1) * Tc],
                    xln2[:, k * L + c * Tc:k * L + (c + 1) * Tc],
                    ps[5][:, 0:Tc], ALU.mult),
                    reads=("a_xln2",), writes=("a_xln2",), waits=[t4, t5])
                tn = P.op("v", lambda E, k=k, c=c: E.tensor_tensor(
                    xln2[:, k * L + c * Tc:k * L + (c + 1) * Tc],
                    xln2[:, k * L + c * Tc:k * L + (c + 1) * Tc],
                    ps[4][:, 0:Tc], ALU.subtract),
                    reads=("a_xln2",), writes=("a_xln2",),
                    inc="v1" if k == KM - 1 else None)
            C.psum_free[4] = tn
            C.psum_free[5] = tn
            rot["xln2_user"] = tn
        tln = P.tok("v1")

        for which, (wname, dst, dstname, bcol) in enumerate(
                [("w_q", qb, "a_qb", 0), ("w_k", kb, "a_kb", KM)]):
            tw = P.op("q", lambda E, wname=wname: E.dma_start(
                out=wqb.ap().rearrange("p (k c) -> p k c", c=DM),
                in_=ds[wname].rearrange("(k p) c -> p k c", p=128)),
                inc="qw0", inc_n=16, waits=[rot.get("wqb_user")])
            for f in range(KM):
                for c in range(NCH):
                    pi = (f * NCH + c) % 4
                    tmm = None
                    for k in range(KM):
                        tmm = P.op("t", lambda E, k=k, f=f, c=c, pi=pi:
                                   E.matmul(
                                       ps[pi][:, 0:Tc],
                                       wqb[:, k * DM + f * 128:
                                           k * DM + (f + 1) * 128],
                                       xln2[:, k * L + c * Tc:
                                            k * L + (c + 1) * Tc],
                                       start=(k == 0), stop=(k == KM - 1)),
                                   waits=([tw, tln, C.psum_free.get(pi)]
                                          if k == 0 else []),
                                   inc="t1" if k == KM - 1 else None)
                    te = P.op("v", lambda E, f=f, c=c, pi=pi, dst=dst,
                              bcol=bcol: E.tensor_scalar(
                                  dst[:, f * L + c * Tc:f * L + (c + 1) * Tc],
                                  ps[pi][:, 0:Tc],
                                  bqt[:, bcol + f:bcol + f + 1], None,
                                  ALU.add),
                              writes=(dstname,), inc="v1",
                              waits=[tmm, rot.get((dstname, "user"))])
                    C.psum_free[pi] = te
            rot["wqb_user"] = P.tok("t1")
        tw = P.op("q", lambda E: E.dma_start(
            out=wqb.ap().rearrange("p (k c) -> p k c", c=DM),
            in_=ds["w_v"].rearrange("(k p) c -> p k c", p=128)),
            inc="qw0", inc_n=16, waits=[rot.get("wqb_user")])
        for tt in range(L // 128):
            for h in range(2):
                pi = (tt * 2 + h) % 4
                tmm = None
                for k in range(KM):
                    tmm = P.op("t", lambda E, k=k, tt=tt, h=h, pi=pi:
                               E.matmul(
                                   ps[pi][:, 0:512],
                                   xln2[:, k * L + tt * 128:
                                        k * L + (tt + 1) * 128],
                                   wqb[:, k * DM + h * 512:
                                       k * DM + (h + 1) * 512],
                                   start=(k == 0), stop=(k == KM - 1)),
                               waits=([tw, tln, C.psum_free.get(pi)]
                                      if k == 0 else []),
                               inc="t1" if k == KM - 1 else None)
                te = P.op("v", lambda E, tt=tt, h=h, pi=pi: E.tensor_tensor(
                    vT[:, tt * DM + h * 512:tt * DM + (h + 1) * 512],
                    ps[pi][:, 0:512], bvrep[:, h * 512:(h + 1) * 512],
                    ALU.add),
                    reads=("a_bvrep",), writes=("a_vT",), inc="v1",
                    waits=[tmm, rot.get("vT_user")])
                C.psum_free[pi] = te
        tqkv = P.tok("v1")
        stA.close()
        P.barrier()

        stB = ExitStack()
        aout = S("a_aout", [128, KM * L], BF16, stB)
        pf = S("a_pf", [128, 256], F32, stB)
        pnb = S("a_pnb", [128, 256], BF16, stB)
        pT = S("a_pT", [128, 2 * 256], BF16, stB)
        aobf = S("a_aobf", [128, KM * Tc], BF16, stB)
        woch = S("a_woch", [128, KM * Tc], F32, stB)
        wo_res = S("a_wores", [128, KM * 128], BF16, stB)
        tz = None
        for k in range(KM):
            tz = P.op("v", lambda E, k=k: E.memset(
                aout[:, k * L:(k + 1) * L], 0.0), writes=("a_aout",),
                inc="v1" if k == KM - 1 else None)
        for st0 in starts:
            for qt in range(2):
                pi = qt
                tmm = None
                for k in range(KM):
                    tmm = P.op("t", lambda E, k=k, st0=st0, qt=qt, pi=pi:
                               E.matmul(
                                   ps[pi][:, 0:256],
                                   qb[:, k * L + st0 + qt * 128:
                                      k * L + st0 + (qt + 1) * 128],
                                   kb[:, k * L + st0:k * L + st0 + 256],
                                   start=(k == 0), stop=(k == KM - 1)),
                               waits=([tqkv, C.psum_free.get(pi)]
                                      if k == 0 else []),
                               inc="t1" if k == KM - 1 else None)
                tr = P.op("v", lambda E, pi=pi: E.reduce_max(
                    srow[:, 0:1], ps[pi][:, 0:256], AX.X, negate=True),
                    writes=("a_srow",), waits=[tmm,
                                               rot.get("srow_user")])
                tsc = P.op("v", lambda E: E.tensor_scalar(
                    srow[:, 1:2], srow[:, 0:1], 0.125, None, ALU.mult),
                    reads=("a_srow",), writes=("a_srow",), inc="v1")
                tex = P.op("s", lambda E, pi=pi: E.activation(
                    pf[:], ps[pi][:, 0:256], AF.Exp, scale=0.125,
                    bias=srow[:, 1:2], accum_out=srow[:, 2:3]),
                    reads=("a_srow",), writes=("a_pf", "a_srow"), inc="s1",
                    waits=[tsc, rot.get("pf_user")])
                trc = P.op("v", lambda E: E.reciprocal(srow[:, 3:4],
                                                       srow[:, 2:3]),
                           reads=("a_srow",), writes=("a_srow",),
                           waits=[tex])
                tpn = P.op("v", lambda E: E.tensor_scalar(
                    pnb[:], pf[:], srow[:, 3:4], None, ALU.mult),
                    reads=("a_pf", "a_srow"), writes=("a_pnb",), inc="v1",
                    waits=[rot.get("pnb_user")])
                rot["srow_user"] = tpn
                rot["pf_user"] = tpn
                C.psum_free[pi] = tpn
                ttr = None
                for j in range(2):
                    ttr = P.op("t", lambda E, j=j, qt=qt: E.transpose(
                        ps[2].ap().bitcast(BF16)[0:128, j * 128:(j + 1) * 128],
                        pnb[:, j * 128:(j + 1) * 128], sb["idb"][:]),
                        waits=([tpn, C.psum_free.get(2)] if j == 0 else []),
                        inc="t1" if j == 1 else None)
                tpe = P.op("s", lambda E, qt=qt: E.activation(
                    pT.ap().bitcast(F32)
                    [:, (0 * 256 + qt * 128) // 2:(0 * 256 + qt * 128) // 2 + 64],
                    ps[2][:, 0:64], AF.Copy),
                    writes=("a_pT",), inc="s1",
                    waits=[ttr, rot.get("pT_user")])
                tpe2 = P.op("s", lambda E, qt=qt: E.activation(
                    pT.ap().bitcast(F32)
                    [:, (1 * 256 + qt * 128) // 2:(1 * 256 + qt * 128) // 2 + 64],
                    ps[2][:, 64:128], AF.Copy),
                    writes=("a_pT",), inc="s1")
                C.psum_free[2] = tpe2
            tpT = P.tok("s1")
            rot["pT_user"] = None
            for f in range(KM):
                pi = 4 + f % 2
                tmm = None
                for kt in range(2):
                    tmm = P.op("t", lambda E, kt=kt, f=f, st0=st0, pi=pi:
                               E.matmul(
                                   ps[pi][:, 0:256],
                                   vT[:, (st0 // 128 + kt) * DM + f * 128:
                                      (st0 // 128 + kt) * DM + (f + 1) * 128],
                                   pT[:, kt * 256:(kt + 1) * 256],
                                   start=(kt == 0), stop=(kt == 1)),
                               waits=([tpT, C.psum_free.get(pi)]
                                      if kt == 0 else []),
                               inc="t1" if kt == 1 else None)
                ta = P.op("v", lambda E, f=f, st0=st0, pi=pi: E.tensor_tensor(
                    aout[:, f * L + st0:f * L + st0 + 256],
                    ps[pi][:, 0:256],
                    aout[:, f * L + st0:f * L + st0 + 256], ALU.add),
                    reads=("a_aout",), writes=("a_aout",), inc="v1",
                    waits=[tmm, tz])
                C.psum_free[pi] = ta
            rot["pT_user"] = P.tok("t1")
        taw = P.tok("v1")
        if C.dbg_aout:
            P.op("s", lambda E, b=b: E.dma_start(
                out=C.out_d[b].rearrange("(a p) d -> p a d", p=128),
                in_=aout.ap().rearrange("p (a d) -> p a d", d=DM)),
                inc=f"go{b}", inc_n=16, waits=[taw])

        for c in range(NCH):
            cs = slice(c * Tc, (c + 1) * Tc)
            tcb = None
            for k in range(KM):
                tcb = P.op("v", lambda E, k=k, c=c, cs=cs: E.tensor_tensor(
                    aobf[:, k * Tc:(k + 1) * Tc],
                    aout[:, k * L + c * Tc:k * L + (c + 1) * Tc],
                    cinv[:, cs], ALU.mult),
                    reads=("a_aout", "a_cinv"), writes=("a_aobf",),
                    waits=[taw] + ([rot.get("aobf_user")] if k == 0 else []),
                    inc="v1" if k == KM - 1 else None)
            for f in range(KM):
                pi = f % 4
                tw = P.op("q", lambda E, f=f: E.dma_start(
                    out=wo_res.ap(), in_=ds["w_o"][f]),
                    inc="qo0", inc_n=16, waits=[rot.get("wores_user")])
                tmm = None
                for k in range(KM):
                    tmm = P.op("t", lambda E, k=k, f=f, pi=pi: E.matmul(
                        ps[pi][:, 0:Tc],
                        wo_res[:, k * 128:(k + 1) * 128],
                        aobf[:, k * Tc:(k + 1) * Tc],
                        start=(k == 0), stop=(k == KM - 1)),
                        waits=([tw, tcb, C.psum_free.get(pi)]
                               if k == 0 else []),
                        inc="t1" if k == KM - 1 else None)
                rot["wores_user"] = tmm
                te = P.op("v", lambda E, f=f, pi=pi: E.tensor_scalar(
                    woch[:, f * Tc:(f + 1) * Tc], ps[pi][:, 0:Tc],
                    bqt[:, 2 * KM + f:2 * KM + f + 1], None, ALU.add),
                    writes=("a_woch",), inc="v1",
                    waits=[tmm, rot.get("woch_user")])
                C.psum_free[pi] = te
            rot["aobf_user"] = P.tok("t1")
            two = P.tok("v1")
            tadd, tst = emit_postln_add(C, "a_woch", woch, b, cs, rot, two)
            rot["woch_user"] = tst
        stB.close()
        P.barrier()

    for b in range(BPC):
        _one_batch(b)

    stP.close()
    return P.tok("gr0")


def s4_ffn1(C):
    P, nc, ds, sb, ps, L = C.P, C.nc, C.ds, C.sb, C.ps, C.L
    Tc = 256
    W = Tc + 2
    st = ExitStack()
    def S(name, shape, dt):
        return st.enter_context(nc.sbuf_tensor(name, shape, dt))
    w1T = S("n_w1T", [128, KM * DH], BF16)
    xln = S("n_xln", [128, KM * W], F32)
    xqb = S("n_xqb", [128, KM * W], BF16)
    sqb4 = S("n_sqb", [128, W], BF16)
    h2b = S("n_h2b", [128, KH * Tc], BF16)
    hr = S("n_hr", [128, 2 * W], F32)
    hmax = S("n_hmax", [128, 2 * W], F32)
    scb = S("n_scb", [128, 2 * W], F32)
    tmpw = S("n_tmpw", [128, W], F32)
    tmpw2 = S("n_tmpw2", [128, W], F32)
    nab = S("n_abs", [128, W], F32)
    w2f = S("n_w2f", [128, 2 * KH * 128], BF16)
    dch = S("n_dch", [128, KM * Tc], F32)
    cw1 = S("n_cw1", [128, KH * 3], F32)
    cb1 = S("n_cb1", [128, KH], F32)
    xtl4 = S("n_xtl", [128, 2 * W], F32)
    C.xtl = xtl4
    C.qtmp2, C.qtmp2name = tmpw2, "n_tmpw2"
    sgt5 = S("n_sgt", [128, Tc], F32)
    C.sgt, C.sgtname = sgt5, "n_sgt"
    rot = {}
    NCH = L // Tc

    tw1 = P.op("q", _dma(w1T.ap().rearrange("p (k c) -> p k c", c=DH),
                         ds["w_w1T"].rearrange("(k p) c -> p k c", p=128)),
               inc="qw0", inc_n=16)
    tcw = P.op("q", _dma(cw1.ap().rearrange("p (k c) -> p k c", c=3),
                         C.din["L1_ffn_conv_w"]
                         .rearrange("(k p) o c -> p k (o c)", p=128)),
               inc="qc", inc_n=16)
    tcb = P.op("q", _dma_nc(cb1[:],
                         C.din["L1_ffn_conv_b"]
                         .rearrange("(k p) -> p k", p=128)),
               inc="qc", inc_n=16)

    for b in range(BPC):
        for c in range(NCH):
            c0 = c * Tc
            hs = max(c0 - 1, 0)
            he = min(c0 + Tc + 1, L)
            Wl = he - hs
            off = 1 if c == 0 else 0
            first = True
            for k in range(KM):
                par = k % 2
                tl = P.op("q", lambda E, b=b, k=k, hs=hs, he=he, par=par:
                          E.dma_start(
                              out=xtl4[:, par * W:par * W + (he - hs)],
                              in_=ds["resid"][b, k, :, hs:he]),
                          inc=f"qt{par}", inc_n=16,
                          waits=[rot.get(("xtl", par)),
                                 rot.get(("xtl2", par))])
                tc1 = P.op("v", lambda E, k=k, par=par, off=off, Wl=Wl:
                           E.tensor_copy(
                               xqb[:, k * W + off:k * W + off + Wl],
                               xtl4[:, par * W:par * W + Wl]),
                           writes=("n_xqb",), inc="v1",
                           waits=[tl, rot.get("xqb_user")])
                tc2 = P.op("s", lambda E, par=par, Wl=Wl: E.activation(
                    sqb4[:, 0:Wl], xtl4[:, par * W:par * W + Wl],
                    AF.Square),
                    writes=("n_sqb",), inc="s1",
                    waits=[tl, rot.get("sqb_user")])
                tc3 = P.op("v", lambda E, k=k, par=par, off=off, Wl=Wl:
                           E.tensor_copy(
                               xln[:, k * W + off:k * W + off + Wl],
                               xtl4[:, par * W:par * W + Wl]),
                           writes=("n_xln",), inc="v1",
                           waits=[tl, rot.get("xln_user")])
                rot[("xtl", par)] = tc3
                rot[("xtl2", par)] = tc2
                t6 = P.op("t", lambda E, k=k, first=first, off=off, Wl=Wl:
                          E.matmul(
                              ps[6][0:1, 0:Wl], sb["onesb"][:, 0:1],
                              xqb[:, k * W + off:k * W + off + Wl],
                              start=first, stop=(k == KM - 1)),
                          inc="t1", waits=[tc1] + ([C.psum_free.get(6)]
                                                   if first else []))
                t7 = P.op("t", lambda E, first=first, k=k, Wl=Wl: E.matmul(
                    ps[7][0:1, 0:Wl], sb["onesb"][:, 0:1], sqb4[:, 0:Wl],
                    start=first, stop=(k == KM - 1)),
                    inc="t1", waits=[tc2] + ([C.psum_free.get(7)]
                                             if first else []))
                rot["sqb_user"] = t7
                first = False
            tmm = P.tok("t1")
            oldTc = C.Tc
            C.Tc = Wl
            t4, t5 = ln_rows(C, ps[6], ps[7], 1.0 / DM, 0, tmm)
            C.Tc = oldTc
            C.psum_free[6] = P.tok("v1")
            C.psum_free[7] = C.psum_free[6]
            tz = P.op("v", lambda E: E.memset(hmax[:, 0:W], 0.0),
                      writes=("n_hmax",), inc="v1",
                      waits=[rot.get("hmax_user")])
            tn = None
            for k in range(KM):
                P.op("v", lambda E, k=k, off=off, Wl=Wl: E.tensor_tensor(
                    xln[:, k * W + off:k * W + off + Wl],
                    xln[:, k * W + off:k * W + off + Wl],
                    ps[5][:, 0:Wl], ALU.mult),
                    reads=("n_xln",), writes=("n_xln",), waits=[t4, t5])
                tnk = P.op("v", lambda E, k=k, off=off, Wl=Wl: E.tensor_tensor(
                    xln[:, k * W + off:k * W + off + Wl],
                    xln[:, k * W + off:k * W + off + Wl],
                    ps[4][:, 0:Wl], ALU.subtract),
                    reads=("n_xln",), writes=("n_xln",), inc="v1")
                tab = P.op("s", lambda E, k=k, off=off, Wl=Wl: E.activation(
                    nab[:, 0:Wl], xln[:, k * W + off:k * W + off + Wl],
                    AF.Abs),
                    reads=("n_xln",), writes=("n_abs",), inc="s1",
                    waits=[tnk, rot.get("nab_user")])
                tn = P.op("v", lambda E, off=off, Wl=Wl: E.tensor_tensor(
                    hmax[:, off:off + Wl], hmax[:, off:off + Wl],
                    nab[:, 0:Wl], ALU.max),
                    reads=("n_abs", "n_hmax"), writes=("n_hmax",),
                    inc="v1", waits=[tz, tab])
                rot["nab_user"] = tn
            C.psum_free[4] = tn
            C.psum_free[5] = tn
            rot["xln_user"] = tn
            tpr = P.op("g", lambda E, off=off, Wl=Wl: E.partition_all_reduce(
                hmax[:, W + off:W + off + Wl], hmax[:, off:off + Wl], 128,
                ReduceOp.max),
                reads=("n_hmax",), writes=("n_hmax",), inc="g1", waits=[tn])
            P.op("v", lambda E, off=off, Wl=Wl: E.tensor_scalar(
                hmax[:, W + off:W + off + Wl], hmax[:, W + off:W + off + Wl],
                1e-5, None, ALU.max),
                reads=("n_hmax",), writes=("n_hmax",), waits=[tpr])
            P.op("v", lambda E, off=off, Wl=Wl: E.reciprocal(
                scb[:, off:off + Wl], hmax[:, W + off:W + off + Wl]),
                reads=("n_hmax",), writes=("n_scb",),
                waits=[rot.get("scb_user")])
            P.op("v", lambda E, off=off, Wl=Wl: E.tensor_scalar(
                scb[:, off:off + Wl], scb[:, off:off + Wl], 127.0, None,
                ALU.mult),
                reads=("n_scb",), writes=("n_scb",))
            tsc = P.op("v", lambda E, off=off, Wl=Wl: E.tensor_scalar(
                scb[:, W + off:W + off + Wl],
                hmax[:, W + off:W + off + Wl], 1.0 / 127.0, None, ALU.mult),
                reads=("n_hmax",), writes=("n_scb",), inc="v1")
            rot["hmax_user"] = tsc
            tq = None
            for k in range(KM):
                if off == 1:
                    P.op("v", lambda E, k=k: E.memset(
                        xqb[:, k * W:k * W + 1], 0.0), writes=("n_xqb",))
                if he == L:
                    P.op("v", lambda E, k=k: E.memset(
                        xqb[:, k * W + off + Wl:(k + 1) * W], 0.0),
                        writes=("n_xqb",))
                tq = emit_qdq_bc(
                    C, xln[:, k * W + off:k * W + off + Wl],
                    xqb[:, k * W + off:k * W + off + Wl],
                    scb[:, off:off + Wl], scb[:, W + off:W + off + Wl],
                    -128.0, 127.0, tmpw[:, 0:Wl], "n_tmpw",
                    reads=("n_xln",), writes=("n_xqb",),
                    waits=[tsc], bcreads=("n_scb",))
            tzh = P.op("v", lambda E: E.memset(hmax[:, 0:Tc], 0.0),
                       writes=("n_hmax",), inc="v1", waits=[tsc])
            Tcc = min(Tc, L - c0)
            for f in range(KH):
                par, pi = f % 2, f % 2
                tmm = None
                for k in range(KM):
                    tmm = P.op("t", lambda E, k=k, f=f, pi=pi, W=W: E.matmul(
                        ps[pi][:, 0:W], w1T[:, k * DH + f * 128:
                                            k * DH + (f + 1) * 128],
                        xqb[:, k * W:(k + 1) * W],
                        start=(k == 0), stop=(k == KM - 1)),
                        waits=([tq, tw1, C.psum_free.get(pi)]
                               if k == 0 else []),
                        inc="t1" if k == KM - 1 else None)
                te = P.op("s", lambda E, pi=pi, par=par, W=W: E.activation(
                    hr[:, par * W:(par + 1) * W], ps[pi][:, 0:W], AF.Copy),
                    writes=("n_hr",), inc="s1",
                    waits=[tmm, rot.get(("hr", par))])
                C.psum_free[pi] = te
                tcv2 = None
                P.op("v", lambda E, f=f, par=par, Tcc=Tcc, W=W:
                     E.tensor_scalar(
                         sgt5[:, 0:Tcc], hr[:, par * W:par * W + Tcc],
                         cw1[:, f * 3:f * 3 + 1], None, ALU.mult),
                     reads=("n_hr", "n_cw1"), writes=("n_sgt",),
                     waits=[te, tcb, rot.get("sgt_user")])
                for kk in range(1, 3):
                    tcv2 = P.op("v", lambda E, f=f, par=par, kk=kk, Tcc=Tcc,
                                W=W: E.scalar_tensor_tensor(
                                    sgt5[:, 0:Tcc],
                                    hr[:, par * W + kk:par * W + kk + Tcc],
                                    cw1[:, f * 3 + kk:f * 3 + kk + 1],
                                    sgt5[:, 0:Tcc], ALU.mult, ALU.add),
                                reads=("n_hr", "n_sgt", "n_cw1"),
                                writes=("n_sgt",),
                                inc="v1" if kk == 2 else None)
                rot[("hr", par)] = tcv2
                ts = P.op("s", lambda E, f=f, Tcc=Tcc: E.activation(
                    h2b[:, f * Tc:f * Tc + Tcc], sgt5[:, 0:Tcc], AF.Silu,
                    bias=cb1[:, f:f + 1]),
                    reads=("n_sgt",), writes=("n_h2b",), inc="s1",
                    waits=[tcv2, tcb, rot.get("h2b_user")])
                rot["sgt_user"] = ts
                tab2 = P.op("s", lambda E, f=f, Tcc=Tcc: E.activation(
                    nab[:, 0:Tcc], h2b[:, f * Tc:f * Tc + Tcc], AF.Abs),
                    reads=("n_h2b",), writes=("n_abs",), inc="s1",
                    waits=[ts, rot.get("nab_user")])
                tmx = P.op("v", lambda E, Tcc=Tcc: E.tensor_tensor(
                    hmax[:, 0:Tcc], hmax[:, 0:Tcc], nab[:, 0:Tcc], ALU.max),
                    reads=("n_abs", "n_hmax"), writes=("n_hmax",),
                    inc="v1", waits=[tzh, tab2])
                rot["nab_user"] = tmx
            th = P.tok("v1")
            tpr = P.op("g", lambda E, Tcc=Tcc: E.partition_all_reduce(
                hmax[:, W:W + Tcc], hmax[:, 0:Tcc], 128, ReduceOp.max),
                reads=("n_hmax",), writes=("n_hmax",), inc="g1", waits=[th])
            P.op("v", lambda E, Tcc=Tcc: E.tensor_scalar(
                hmax[:, W:W + Tcc], hmax[:, W:W + Tcc], 1e-5, None,
                ALU.max),
                reads=("n_hmax",), writes=("n_hmax",), waits=[tpr])
            P.op("v", lambda E, Tcc=Tcc: E.reciprocal(
                scb[:, 0:Tcc], hmax[:, W:W + Tcc]),
                reads=("n_hmax",), writes=("n_scb",), waits=[tsc])
            P.op("v", lambda E, Tcc=Tcc: E.tensor_scalar(
                scb[:, 0:Tcc], scb[:, 0:Tcc], 127.0, None, ALU.mult),
                reads=("n_scb",), writes=("n_scb",))
            tsc2 = P.op("v", lambda E, Tcc=Tcc: E.tensor_scalar(
                scb[:, W:W + Tcc], hmax[:, W:W + Tcc], 1.0 / 127.0, None,
                ALU.mult),
                reads=("n_hmax",), writes=("n_scb",), inc="v1")
            rot["hmax_user"] = tsc2
            rot["scb_user"] = tsc2
            tq2 = None
            for f in range(KH):
                tq2 = emit_qdq_bc(
                    C, h2b[:, f * Tc:f * Tc + Tcc],
                    h2b[:, f * Tc:f * Tc + Tcc],
                    scb[:, 0:Tcc], scb[:, W:W + Tcc], -128.0, 127.0,
                    tmpw[:, 0:Tcc], "n_tmpw",
                    reads=("n_h2b",), writes=("n_h2b",),
                    waits=[tsc2], bcreads=("n_scb",))
            rot["h2b_user"] = tq2
            cs = slice(c0, c0 + Tcc)
            for f in range(KM):
                par, pi = f % 2, f % 2
                tw2 = P.op("q", lambda E, f=f, par=par: E.dma_start(
                    out=w2f[:, par * KH * 128:(par + 1) * KH * 128],
                    in_=ds["w_w2T"][f]),
                    inc=f"qp{par}", inc_n=16, waits=[rot.get(("w2f", par))])
                tmm = None
                for k in range(KH):
                    tmm = P.op("t", lambda E, k=k, par=par, pi=pi, Tcc=Tcc:
                               E.matmul(
                                   ps[pi][:, 0:Tcc],
                                   w2f[:, par * KH * 128 + k * 128:
                                       par * KH * 128 + (k + 1) * 128],
                                   h2b[:, k * Tc:k * Tc + Tcc],
                                   start=(k == 0), stop=(k == KH - 1)),
                               waits=([tw2, tq2, C.psum_free.get(pi)]
                                      if k == 0 else []),
                               inc="t1" if k == KH - 1 else None)
                rot[("w2f", par)] = tmm
                te = P.op("s", lambda E, f=f, pi=pi, Tcc=Tcc: E.activation(
                    dch[:, f * Tc:f * Tc + Tcc], ps[pi][:, 0:Tcc], AF.Copy),
                    writes=("n_dch",), inc="s1",
                    waits=[tmm, rot.get("dch_user")])
                C.psum_free[pi] = te
            tdn = P.tok("s1")
            oldTc = C.Tc
            C.Tc = Tcc
            dchv = dch.ap().rearrange("p (k t) -> p k t", t=Tc)[:, :, 0:Tcc]
            tadd, tst = emit_postln_add_v(C, "n_dch", dch, Tc, b, cs, rot,
                                          tdn, dst="xln1")
            C.Tc = oldTc
            rot["dch_user"] = tst
            rot["xqb_user"] = P.tok("t1")
    st.close()
    return P.tok("gr0")


def emit_postln_add_v(C, srcname, srcch, stride, b, cs, rot, tready,
                      dst="resid"):
    P, ps, sb = C.P, C.ps, C.sb
    Tc = C.Tc
    xtl, sgt = C.xtl, C.sgt
    first = True
    for k in range(KM):
        t6 = P.op("t", lambda E, k=k, first=first: E.matmul(
            ps[6][0:1, 0:Tc], sb["onesf"][:, 0:1],
            srcch[:, k * stride:k * stride + Tc], start=first,
            stop=(k == KM - 1)),
            inc="t1", waits=[tready] + ([C.psum_free.get(6)]
                                        if first else []))
        tsq = P.op("s", lambda E, k=k: E.activation(
            sgt[:, 0:Tc], srcch[:, k * stride:k * stride + Tc], AF.Square),
            writes=(C.sgtname,), inc="s1",
            waits=[tready, rot.get("sgt_user")])
        t7 = P.op("t", lambda E, first=first, k=k: E.matmul(
            ps[7][0:1, 0:Tc], sb["onesf"][:, 0:1], sgt[:, 0:Tc],
            start=first, stop=(k == KM - 1)), inc="t1",
            waits=[tsq] + ([C.psum_free.get(7)] if first else []))
        rot["sgt_user"] = t7
        first = False
    tmm = P.tok("t1")
    t4, t5 = ln_rows(C, ps[6], ps[7], 1.0 / DM, 0, tmm)
    C.psum_free[6] = P.tok("v1")
    C.psum_free[7] = C.psum_free[6]
    for k in range(KM):
        par = k % 2
        HW = xtl.shape[1] // 2
        tl = P.op("q", lambda E, k=k, par=par, b=b, cs=cs, HW=HW:
                  E.dma_start(out=xtl[:, par * HW:par * HW + Tc],
                              in_=C.ds["resid"][b, k, :, cs]),
                  inc=f"qz{par}", inc_n=16,
                  waits=[rot.get(("xtl", par)), rot.get(("xtl2", par))])
        P.op("v", lambda E, k=k: E.tensor_tensor(
            srcch[:, k * stride:k * stride + Tc],
            srcch[:, k * stride:k * stride + Tc], ps[5][:, 0:Tc], ALU.mult),
            reads=(srcname,), writes=(srcname,), waits=[t4, t5])
        P.op("v", lambda E, k=k: E.tensor_tensor(
            srcch[:, k * stride:k * stride + Tc],
            srcch[:, k * stride:k * stride + Tc], ps[4][:, 0:Tc],
            ALU.subtract),
            reads=(srcname,), writes=(srcname,))
        ta = P.op("v", lambda E, k=k, par=par, HW=HW: E.tensor_tensor(
            srcch[:, k * stride:k * stride + Tc],
            srcch[:, k * stride:k * stride + Tc],
            xtl[:, par * HW:par * HW + Tc], ALU.add),
            reads=(srcname,), writes=(srcname,), inc="v1", waits=[tl])
        rot[("xtl", par)] = ta
        rot[("xtl2", par)] = ta
    tadd = P.tok("v1")
    C.psum_free[4] = tadd
    C.psum_free[5] = tadd
    tst = P.op("s", lambda E, b=b, cs=cs, dst=dst: E.dma_start(
        out=C.ds[dst][b, :, :, cs].rearrange("k p t -> p k t"),
        in_=srcch.ap().rearrange("p (k t) -> p k t", t=stride)
        [:, :, 0:Tc]),
        inc="gr0", inc_n=16, waits=[tadd])
    return tadd, tst


def build_nc(L=2048, Tc=512, dbg=None, ncore=NCORE):
    global _LASTP
    nc = bass.Bass("TRN2", target_bir_lowering=False)
    P = Prog(nc)
    _LASTP = P
    es = P.es

    din = {}
    for name, shape in [
        ("L0_attn_x", [BPC, L, DM]), ("L0_attn_adj", [DI, DI]),
        ("L0_attn_in_proj_w", [DM, 2 * DI]), ("L0_attn_conv_w", [DI, 1, 4]),
        ("L0_attn_conv_b", [DI]), ("L0_attn_x_proj_w", [DI, DTR + 2 * NST]),
        ("L0_attn_dt_proj_w", [DTR, DI]), ("L0_attn_dt_proj_b", [DI]),
        ("L0_attn_A_log", [DI, NST]), ("L0_attn_D", [DI]),
        ("L0_attn_out_proj_w", [DI, DM]),
        ("L0_ffn_w_gate", [HF, DM]), ("L0_ffn_w_up", [HF, DM]),
        ("L0_ffn_w_down", [DM, HF]),
        ("L1_attn_Wq", [DM, DM]), ("L1_attn_bq", [DM]),
        ("L1_attn_Wk", [DM, DM]), ("L1_attn_bk", [DM]),
        ("L1_attn_Wv", [DM, DM]), ("L1_attn_bv", [DM]),
        ("L1_attn_Wo", [DM, DM]), ("L1_attn_bo", [DM]),
        ("L1_ffn_w1", [DH, DM]), ("L1_ffn_conv_w", [DH, 1, 3]),
        ("L1_ffn_conv_b", [DH]), ("L1_ffn_w2", [DM, DH]),
    ]:
        din[name] = nc.dram_tensor(name, shape, F32, kind="ExternalInput").ap()
    out_d = nc.dram_tensor("out", [BPC, L, DM], F32, kind="ExternalOutput").ap()

    ds = {}
    for name, shape, dt in [
        ("w_inproj", [2 * KI, 128, DM], BF16),
        ("w_adjn", [KI, 128, DI], BF16),
        ("w_outproj", [KM, 128, DI], BF16), ("w_xproj", [DI, 96], BF16),
        ("w_dtproj", [DTR, DI], BF16),
        ("w_upT", [DM, HF], BF16), ("w_gateT", [DM, HF], BF16),
        ("w_downT", [KM, 128, HF], BF16),
        ("w_q", [DM, DM], BF16), ("w_k", [DM, DM], BF16),
        ("w_v", [DM, DM], BF16), ("w_o", [KM, 128, DM], BF16),
        ("w_w1T", [DM, DH], BF16), ("w_w2T", [KM, 128, DH], BF16),
        ("resid", [BPC, KM, 128, L], F32),
        ("xln1", [BPC, KM, 128, L], F32),
        ("up_raw", [BPC, KF, 128, L], F32),
        ("gate_raw", [BPC, KF, 128, L], F32),
        ("g2pre", [BPC, KF, 128, L], F32),
        ("o_pre", [BPC, KF, 128, L], F32),
        ("cc_in", [1, 8], F32), ("cc_out", [1, 8], F32),
    ]:
        ds[name] = nc.dram_tensor(name, shape, dt).ap()

    sb = {}
    def sbuf(name, shape, dt):
        sb[name] = es.enter_context(nc.sbuf_tensor(name, shape, dt))
        return sb[name]
    sbuf("idf", [128, 128], F32)
    sbuf("idb", [128, 128], BF16)
    sbuf("onesf", [128, 128], F32)
    sbuf("onesb", [128, 128], BF16)
    sbuf("selb", [32, 32 * 128], BF16)
    init_es = ExitStack()
    sb["self_sel"] = init_es.enter_context(
        nc.sbuf_tensor("self_sel", [32, 32 * 128], F32, side="right"))
    sbuf("negA", [128, KI * NST], F32)
    sbuf("Dsb", [128, KI], F32)
    sbuf("cwsb", [128, KI * 4], F32)
    sbuf("cbsb", [128, KI], F32)
    sbuf("dtbn", [128, KI], F32)
    sbuf("hcar", [128, KI * NST], F32)
    sbuf("tails", [128, KI * 3], F32)
    sbuf("xprj", [128, KI * 96], BF16)
    sbuf("dtprj", [64, DI], BF16)
    sbuf("rows", [1, 4 * 520], F32)
    ps = [es.enter_context(nc.psum_tensor(f"ps{i}", [128, 512], F32))
          for i in range(8)]

    C = SimpleNamespace(P=P, nc=nc, din=din, ds=ds, sb=sb, ps=ps, out_d=out_d,
                        L=L, Tc=Tc, rows=sb["rows"], psum_free={})

    P.op("g", lambda E: E.iota(sb["idf"][:], pattern=[[1, 128]], base=0,
                               channel_multiplier=-1,
                               allow_small_or_imprecise_dtypes=True),
         writes=("idf",))
    P.op("g", lambda E: E.tensor_scalar(sb["idf"][:], sb["idf"][:], 0.0,
                                        None, ALU.is_equal),
         reads=("idf",), writes=("idf",))
    P.op("g", lambda E: E.iota(sb["self_sel"][:], pattern=[[1, 32], [0, 128]],
                               base=0, channel_multiplier=-1,
                               allow_small_or_imprecise_dtypes=True),
         writes=("self_sel",))
    P.op("g", lambda E: E.tensor_scalar(sb["selb"][:], sb["self_sel"][:],
                                        0.0, None, ALU.is_equal),
         reads=("self_sel",), writes=("selb",))
    P.op("g", lambda E: E.tensor_copy(sb["idb"][:], sb["idf"][:]),
         reads=("idf",), writes=("idb",))
    P.op("g", lambda E: E.memset(sb["onesf"][:], 1.0), writes=("onesf",))
    tinit = P.op("g", lambda E: E.memset(sb["onesb"][:], 1.0),
                 writes=("onesb",), inc="g0")
    P.raw("g", lambda E: E.load_library(_attn_lib))
    for e in ("t", "v", "s", "q"):
        P.waitt(e, tinit)

    init_es.close()
    for k in range(KI):
        P.op("q", _dma(sb["negA"][:, k * NST:(k + 1) * NST],
                       din["L0_attn_A_log"][k * 128:(k + 1) * 128, :]),
             inc="qc", inc_n=16)
        P.op("q", _dma(sb["cwsb"][:, k * 4:(k + 1) * 4],
                       din["L0_attn_conv_w"][k * 128:(k + 1) * 128, 0, :]),
             inc="qc", inc_n=16)
        P.op("q", _dma(sb["Dsb"][:, k:k + 1],
                       din["L0_attn_D"][k * 128:(k + 1) * 128]
                       .rearrange("(p o) -> p o", o=1)), inc="qc", inc_n=16)
        P.op("q", _dma(sb["cbsb"][:, k:k + 1],
                       din["L0_attn_conv_b"][k * 128:(k + 1) * 128]
                       .rearrange("(p o) -> p o", o=1)), inc="qc", inc_n=16)
        P.op("q", _dma(sb["dtbn"][:, k:k + 1],
                       din["L0_attn_dt_proj_b"][k * 128:(k + 1) * 128]
                       .rearrange("(p o) -> p o", o=1)), inc="qc", inc_n=16)
    qc = P.tok("qc")
    tna = P.op("s", lambda E: E.activation(sb["negA"][:], sb["negA"][:],
                                           AF.Exp),
               reads=("negA",), writes=("negA",), inc="s1", waits=[qc])
    P.op("v", lambda E: E.tensor_scalar(sb["negA"][:], sb["negA"][:], -1.0,
                                        None, ALU.mult),
         reads=("negA",), writes=("negA",), waits=[tna])
    P.op("v", lambda E: E.tensor_scalar(sb["dtbn"][:], sb["dtbn"][:], -1.0,
                                        None, ALU.mult),
         reads=("dtbn",), writes=("dtbn",), waits=[qc], inc="v1")

    st = ExitStack()
    ti = st.enter_context(nc.sbuf_tensor("i_ti", [128, 2 * DM], F32))
    to = st.enter_context(nc.sbuf_tensor("i_to", [128, 2 * KM * 128], F32))
    rot = {}
    for b in range(BPC):
        for tt in range(L // 128):
            par = tt % 2
            tl = P.op("q", lambda E, b=b, tt=tt, par=par: E.dma_start(
                out=ti[:, par * DM:(par + 1) * DM],
                in_=din["L0_attn_x"][b, tt * 128:(tt + 1) * 128, :]),
                inc=f"qi{par}", inc_n=16, waits=[rot.get(("ti", par))])
            for half in range(2):
                pi = half
                tmm = None
                for j in range(4):
                    k = half * 4 + j
                    tmm = P.op("t", lambda E, k=k, j=j, par=par, pi=pi:
                               E.transpose(
                                   ps[pi][:, j * 128:(j + 1) * 128],
                                   ti[:, par * DM + k * 128:
                                      par * DM + (k + 1) * 128],
                                   sb["idf"][:]),
                               waits=([tl, C.psum_free.get(pi)]
                                      if j == 0 else []),
                               inc="t1" if j == 3 else None)
                te = P.op("s", lambda E, par=par, half=half, pi=pi:
                          E.activation(
                              to[:, par * KM * 128 + half * 512:
                                 par * KM * 128 + (half + 1) * 512],
                              ps[pi][:, 0:512], AF.Copy),
                          writes=("i_to",), inc="s1",
                          waits=[tmm, rot.get(("to", par))])
                C.psum_free[pi] = te
            tev = P.tok("s1")
            rot[("ti", par)] = tev
            tst = P.op("s", lambda E, b=b, tt=tt, par=par: E.dma_start(
                out=ds["resid"][b, :, :, tt * 128:(tt + 1) * 128]
                .rearrange("k p t -> p k t"),
                in_=to[:, par * KM * 128:(par + 1) * KM * 128]
                .rearrange("p (k t) -> p k t", t=128)),
                inc=f"gi{par}", inc_n=16, waits=[tev])
            rot[("to", par)] = tst
    st.close()
    P.barrier()
    x_staged = None

    need = {"x1": ("mamba",), "x2": ("mamba", "ffn0"),
            "x3": ("mamba", "ffn0", "attn1"),
            "aout": ("mamba", "ffn0", "attn1")}.get(
                dbg, ("mamba", "ffn0", "attn1", "ffn1"))
    C.dbg_aout = False
    C.w_done = s0_weights(C, need)
    P.barrier()
    s1_done = s1_mamba(C)
    def _finish():
        block = es.enter_context(nc.Block())
        P.build(block)
        es.close()
        mybir.codegen_inst_isa_subclasses(nc)
        return nc
    if dbg == "x1":
        P.barrier()
        s_out(C, s1_done)
        return _finish()
    P.barrier()
    s2_done = s2_ffn0(C, ncore)
    if dbg == "x2":
        P.barrier()
        s_out(C, s2_done)
        return _finish()
    P.barrier()
    C.dbg_aout = (dbg == "aout")
    s3_done = s3_attn(C)
    if dbg == "aout":
        P.dma_wait_all("g", prefixes=("go",))
        return _finish()
    if dbg == "x3":
        P.barrier()
        s_out(C, s3_done)
        return _finish()
    P.barrier()
    s4_done = s4_ffn1(C)
    P.barrier()
    s_out(C, s4_done, src="xln1")
    return _finish()


def kernel(**inputs):
    full = {k: np.ascontiguousarray(np.asarray(v, dtype=np.float32))
            for k, v in inputs.items()}
    nc = build_nc(L=2048, Tc=512, dbg=None)
    in_maps = []
    for c in range(NCORE):
        m = dict(full)
        m["L0_attn_x"] = full["L0_attn_x"][c * BPC:(c + 1) * BPC]
        in_maps.append(m)
    res = run_bass_kernel_spmd(nc, in_maps, core_ids=list(range(NCORE)))
    return np.concatenate([r["out"] for r in res.results], axis=0)


if __name__ == "__main__":
    print("kernel module OK")
